# version 09
# baseline (speedup 1.0000x reference)
import numpy as np
import ml_dtypes
import concourse.bass as bass
import concourse.mybir as mybir
from concourse.bass_utils import run_bass_kernel_spmd
from contextlib import ExitStack

F32 = mybir.dt.float32
BF16 = mybir.dt.bfloat16
AF = mybir.ActivationFunctionType
ALU = mybir.AluOpType

H, DK, DV, D = 8, 64, 128, 1024
OWN, HALO, WMEM = 2048, 384, 192
NT = OWN + HALO
NTT = NT // 128
TSC = NT - WMEM
OTT = OWN // 128
CCS = 32
HLF = 1120
NCH2 = HLF // CCS
SCW = 1280
NSEG, LSEG, WSEG = 32, 70, 192
STEPS = WSEG + LSEG
MAGIC = 6291456.0
VELT = float(2 ** 22 + 1)
AW = 29696
NQW = 2816


class Arena:
    def __init__(self, t, kern=None):
        self.t = t
        self.kern = kern
        self.barrier = {}
        self.regions = {}
        self.free_list = [(0, AW)]

    def alloc(self, name, words, align=16):
        assert name not in self.regions
        if self.kern is not None:
            self.barrier[name] = self.kern.mark()
        for idx, (off, sz) in enumerate(self.free_list):
            a = (off + align - 1) // align * align
            if a + words <= off + sz:
                repl = [p for p in ((off, a - off), (a + words, off + sz - a - words)) if p[1] > 0]
                self.free_list[idx:idx + 1] = repl
                self.regions[name] = (a, words)
                return
        raise MemoryError(f"arena OOM: {name} ({words}w); live={self.regions}")

    def free(self, name):
        off, words = self.regions.pop(name)
        self.free_list.append((off, words))
        self.free_list.sort()
        m = []
        for o, s in self.free_list:
            if m and m[-1][0] + m[-1][1] == o:
                m[-1] = (m[-1][0], m[-1][1] + s)
            else:
                m.append((o, s))
        self.free_list = m

    def f32(self, name, lo=0, hi=None, parts=slice(0, 128)):
        off, words = self.regions[name]
        hi = words if hi is None else hi
        assert 0 <= lo <= hi <= words, (name, lo, hi, words)
        return self.t[parts, off + lo: off + hi]

    def bf(self, name, lo=0, hi=None, parts=slice(0, 128)):
        off, words = self.regions[name]
        hi = 2 * words if hi is None else hi
        assert 0 <= lo <= hi <= 2 * words, (name, lo, hi, words)
        return self.t[parts, off: off + words].bitcast(BF16)[:, lo:hi]


class Kern:

    def __init__(self):
        self.nc = bass.Bass("TRN2", target_bir_lowering=False, debug=False)
        self.sched = []
        self.counts = {}
        self.es = ExitStack()

    SELF_SYNC = {"vector": "ve", "scalar": "ac"}
    DMA_SEMS = ("dS", "dA", "dG", "sC0", "sC1", "dO", "sCNH", "sCNG", "sXLH", "sXLG", "sH0H", "sH0G", "sXQH", "sXQG", "sSTH", "sSTG", "sWPH", "sWPG", "sQKH", "sQKG", "sVH", "sVG", "sSDH", "sSDG", "sABH", "sABG", "sSPH", "sSPG", "sW4H", "sW4G", "sXTH", "sXTG", "sTSH", "sTSG", "sFCH", "sFCG", "sDLH", "sDLG", "sCTH", "sCTG", "sYSH", "sYSG",
                "wp0", "wp1", "wp2", "wp3", "wp4", "wp5", "wp6", "wp7")

    def op(self, eng, fn, waits=(), inc=None, amt=1):
        waits = list(waits)
        waits = [(s, self.counts.get(s, 0) if s in self.DMA_SEMS else v) for (s, v) in waits]
        ss = self.SELF_SYNC.get(eng)
        if ss is not None and inc == ss:
            prev = self.counts.get(ss, 0)
            if prev > 0:
                waits.append((ss, prev))
        if inc is not None:
            self.counts[inc] = self.counts.get(inc, 0) + amt
        self.sched.append((eng, fn, tuple(waits), inc, amt))

    def stream(self, name):
        self._cur_stream = name

    def dma(self, eng, out, in_, waits=(), sem=None, **kw):
        if sem is None:
            base = getattr(self, "_cur_stream", None)
            if base is not None:
                sem = base + ("G" if eng == "gpsimd" else "H")
            else:
                sem = {"sync": "dS", "scalar": "dA", "gpsimd": "dG"}[eng]
        self.op(eng, (lambda q, o=out, i=in_, kw=kw: q.dma_start(out=o, in_=i, **kw)),
                waits=waits, inc=sem, amt=16)
        return (sem, self.now(sem))

    def dmaT(self, eng, out, in_, waits=(), sem=None):
        if sem is None:
            base = getattr(self, "_cur_stream", None)
            sem = (base + "H") if base is not None else {"sync": "dS", "scalar": "dA"}[eng]
        self.op(eng, (lambda q, o=out, i=in_: q.dma_start_transpose(o, i)),
                waits=waits, inc=sem, amt=16)
        return (sem, self.now(sem))

    def ve(self, fn, waits=()):
        self.op("vector", fn, waits=waits, inc="ve")
        return ("ve", self.now("ve"))

    def ac(self, fn, waits=()):
        self.op("scalar", fn, waits=waits, inc="ac")
        return ("ac", self.now("ac"))

    def pe(self, fn, waits=()):
        self.op("tensor", fn, waits=waits, inc="pe")
        return ("pe", self.now("pe"))

    def now(self, sem):
        return self.counts.get(sem, 0)

    def mark(self):
        return [(s, v) for s, v in self.counts.items() if v > 0]

    def build_with_dumps(self, dump_names=()):
        self._dump_names = list(dump_names)
        return self.build()

    def build(self):
        nc = self.nc
        S = self.es.enter_context

        def din(name, shape, dt=F32):
            return nc.dram_tensor(name, shape, dt, kind="ExternalInput").ap()

        def hbm(name, shape, dt=F32):
            return nc.dram_tensor(name, shape, dt, kind="Internal").ap()

        self.xg_d = din("xg", [NT, D])
        self.xh_d = din("xh128", [128, D])
        self.sdm_d = din("sdmask", [NT, 1])
        self.wq_d = din("wq", [D, 512]); self.wk_d = din("wk", [D, 512])
        self.wv_d = din("wv", [D, 1024]); self.wsp_d = din("w_spike", [D, 512])
        self.wau_d = din("w_au", [D, 64]); self.wad_d = din("w_ad", [64, 512])
        self.wbe_d = din("w_beta", [D, 8])
        self.wu1_d = din("w_u1", [D, 64]); self.wu2_d = din("w_u2", [64, D])
        self.wf1_d = din("w_ff1", [D, 4096]); self.wf3_d = din("w_ff3", [D, 4096])
        self.wf2_d = din("w_ff2", [4096, D]); self.wout_d = din("w_out", [D, D])
        self.cw_d = din("conv_w", [D, 4]); self.cb_d = din("conv_b", [D, 1])
        self.wni_d = din("w_norm_in", [D, 1])
        self.wg_my_d = din("wg_my", [352, D]); self.wu_my_d = din("wu_my", [352, D])
        self.wd_my_d = din("wd_my", [128, NQW])
        self.bau_d = din("b_au", [1, 64]); self.bad_d = din("b_ad", [1, 512])
        self.aspk_d = din("alpha_spike", [1, 512]); self.bspk_d = din("beta_spike", [1, 512])
        self.bbe_d = din("b_beta", [1, 8]); self.whn_d = din("w_hn", [1, 1024])
        self.bu1_d = din("b_u1", [1, 64]); self.bu2_d = din("b_u2", [1, D])
        self.wffn_d = din("w_ffn", [1, D])
        self.id_d = din("c_ident", [128, 128])
        self.idb_d = din("c_identb", [128, 128], BF16)
        self.negi_d = din("c_negi", [128, 32], BF16)
        self.ones_d = din("c_ones", [1, 128], BF16)
        self.onesf_d = din("c_onesf", [128, 128])
        self.mskN_d = din("c_mskN", [128, 64]); self.mskNT_d = din("c_mskNT", [128, 64])
        self.mskHq_d = din("c_mskHq", [128, 256]); self.idp_d = din("c_identp", [128, 64])
        self.mst_d = din("c_mst", [128, SCW], BF16)

        self.out_d = nc.dram_tensor("out", [OWN, D], F32, kind="ExternalOutput").ap()

        self.sd_h = hbm("sd_h", [NT, 512])
        self.qr_h = hbm("qr_h", [NT, 512], BF16)
        self.kr_h = hbm("kr_h", [NT, 512], BF16)
        self.v_h = hbm("v_h", [NT, 1024], BF16)
        self.ab_h = hbm("ab_h", [NT, 512])
        self.rn_h = hbm("rn_h", [NT, 16])
        self.bb_h = hbm("bb_h", [NT, 8])
        self.spk_h = hbm("spk_h", [TSC, 512], BF16)
        self.xq_h = hbm("xq_h", [OWN, D], BF16)
        self.xnb_h = hbm("xnb_h", [OWN, D], BF16)
        self.xqT_h = hbm("xqT_h", [D, OWN], BF16)
        self.y_h = hbm("y_h", [TSC, 1024], BF16)
        self.hq_h = hbm("hq_h", [OWN, NQW], BF16)
        self.hqT_h = hbm("hqT_h", [16 * NQW, 128], BF16)
        self.wgq_my = hbm("wgq_my", [352, D], BF16)
        self.wuq_my = hbm("wuq_my", [352, D], BF16)
        self.wdq_my = hbm("wdq_my", [128, NQW], BF16)
        self.wgq_all = hbm("wgq_all", [NQW, D], BF16)
        self.wuq_all = hbm("wuq_all", [NQW, D], BF16)
        self.wdq_all = hbm("wdq_all", [1024, NQW], BF16)
        self.wgqT_h = hbm("wgqT_h", [D, NQW], BF16)
        self.wuqT_h = hbm("wuqT_h", [D, NQW], BF16)
        self.wdqT_h = hbm("wdqT_h", [NQW, D], BF16)
        self.h0_h = hbm("h0_h", [2560, D], BF16)
        self.h0l_h = hbm("h0l_h", [2560, D], BF16)
        self.wf1b_h = hbm("wf1b_h", [D, 4096], BF16)
        self.wf3b_h = hbm("wf3b_h", [D, 4096], BF16)
        self.wf2b_h = hbm("wf2b_h", [4096, D], BF16)
        self.xn32_h = hbm("xn32_h", [OWN, D])
        self.s_h = hbm("s_h", [OWN, D])
        cand = dict(h0=self.h0_h, sd=self.sd_h, qr=self.qr_h, kr=self.kr_h, v=self.v_h, ab=self.ab_h,
                    rn=self.rn_h, bb=self.bb_h, spk=self.spk_h, xq=self.xq_h, xnb=self.xnb_h,
                    y=self.y_h, hq=self.hq_h, wgq=self.wgq_all, wdq=self.wdq_all, wgqmy=self.wgq_my,
                    s=self.s_h, xn32=self.xn32_h, xqT=self.xqT_h, wgqT=self.wgqT_h,
                    wdqT=self.wdqT_h)
        self.dbg_dumps = [(n, cand[n]) for n in getattr(self, "_dump_names", []) if n in cand]

        self.A = Arena(S(nc.sbuf_tensor("arena", [128, AW], F32)), kern=self)
        self.consts = S(nc.sbuf_tensor("consts", [128, 1024], F32))
        self.CN = dict(ID=0, MN=128, MNT=192, MHQ=256, IP=512, CW=576, CB=608, WNI=616)
        self.cmsk = S(nc.sbuf_tensor("cmsk", [128, SCW], BF16))
        self.identb = S(nc.sbuf_tensor("identb", [128, 128], BF16))
        self.onesb = S(nc.sbuf_tensor("onesb", [1, 128], BF16))
        self.onesf = S(nc.sbuf_tensor("onesf", [128, 128], F32))
        self.negib = S(nc.sbuf_tensor("negib", [128, 32], BF16))
        self.biasb = S(nc.sbuf_tensor("biasb", [128, 2048], BF16))
        self.biasf = S(nc.sbuf_tensor("biasf", [128, 2048], F32))
        self.bcast = S(nc.sbuf_tensor("bcast", [128, 2048], BF16))
        self.wffnb = S(nc.sbuf_tensor("wffnb", [128, 1024], BF16))
        self.sdm_s = S(nc.sbuf_tensor("sdm_s", [128, NTT], F32))
        self.nrm2 = S(nc.sbuf_tensor("nrm2", [128, NTT * 16], F32))
        self.aum = S(nc.sbuf_tensor("aum", [128, NTT * 64], BF16))
        self.aumT = S(nc.sbuf_tensor("aumT", [128, NT], BF16))
        self.wadb = S(nc.sbuf_tensor("wadb", [64, 512], BF16))
        self.wu1b = S(nc.sbuf_tensor("wu1b", [128, 8 * 64], BF16))
        self.wu2b = S(nc.sbuf_tensor("wu2b", [64, 1024], BF16))
        self.scr = [S(nc.sbuf_tensor(f"scr{i}", [128, 1024], F32)) for i in range(6)]
        self.sqscr = S(nc.sbuf_tensor("sqscr", [128, 1024], F32))
        self.bbsb = S(nc.sbuf_tensor("bbsb", [128, NTT * 8], F32))
        self.rnsb = S(nc.sbuf_tensor("rnsb", [128, NTT * 16], F32))
        self.scrb = [S(nc.sbuf_tensor(f"scrb{i}", [128, 1024], BF16)) for i in range(4)]
        self.memM = S(nc.sbuf_tensor("memM", [128, 128], F32))
        self.dscr = S(nc.sbuf_tensor("dscr", [128, 256], F32))
        self.smallq = S(nc.sbuf_tensor("smallq", [128, 512], F32))
        self.zsb = S(nc.sbuf_tensor("zsb", [128, 512], F32))
        self.zsbb = S(nc.sbuf_tensor("zsbb", [128, 512], BF16))
        self.pend = S(nc.sbuf_tensor("pendt", [128, 4 * NCH2], F32))
        self.sol = S(nc.sbuf_tensor("sol", [128, 13 * 64], BF16))
        self.SO = {k: i * 64 for i, k in enumerate(
            ["N", "NT", "N2", "N2T", "N4", "N4T", "N8", "N8T", "N16", "N16T", "P2", "P1T", "Y1"])}
        self.pb = [S(nc.psum_tensor(f"pb{i}", [128, 512], F32)) for i in range(7)]
        self.pbh = S(nc.psum_tensor("pbh", [128, 1024], BF16))
        self.sems = {k: S(nc.semaphore(k)) for k in ("dS", "dA", "dG", "pe", "ve", "ac", "cc", "dO", "sC0", "sC1", "sCNH", "sCNG", "sXLH", "sXLG", "sH0H", "sH0G", "sXQH", "sXQG", "sSTH", "sSTG", "sWPH", "sWPG", "sQKH", "sQKG", "sVH", "sVG", "sSDH", "sSDG", "sABH", "sABG", "sSPH", "sSPG", "sW4H", "sW4G", "sXTH", "sXTG", "sTSH", "sTSG", "sFCH", "sFCG", "sDLH", "sDLG", "sCTH", "sCTG", "sYSH", "sYSG", "wp0", "wp1", "wp2", "wp3", "wp4", "wp5", "wp6", "wp7")}

        phases = ["w0", "a", "b", "w4", "c", "def_", "g"]
        upto = getattr(self, "stop", "g")
        skip = getattr(self, "skip", set())
        for p in phases:
            if p not in skip:
                getattr(self, "ph_" + p)()
            if p == upto:
                break
        self.finish()
        return nc

    def ph_w0(self):
        B, A, CN = self, self.A, self.CN
        c = self.consts
        B.stream("sCN")
        B.dma("sync", c[:, CN["ID"]:CN["ID"] + 128], self.id_d)
        B.dma("sync", c[:, CN["MN"]:CN["MN"] + 64], self.mskN_d)
        B.dma("sync", c[:, CN["MNT"]:CN["MNT"] + 64], self.mskNT_d)
        B.dma("sync", c[:, CN["MHQ"]:CN["MHQ"] + 256], self.mskHq_d)
        B.dma("sync", c[:, CN["IP"]:CN["IP"] + 64], self.idp_d)
        for d in range(8):
            rows = slice(d * 128, (d + 1) * 128)
            B.dma("sync", c[:, CN["CW"] + 4 * d:CN["CW"] + 4 * d + 4], self.cw_d[rows, :])
            B.dma("sync", c[:, CN["CB"] + d:CN["CB"] + d + 1], self.cb_d[rows, :])
            B.dma("sync", c[:, CN["WNI"] + d:CN["WNI"] + d + 1], self.wni_d[rows, :])
        B.dma("sync", self.cmsk[:, 0:SCW], self.mst_d)
        B.dma("gpsimd", self.identb[:, :], self.idb_d)
        B.dma("gpsimd", self.onesb[0:1, :], self.ones_d)
        B.dma("sync", self.onesf[:, :], self.onesf_d)
        B.dma("gpsimd", self.negib[:, :], self.negi_d)
        B.dma("sync", self.biasf[0:1, 0:64], self.bau_d)
        B.dma("sync", self.biasf[0:1, 64:576], self.bad_d)
        B.dma("sync", self.biasf[0:1, 576:584], self.bbe_d)
        B.dma("sync", self.biasf[0:1, 584:648], self.bu1_d)
        B.dma("sync", self.biasf[0:1, 648:1672], self.bu2_d)
        B.dma("sync", self.biasf[32:33, 0:512], self.aspk_d)
        B.dma("sync", self.biasf[32:33, 512:1024], self.bspk_d)
        B.dma("sync", self.biasf[32:33, 1024:2048], self.whn_d)
        B.dma("sync", self.biasf[64:65, 0:1024], self.wffn_d)
        B.dma("sync", self.sdm_s[:, :].rearrange("p (a o) -> p a o", o=1), self.sdm_d.rearrange("(a p) o -> p a o", p=128), allow_slow_non_contiguous=True)
        self.WPO = dict(q=0, k=512, v=1024, sd=2048, au=2560, be=2624)
        B.dma("gpsimd", self.wadb[:, :], self.wad_d)
        B.dma("gpsimd", self.wu1b[:, :].rearrange("p (a f) -> p a f", f=64), self.wu1_d.rearrange("(a p) f -> p a f", p=128))
        B.dma("gpsimd", self.wu2b[:, :], self.wu2_d)
        w0 = self.mark()

        for d in range(8):
            B.ve(lambda q, d=d: q.tensor_tensor(
                self.consts[:, CN["CW"] + 4 * d:CN["CW"] + 4 * d + 4],
                self.consts[:, CN["CW"] + 4 * d:CN["CW"] + 4 * d + 4],
                self.consts[:, CN["WNI"] + d:CN["WNI"] + d + 1].broadcast_to([128, 4]), ALU.mult),
                waits=w0 if d == 0 else ())
        B.ve(lambda q: q.tensor_copy(self.biasb[0:1, 0:1672], self.biasf[0:1, 0:1672]))
        B.ve(lambda q: q.tensor_copy(self.biasb[0:1, 1672:1736], self.biasf[0:1, 0:64]))
        B.ve(lambda q: q.tensor_copy(self.biasb[0:1, 1736:1744], self.biasf[0:1, 576:584]))
        wv = ("ve", self.now("ve"))

        nonlocal_guard = []
        def bc_guarded(dst, dst_lo, src, width):
            srow, src_lo = src
            for o in range(0, width, 512):
                w = min(512, width - o)
                waits = [wv] + nonlocal_guard[-1:]
                pw = B.pe(lambda q, o=o, w=w, srow=srow, src_lo=src_lo: q.matmul(
                    self.pb[6][:, 0:w], self.onesf[srow:srow + 1, :],
                    self.biasf[srow:srow + 1, src_lo + o:src_lo + o + w],
                    start=True, stop=True), waits=waits)
                aw = B.ac(lambda q, dst=dst, dst_lo=dst_lo, o=o, w=w: q.activation(
                    dst[:, dst_lo + o:dst_lo + o + w], self.pb[6][:, 0:w], AF.Copy),
                    waits=[pw])
                nonlocal_guard.append(aw)

        bc_guarded(self.bcast, 0, (32, 0), 512)
        bc_guarded(self.bcast, 512, (32, 512), 512)
        bc_guarded(self.bcast, 1024, (32, 1024), 1024)
        bc_guarded(self.wffnb, 0, (64, 0), 1024)
        self.w0_done = self.mark()

    def ph_a(self):
        self.stream(None)
        """LN + rms-fold + xn/xq own spills + h0 transpose + conv -> hbf."""
        B, A = self, self.A
        A.alloc("h0s", 2560)
        A.alloc("h0l", 2560)
        A.alloc("h0f", 2560)
        A.alloc("h0lo", 1024)
        A.alloc("hbf", 9728)
        sq = self.smallq
        last_dmaT = [None, None]
        last_dmaL = [None, None]
        last_xt_use = [None, None]
        last_xnbq = [None, None]
        last_s2red = None
        for i in range(20):
            par = i % 2
            xt = self.scr[par]
            xnf = self.scr[2 + par]
            sqr = self.sqscr
            h0b = self.scrb[par]
            xnb = self.scrb[2 + par]
            xqb = self.scrb[2 + par]
            src = self.xh_d if i == 0 else self.xg_d[(i - 1) * 128:i * 128, :]
            o = par * 8
            M, S2, VAR, RSTD, T1, HSC = o, o + 1, o + 2, o + 3, o + 4, o + 5
            xw = []
            if i == 0:
                xw = self.w0_done
            elif last_xt_use[par] is not None:
                xw = [last_xt_use[par]]
            B.stream("sXL")
            wx = B.dma("sync", xt[:, :], src, waits=xw)
            wr = B.ve(lambda q, xt=xt, M=M: q.tensor_reduce(
                sq[:, M:M + 1].rearrange("p (a o) -> p a o", a=1), xt[:, :].rearrange("p (a f) -> p a f", a=1),
                mybir.AxisListType.X, ALU.add), waits=[wx])
            B.ve(lambda q, M=M: q.tensor_scalar(sq[:, M:M + 1], sq[:, M:M + 1], 1.0 / D, None, ALU.mult))
            wsq = B.ac(lambda q, xt=xt, sqr=sqr: q.activation(sqr[:, :], xt[:, :], AF.Square),
                       waits=[wx] + ([last_s2red] if last_s2red else []))
            last_s2red = B.ve(lambda q, sqr=sqr, S2=S2: q.tensor_reduce(
                sq[:, S2:S2 + 1].rearrange("p (a o) -> p a o", a=1), sqr[:, :].rearrange("p (a f) -> p a f", a=1),
                mybir.AxisListType.X, ALU.add), waits=[wsq])
            B.ve(lambda q, S2=S2: q.tensor_scalar(sq[:, S2:S2 + 1], sq[:, S2:S2 + 1], 1.0 / D, None, ALU.mult))
            B.ve(lambda q, M=M, T1=T1: q.tensor_tensor(sq[:, T1:T1 + 1], sq[:, M:M + 1], sq[:, M:M + 1], ALU.mult))
            wv1 = B.ve(lambda q, S2=S2, T1=T1, VAR=VAR: q.tensor_tensor(
                sq[:, VAR:VAR + 1], sq[:, S2:S2 + 1], sq[:, T1:T1 + 1], ALU.subtract))
            wv1b = B.ve(lambda q, VAR=VAR, T1=T1: q.tensor_scalar(
                sq[:, T1:T1 + 1], sq[:, VAR:VAR + 1], 1e-5, None, ALU.add), waits=[wv1])
            wl = B.ac(lambda q, T1=T1: q.activation(sq[:, T1:T1 + 1], sq[:, T1:T1 + 1], AF.Ln),
                      waits=[wv1b])
            wr2 = B.ac(lambda q, T1=T1, RSTD=RSTD: q.activation(sq[:, RSTD:RSTD + 1], sq[:, T1:T1 + 1], AF.Exp, scale=-0.5),
                       waits=[wl])
            B.ve(lambda q, RSTD=RSTD, T1=T1: q.tensor_tensor(sq[:, T1:T1 + 1], sq[:, RSTD:RSTD + 1], sq[:, RSTD:RSTD + 1], ALU.mult), waits=[wr2])
            B.ve(lambda q, T1=T1, VAR=VAR: q.tensor_tensor(sq[:, T1:T1 + 1], sq[:, T1:T1 + 1], sq[:, VAR:VAR + 1], ALU.mult))
            wv2 = B.ve(lambda q, T1=T1: q.tensor_scalar(sq[:, T1:T1 + 1], sq[:, T1:T1 + 1], 1e-6, None, ALU.add))
            wl2 = B.ac(lambda q, T1=T1: q.activation(sq[:, T1:T1 + 1], sq[:, T1:T1 + 1], AF.Ln), waits=[wv2])
            wl3 = B.ac(lambda q, T1=T1: q.activation(sq[:, T1:T1 + 1], sq[:, T1:T1 + 1], AF.Exp, scale=-0.5), waits=[wl2])
            whs = B.ve(lambda q, RSTD=RSTD, T1=T1, HSC=HSC: q.tensor_tensor(
                sq[:, HSC:HSC + 1], sq[:, RSTD:RSTD + 1], sq[:, T1:T1 + 1], ALU.mult), waits=[wl3])
            wxn = B.ve(lambda q, xt=xt, xnf=xnf, M=M, RSTD=RSTD: q.scalar_tensor_tensor(
                xnf[:, :], xt[:, :], sq[:, M:M + 1], sq[:, RSTD:RSTD + 1].broadcast_to([128, D]),
                ALU.subtract, ALU.mult), waits=[wr2])
            if i >= 4:
                ownrow = (i - 4) * 128
                guard = [last_xnbq[par]] if last_xnbq[par] else []
                wcb = B.ve(lambda q, xnf=xnf, xnb=xnb: q.tensor_copy(xnb[:, :], xnf[:, :]),
                           waits=[wxn] + guard)
                B.stream("sXQ")
                wnb = B.dma("gpsimd", self.xnb_h[ownrow:ownrow + 128, :], xnb[:, :], waits=[wcb])
                wsp32 = B.dma("gpsimd", self.xn32_h[ownrow:ownrow + 128, :], xnf[:, :], waits=[wxn])
                wq = self.quant_chain(xnf, xqb, width=D, blocks=64, sq_lo=16 + par * 192,
                                      wait=wsp32, extra_out_wait=wnb)
                last_xnbq[par] = B.dma("gpsimd", self.xq_h[ownrow:ownrow + 128, :], xqb[:, :], waits=[wq])
            lob = A.bf("h0lo", par * 1024, (par + 1) * 1024)
            wh0f = B.ve(lambda q, xt=xt, sqr=sqr, M=M, HSC=HSC: q.scalar_tensor_tensor(
                sqr[:, :], xt[:, :], sq[:, M:M + 1], sq[:, HSC:HSC + 1].broadcast_to([128, D]),
                ALU.subtract, ALU.mult), waits=[whs])
            last_xt_use[par] = wh0f
            guard2 = [last_dmaT[par]] if last_dmaT[par] else []
            wh0 = B.ac(lambda q, sqr=sqr, h0b=h0b: q.activation(h0b[:, :], sqr[:, :], AF.Copy),
                       waits=[wh0f] + guard2)
            guard3 = [last_dmaL[par]] if last_dmaL[par] else []
            wlo = B.ve(lambda q, sqr=sqr, h0b=h0b, lob=lob: q.tensor_tensor(
                lob[:, :], sqr[:, :], h0b[:, :], ALU.subtract), waits=[wh0] + guard3)
            last_s2red = wlo
            B.stream("sH0")
            last_dmaT[par] = B.dma("sync", self.h0_h[i * 128:(i + 1) * 128, :], h0b[:, :], waits=[wh0])
            last_dmaL[par] = B.dma("sync", self.h0l_h[i * 128:(i + 1) * 128, :], lob[:, :], waits=[wlo])
        wallH = ("sH0H", self.now("sH0H"))
        A.alloc("cva", 2432)
        conv_done = []
        strip_rdy = [None, None]
        for dc in range(8):
            sp = dc % 2
            slo = sp * 1280
            tw = None
            for i in range(20):
                wts = [wallH] if i == 0 else []
                if i == 0 and strip_rdy[sp] is not None:
                    wts = [wallH, strip_rdy[sp]]
                B.stream("sST")
                tw = B.dmaT("scalar", A.bf("h0s", slo * 2 + i * 128, slo * 2 + (i + 1) * 128),
                            self.h0_h[i * 128:(i + 1) * 128, dc * 128:(dc + 1) * 128], waits=wts)
            twl = None
            for i in range(20):
                B.stream("sST")
                twl = B.dmaT("scalar", A.bf("h0l", slo * 2 + i * 128, slo * 2 + (i + 1) * 128),
                             self.h0l_h[i * 128:(i + 1) * 128, dc * 128:(dc + 1) * 128], waits=[])
            hiv = A.bf("h0s", slo * 2, slo * 2 + 2560)
            lov = A.bf("h0l", slo * 2, slo * 2 + 2560)
            h0fv = A.f32("h0f", 0, 2560)
            wcomb = B.ve(lambda q, h0fv=h0fv, hiv=hiv, lov=lov: q.tensor_tensor(
                h0fv, hiv, lov, ALU.add), waits=[tw, twl] + (A.barrier["h0f"] if dc == 0 else []))
            strip_rdy[sp] = wcomb
            CW0 = self.CN["CW"] + 4 * dc
            waits = [] if dc < 1 else [conv_done[dc - 1]]
            acc = A.f32("cva")
            h0v = [A.f32("h0f", 125 + j, 125 + j + NT) for j in range(4)]
            hbv = A.bf("hbf", dc * 2432, (dc + 1) * 2432)
            B.ve(lambda q, acc=acc, h0v=h0v, CW0=CW0: q.tensor_scalar(
                acc, h0v[0], self.consts[:, CW0:CW0 + 1], None, ALU.mult), waits=waits)
            for j in (1, 2, 3):
                B.ve(lambda q, acc=acc, h0v=h0v, j=j, CW0=CW0: q.scalar_tensor_tensor(
                    acc, h0v[j], self.consts[:, CW0 + j:CW0 + j + 1], acc, ALU.mult, ALU.add))
            wv3 = ("ve", self.now("ve"))
            wcv = B.ac(lambda q, acc=acc, hbv=hbv, dc=dc: q.activation(
                hbv, acc, AF.Silu,
                bias=self.consts[:, self.CN["CB"] + dc:self.CN["CB"] + dc + 1]), waits=[wv3])
            conv_done.append(wcv)
        self.a_done = self.mark()
        A.free("h0s"); A.free("h0l"); A.free("h0f"); A.free("h0lo"); A.free("cva")

    def quant_chain(self, xin, xout, width, blocks, sq_lo, wait, extra_out_wait=None,
                    scr_r=None, scr_t2=None, sq=None):
        B = self
        sq = self.smallq if sq is None else sq
        BM, SC, RS = sq_lo, sq_lo + blocks, sq_lo + 2 * blocks
        r = self.scr[4] if scr_r is None else scr_r
        t2 = self.scr[5] if scr_t2 is None else scr_t2
        Pq = xin.shape[0]
        sqv = (self.smallq if sq is None else sq)[0:Pq, :]
        wb = B.ve(lambda q: q.tensor_reduce(
            sqv[:, BM:BM + blocks].rearrange("p (b o) -> p b o", o=1),
            xin[:, :].rearrange("p (b k) -> p b k", k=16),
            mybir.AxisListType.X, ALU.max, apply_absolute_value=True), waits=[wait])
        B.ve(lambda q: q.tensor_scalar(sqv[:, SC:SC + blocks], sqv[:, BM:BM + blocks],
                                       1.0 / 6.0, 1e-12, ALU.mult, ALU.max))
        wsc = B.ve(lambda q: q.reciprocal(sqv[:, RS:RS + blocks], sqv[:, SC:SC + blocks]))
        r = r[0:Pq, 0:width]; t2 = t2[0:Pq, 0:width]
        wr = B.ve(lambda q: q.tensor_tensor(
            r[:, 0:width], xin[:, :],
            sqv[:, RS:RS + blocks].rearrange("p (b o) -> p b o", o=1).broadcast_to([Pq, blocks, 16]),
            ALU.mult), waits=[wsc])
        B.ve(lambda q: q.tensor_scalar(t2[:, 0:width], r[:, 0:width], MAGIC, -MAGIC, ALU.add, ALU.add))
        B.ve(lambda q: q.tensor_scalar(t2[:, 0:width], t2[:, 0:width], 2.0, -2.0, ALU.min, ALU.max))
        B.ve(lambda q: q.tensor_scalar(xin[:, :], r[:, 0:width], VELT, None, ALU.mult))
        B.ve(lambda q: q.tensor_tensor(r[:, 0:width], xin[:, :], r[:, 0:width], ALU.subtract))
        B.ve(lambda q: q.tensor_tensor(r[:, 0:width], xin[:, :], r[:, 0:width], ALU.subtract))
        B.ve(lambda q: q.tensor_scalar(xin[:, :], r[:, 0:width], 2.0, -2.0, ALU.min, ALU.max))
        B.ve(lambda q: q.scalar_tensor_tensor(r[:, 0:width], xin[:, :], -1.0, r[:, 0:width], ALU.mult, ALU.add))
        B.ve(lambda q: q.tensor_tensor(r[:, 0:width], t2[:, 0:width], r[:, 0:width], ALU.add))
        eow = (extra_out_wait if isinstance(extra_out_wait, list)
               else ([extra_out_wait] if extra_out_wait else []))
        wq = B.ve(lambda q: q.tensor_tensor(
            xout[:, 0:width], r[:, 0:width],
            sqv[:, SC:SC + blocks].rearrange("p (b o) -> p b o", o=1).broadcast_to([Pq, blocks, 16]),
            ALU.mult), waits=eow)
        return wq

    def ph_b(self):
        self.stream(None)
        """Projections q/k/v/sd/aumid/bb + norms stats + ab; spills to HBM."""
        B, A = self, self.A
        WP = self.WPO
        A.alloc("wproj", 10528)
        B.stream("sWP")
        for d in range(8):
            rows = slice(d * 128, (d + 1) * 128)
            base = d * 2632
            B.dma("gpsimd", A.bf("wproj", base + 0, base + 512), self.wq_d[rows, :],
                  waits=A.barrier["wproj"] if d == 0 else ())
            B.dma("gpsimd", A.bf("wproj", base + 512, base + 1024), self.wk_d[rows, :])
            B.dma("gpsimd", A.bf("wproj", base + 1024, base + 2048), self.wv_d[rows, :])
            B.dma("gpsimd", A.bf("wproj", base + 2048, base + 2560), self.wsp_d[rows, :])
            B.dma("gpsimd", A.bf("wproj", base + 2560, base + 2624), self.wau_d[rows, :])
            B.dma("gpsimd", A.bf("wproj", base + 2624, base + 2632), self.wbe_d[rows, :])
        wld = ("sWPG", self.now("sWPG"))
        bank_guard = {b: None for b in range(7)}
        sq_guard = None
        scrb_guard = {}
        pbh_guard = None
        for i in range(19):
            par = i % 2
            hsl = lambda dc, i=i: A.bf("hbf", dc * 2432 + i * 128, dc * 2432 + (i + 1) * 128)
            mm_waits = (self.a_done + [wld]) if i == 0 else []
            blocks = [(0, WP["q"], 512), (1, WP["k"], 512), (2, WP["v"], 512),
                      (3, WP["v"] + 512, 512), (4, WP["sd"], 512), (5, WP["au"], 72)]
            for dc in range(8):
                lhs = A.bf("hbf", dc * 2432 + i * 128, dc * 2432 + (i + 1) * 128)
                for (bank, wo, wn) in blocks:
                    waits = list(mm_waits)
                    if dc == 0 and bank_guard[bank] is not None:
                        waits.append(bank_guard[bank])
                    rhs = A.bf("wproj", dc * 2632 + wo, dc * 2632 + wo + wn)
                    B.pe(lambda q, lhs=lhs, rhs=rhs, bank=bank, wn=wn, dc=dc: q.matmul(
                        self.pb[bank][:, 0:wn], lhs, rhs,
                        start=(dc == 0), stop=(dc == 7 and bank != 5)), waits=waits)
            wbias = B.pe(lambda q: q.matmul(self.pb[5][:, 0:72], self.onesb[0:1, :],
                                            self.biasb[0:1, 1672:1744], start=False, stop=True))
            wpe = ("pe", self.now("pe"))
            wsq1 = B.ac(lambda q: q.activation(self.sqscr[:, 0:512], self.pb[0][:, 0:512], AF.Square),
                        waits=[wpe] + ([sq_guard] if sq_guard else []))
            wr1 = B.ve(lambda q, i=i: q.tensor_reduce(
                self.nrm2[:, i * 16:i * 16 + 8].rearrange("p (b o) -> p b o", o=1),
                self.sqscr[:, 0:512].rearrange("p (b k) -> p b k", k=64),
                mybir.AxisListType.X, ALU.add), waits=[wsq1])
            qb = self.scrb[par]
            g = scrb_guard.get(("qk", par))
            wc1 = B.ac(lambda q, qb=qb: q.activation(qb[:, 0:512], self.pb[0][:, 0:512], AF.Copy),
                       waits=[wpe] + ([g] if g else []))
            B.stream("sQK")
            scrb_guard[("q", par)] = B.dma("gpsimd", self.qr_h[i * 128:(i + 1) * 128, :], qb[:, 0:512], waits=[wc1])
            bank_guard[0] = wc1
            wsq2 = B.ac(lambda q: q.activation(self.sqscr[:, 512:1024], self.pb[1][:, 0:512], AF.Square),
                        waits=[wpe, wr1])
            sq_guard = B.ve(lambda q, i=i: q.tensor_reduce(
                self.nrm2[:, i * 16 + 8:i * 16 + 16].rearrange("p (b o) -> p b o", o=1),
                self.sqscr[:, 512:1024].rearrange("p (b k) -> p b k", k=64),
                mybir.AxisListType.X, ALU.add), waits=[wsq2])
            kb = self.scrb[par]
            g = scrb_guard.get(("qk", par))
            wc2 = B.ac(lambda q, kb=kb: q.activation(kb[:, 512:1024], self.pb[1][:, 0:512], AF.Copy),
                       waits=[wpe] + ([g] if g else []))
            B.stream("sQK")
            kdma = B.dma("gpsimd", self.kr_h[i * 128:(i + 1) * 128, :], kb[:, 512:1024], waits=[wc2])
            scrb_guard[("qk", par)] = kdma
            bank_guard[1] = wc2
            vb = self.scrb[2 + par]
            g = scrb_guard.get(("v", par))
            wc3 = B.ac(lambda q, vb=vb: q.activation(vb[:, 0:512], self.pb[2][:, 0:512], AF.Copy),
                       waits=[wpe] + ([g] if g else []))
            wc4 = B.ac(lambda q, vb=vb: q.activation(vb[:, 512:1024], self.pb[3][:, 0:512], AF.Copy))
            B.stream("sV")
            scrb_guard[("v", par)] = B.dma("gpsimd", self.v_h[i * 128:(i + 1) * 128, :], vb[:, 0:1024], waits=[wc4])
            bank_guard[2] = wc3
            bank_guard[3] = wc4
            sdf = self.scr[par]
            g = scrb_guard.get(("sd", par))
            wc5 = B.ve(lambda q, sdf=sdf, i=i: q.tensor_scalar(
                sdf[:, 0:512], self.pb[4][:, 0:512], self.sdm_s[:, i:i + 1], None, ALU.mult),
                waits=[wpe] + ([g] if g else []))
            B.stream("sSD")
            scrb_guard[("sd", par)] = B.dma("sync", self.sd_h[i * 128:(i + 1) * 128, :], sdf[:, 0:512], waits=[wc5])
            bank_guard[4] = wc5
            wc6 = B.ac(lambda q, i=i: q.activation(self.aum[:, i * 64:(i + 1) * 64], self.pb[5][:, 0:64], AF.Silu),
                       waits=[wpe])
            wc7 = B.ve(lambda q, i=i: q.tensor_copy(self.bbsb[:, i * 8:(i + 1) * 8], self.pb[5][:, 64:72]),
                       waits=[wpe])
            bank_guard[5] = wc7
            wtr = B.pe(lambda q, i=i: q.transpose(self.pbh[0:64, 0:128], self.aum[:, i * 64:(i + 1) * 64],
                                                  self.identb[:, :]),
                       waits=[wc6] + ([pbh_guard] if pbh_guard else []))
            wc8 = B.ac(lambda q, i=i: q.activation(self.aumT[0:64, i * 128:(i + 1) * 128],
                                                   self.pbh[0:64, 0:128], AF.Copy), waits=[wtr])
            pbh_guard = wc8
            wab_waits = [wc8]
            if bank_guard[6] is not None:
                wab_waits.append(bank_guard[6])
            B.pe(lambda q, i=i: q.matmul(self.pb[6][:, 0:512], self.aumT[0:64, i * 128:(i + 1) * 128],
                                         self.wadb[:, :], start=True, stop=False), waits=wab_waits)
            wab = B.pe(lambda q: q.matmul(self.pb[6][:, 0:512], self.onesb[0:1, :],
                                          self.biasb[0:1, 64:576], start=False, stop=True))
            abf = self.scr[2 + par]
            g = scrb_guard.get(("ab", par))
            wc9 = B.ac(lambda q, abf=abf: q.activation(abf[:, 0:512], self.pb[6][:, 0:512], AF.Copy),
                       waits=[wab] + ([g] if g else []))
            B.stream("sAB")
            scrb_guard[("ab", par)] = B.dma("sync", self.ab_h[i * 128:(i + 1) * 128, :], abf[:, 0:512], waits=[wc9])
            bank_guard[6] = wc9
        wl = B.ac(lambda q: q.activation(self.rnsb[:, :], self.nrm2[:, :], AF.Ln), waits=[sq_guard])
        we = B.ac(lambda q: q.activation(self.rnsb[:, :], self.rnsb[:, :], AF.Exp, scale=0.5), waits=[wl])
        wa = B.ve(lambda q: q.tensor_scalar(self.rnsb[:, :], self.rnsb[:, :], 1e-6, None, ALU.add), waits=[we])
        wrc = B.ve(lambda q: q.reciprocal(self.rnsb[:, :], self.rnsb[:, :]))
        B.stream("sAB")
        B.dma("sync", self.rn_h.rearrange("(a p) w -> p a w", p=128), self.rnsb[:, :].rearrange("p (a w) -> p a w", w=16), waits=[wrc])
        B.dma("sync", self.bb_h.rearrange("(a p) w -> p a w", p=128), self.bbsb[:, :].rearrange("p (a w) -> p a w", w=8))
        self.b_done = self.mark()
        if "wproj" not in getattr(self, "_dump_names", []):
            A.free("wproj")
        A.free("hbf")

    def ph_c(self):
        self.stream(None)
        """Segment-parallel rescaled mem/spike scan."""
        B, A = self, self.A
        A.alloc("sdseg", 8192)
        A.alloc("dbuf", 8960)
        A.alloc("spkc", 4480)
        sd3 = self.sd_h.rearrange("t (g c) -> g t c", g=4)
        sseg = [A.f32("sdseg", 0, 4096).rearrange("p (i c) -> p i c", c=128),
                A.f32("sdseg", 4096, 8192).rearrange("p (i c) -> p i c", c=128)]
        CLEN = [32] * 8 + [6]
        buf_free = [None, None]
        chunk_ld = [None] * 9

        def load_chunk(cc):
            par = cc % 2
            i0 = cc * 32
            w0 = [x for x in (buf_free[par],) if x]
            if cc < 2:
                w0 = w0 + A.barrier["sdseg"] + self.b_done
            for s in range(NSEG):
                eng = "sync" if s % 2 == 0 else "scalar"
                wts = w0 if s < 2 else ()
                chunk_ld[cc] = B.dma(eng, sseg[par][4 * s:4 * s + 4, 0:CLEN[cc], :],
                                     sd3[:, s * LSEG + i0:s * LSEG + i0 + CLEN[cc], :],
                                     waits=wts, sem="sC0" if par == 0 else "sC1")

        load_chunk(0)
        load_chunk(1)
        wz = B.ve(lambda q: q.memset(self.memM[:, :], 0.0), waits=self.b_done)
        dbv = A.f32("dbuf")
        for i in range(STEPS):
            cc, par, col = i // 32, (i // 32) % 2, i % 32
            sdcol = sseg[par][:, col, :]
            scale = float(np.float32(0.9) ** (-np.float64(i))) if False else float(0.9 ** (-i))
            thr = float(0.5 * 0.9 ** (-i))
            w1 = [chunk_ld[cc]] if col == 0 else []
            B.ve(lambda q, sdcol=sdcol, scale=scale: q.scalar_tensor_tensor(
                self.memM[:, :], sdcol, scale, self.memM[:, :], ALU.mult, ALU.add), waits=w1)
            if i >= WSEG:
                dcol = dbv[:, (i - WSEG) * 128:(i - WSEG + 1) * 128]
            else:
                dcol = self.dscr[:, (i % 2) * 128:(i % 2) * 128 + 128]
            B.ve(lambda q, dcol=dcol, thr=thr: q.tensor_scalar(
                dcol, self.memM[:, :], thr, -thr, ALU.is_gt, ALU.mult))
            wlast = B.ve(lambda q, dcol=dcol: q.tensor_tensor(
                self.memM[:, :], self.memM[:, :], dcol, ALU.add))
            if col == CLEN[cc] - 1:
                buf_free[par] = wlast
                if cc + 2 <= 8:
                    load_chunk(cc + 2)
        spk_ap, dbuf_ap = A.bf("spkc"), A.f32("dbuf")
        wsp = B.ve(lambda q, o=spk_ap, i=dbuf_ap: q.tensor_scalar(o, i, -1.0, None, ALU.is_lt),
                   waits=A.barrier["spkc"])
        spdst = self.spk_h.rearrange("(s iw) (g c) -> s g iw c", s=NSEG, g=4)
        spsrc = A.bf("spkc").rearrange("p (i c) -> p i c", c=128)
        B.stream("sSP")
        for s in range(NSEG):
            B.dma("sync" if s % 2 == 0 else "scalar", spdst[s], spsrc[4 * s:4 * s + 4, :, :],
                  waits=[wsp] if s < 2 else ())
        self.c_done = self.mark()
        A.free("sdseg"); A.free("dbuf"); A.free("spkc")

    def ph_w4(self):
        self.stream(None)
        """fp4 weight quant (my 1/8 shard) + AllGather + transposed/bf16 spills."""
        B, A = self, self.A
        A.alloc("w4a", 2816)
        A.alloc("w4b", 1408)
        A.alloc("w4r", 2816)
        A.alloc("w4t", 2816)
        A.alloc("w4q", 1024)
        win = A.f32("w4a"); wr = A.f32("w4r"); wt2 = A.f32("w4t"); wsq = A.f32("w4q")
        wout = A.bf("w4b")
        B.stream("sW4")
        jobs = [(self.wg_my_d, self.wgq_my, [128, 128, 96], 1024, 64),
                (self.wu_my_d, self.wuq_my, [128, 128, 96], 1024, 64),
                (self.wd_my_d, self.wdq_my, [128], 2816, 176)]
        wprev = None
        for (srcd, dstd, tiles, width, blocks) in jobs:
            r0 = 0
            for P in tiles:
                wts = list(A.barrier["w4a"]) if (r0 == 0 and srcd is self.wg_my_d) else []
                if wprev:
                    wts.append(wprev)
                wld = B.dma("gpsimd", win[0:P, 0:width], srcd[r0:r0 + P, :], waits=wts)
                wq = self.quant_chain(win[0:P, 0:width], wout[0:P, 0:width], width, blocks,
                                      0, wld, extra_out_wait=wprev, scr_r=wr, scr_t2=wt2, sq=wsq)
                wprev = B.dma("gpsimd", dstd[r0:r0 + P, :], wout[0:P, 0:width], waits=[wq])
                r0 += P
        wg_all = self.mark()
        for (mine, alla) in ((self.wgq_my, self.wgq_all), (self.wuq_my, self.wuq_all),
                             (self.wdq_my, self.wdq_all)):
            self.op("gpsimd", (lambda q, mine=mine, alla=alla: q.collective_compute(
                "AllGather", ALU.bypass, replica_groups=[list(range(8))],
                ins=[mine.opt()], outs=[alla.opt()])), waits=wg_all, inc="cc", amt=1)
        self.w4cc = ("cc", self.now("cc"))
        A.free("w4a"); A.free("w4b"); A.free("w4r"); A.free("w4t"); A.free("w4q")
        A.alloc("w4c", 512)
        A.alloc("w4x", 512)
        self.w4_fill = self._w4_filler()

    def _w4_filler(self):
        B, A = self, self.A
        wcast = None
        for (srcd, dstd) in ((self.wf1_d, self.wf1b_h), (self.wf3_d, self.wf3b_h),
                             (self.wf2_d, self.wf2b_h)):
            nqc = 4 if srcd is not self.wf2_d else 1
            for ch in range(8 if srcd is not self.wf2_d else 32):
                rows = slice(ch * 128, (ch + 1) * 128)
                for qc in range(nqc):
                    B.stream("sFC")
                    wts = []
                    if wcast:
                        wts = [wcast]
                    cols = slice(qc * 1024, (qc + 1) * 1024)
                    wl = B.dma("gpsimd", A.bf("w4c", 0, 1024)[:, 0:1024], srcd[rows, cols], waits=wts)
                    wcast = B.dma("gpsimd", dstd[rows, cols], A.bf("w4c", 0, 1024)[:, 0:1024], waits=[wl])
                    yield
        xsc = [A.bf("w4x", j * 128, (j + 1) * 128) for j in range(8)]
        wx = [None] * 8
        first = True
        for t in range(OTT):
            for dc in range(8):
                B.stream("sXT")
                j = (t * 8 + dc) % 4
                wts = list(A.barrier["w4x"]) if first else ([wx[j]] if wx[j] else [])
                first = False
                wt = B.dmaT("scalar", xsc[j], self.xq_h[t * 128:(t + 1) * 128, dc * 128:(dc + 1) * 128],
                            waits=wts)
                wx[j] = B.dma("sync", self.xqT_h[dc * 128:(dc + 1) * 128, t * 128:(t + 1) * 128],
                              xsc[j], sem=f"wp{j}", waits=[wt])
                yield
        wy = [None] * 8
        for (alla, dstT, R, C) in ((self.wgq_all, self.wgqT_h, NQW, 1024),
                                   (self.wuq_all, self.wuqT_h, NQW, 1024),
                                   (self.wdq_all, self.wdqT_h, 1024, NQW)):
            for rb in range(R // 128):
                for dc in range(C // 128):
                    B.stream("sTS")
                    j = (rb * (C // 128) + dc) % 4
                    wts = [self.w4cc] + ([wy[j]] if wy[j] else [])
                    wt = B.dmaT("scalar", xsc[4 + j], alla[rb * 128:(rb + 1) * 128, dc * 128:(dc + 1) * 128],
                                waits=wts)
                    wy[j] = B.dma("sync", dstT[dc * 128:(dc + 1) * 128, rb * 128:(rb + 1) * 128],
                                  xsc[4 + j], sem=f"wp{j}", waits=[wt])
                    yield

    def w4_drip(self, n):
        g = getattr(self, "w4_fill", None)
        if g is None:
            return
        cur = getattr(self, "_cur_stream", None)
        for _ in range(n):
            try:
                next(g)
            except StopIteration:
                self.w4_fill = None
                break
        self.stream(cur)

    def ph_def_(self):
        B, A = self, self.A
        SO = self.SO
        wz = B.ve(lambda q: q.memset(self.zsb[:, :], 0.0), waits=self.c_done)
        self.zw = B.ve(lambda q: q.memset(self.zsbb[:, :], 0.0))
        self.yst = []
        import os as _os
        for hf in range(int(_os.environ.get("KDBG_NHF", 2))):
            self.half_def(hf)
        self.w4_drip(10 ** 9)
        A.free("w4c"); A.free("w4x")
        self.w4_done = self.mark()
        self.def_done = self.mark()

    def load_scb(self, dst3, dram, row0, width, queue="sync", waits=()):
        B = self
        w1 = B.dma(queue, dst3[64:128, 0:1, :], dram[row0:row0 + 64, None, 0:width], waits=waits)
        w2 = B.dma(queue, dst3[:, 1:9, :],
                   dram[row0 + 64:row0 + 1088, 0:width].rearrange("(a p) w -> p a w", p=128), waits=waits)
        w3 = B.dma(queue, dst3[0:32, 9:10, :], dram[row0 + 1088:row0 + 1120, None, 0:width], waits=waits)
        m1 = B.ve(lambda q: q.memset(dst3[0:64, 0:1, :], 0.0), waits=waits)
        m2 = B.ve(lambda q: q.memset(dst3[32:64, 9:10, :], 0.0))
        m3 = B.ve(lambda q: q.memset(dst3[64:128, 9:10, :], 0.0))
        return [w1, w2, w3, m1, m2, m3]

    def store_scb(self, dram, src3, row0, width, queue="sync", waits=()):
        B = self
        w1 = B.dma(queue, dram[row0:row0 + 64, None, 0:width], src3[64:128, 0:1, :], waits=waits)
        w2 = B.dma(queue, dram[row0 + 64:row0 + 1088, 0:width].rearrange("(a p) w -> p a w", p=128),
                   src3[:, 1:9, :], waits=waits)
        w3 = B.dma(queue, dram[row0 + 1088:row0 + 1120, None, 0:width], src3[0:32, 9:10, :], waits=waits)
        return [w1, w2, w3]

    def half_def(self, hf):
        B, A, SO = self, self.A, self.SO
        self.stream(None)
        s0 = hf * HLF
        nt0 = WMEM + s0
        sq = self.smallq
        RNS, ACT01, BETM, BBS, BSV = 0, 160, 240, 320, 400
        A.alloc("spk", 2560); A.alloc("abb", 5120); A.alloc("alfa", 5120)
        spk3 = A.bf("spk").rearrange("p (a w) -> p a w", w=512)
        abb3 = A.f32("abb").rearrange("p (a w) -> p a w", w=512)
        B.stream("sDL")
        wl1 = self.load_scb(spk3, self.spk_h, s0, 512, "sync", waits=A.barrier["spk"])
        wl2 = self.load_scb(abb3, self.ab_h, nt0, 512, "scalar", waits=A.barrier["abb"])
        rns3 = sq[:, RNS:RNS + 160].rearrange("p (a w) -> p a w", w=16)
        wrns = self.load_scb(rns3, self.rn_h, nt0, 16, "gpsimd", waits=self.c_done)
        bbs3 = sq[:, BBS:BBS + 80].rearrange("p (a w) -> p a w", w=8)
        self.load_scb(bbs3, self.bb_h, nt0, 8, "gpsimd", waits=self.c_done)
        wld = self.mark()
        spkf = A.bf("spk")
        spkv = A.bf("spk").rearrange("p (a w) -> p a w", w=512)
        alfa = A.f32("alfa")
        aspk_b = self.bcast[:, 0:512].rearrange("p (o w) -> p o w", o=1).broadcast_to([128, 10, 512])
        bspk_b = self.bcast[:, 512:1024].rearrange("p (o w) -> p o w", o=1).broadcast_to([128, 10, 512])
        abbf = A.f32("abb")
        B.ve(lambda q: q.tensor_tensor(alfa.rearrange("p (a w) -> p a w", w=512), spkv, aspk_b, ALU.mult),
             waits=wld + A.barrier["alfa"])
        B.ve(lambda q: q.tensor_tensor(alfa, alfa, abbf, ALU.add))
        wsg = B.ac(lambda q: q.activation(alfa, alfa, AF.Sigmoid), waits=[("ve", self.now("ve"))])
        spk64 = A.bf("spk").rearrange("p (a w) -> p a w", w=64)
        B.ve(lambda q: q.tensor_reduce(sq[:, ACT01:ACT01 + 80].rearrange("p (b o) -> p b o", o=1),
                                       spk64, mybir.AxisListType.X, ALU.add), waits=wld)
        B.ve(lambda q: q.tensor_scalar(sq[:, ACT01:ACT01 + 80], sq[:, ACT01:ACT01 + 80],
                                       0.5, None, ALU.is_gt))
        A.alloc("dd", 5120)
        ddf = A.f32("dd")
        B.ve(lambda q: q.tensor_tensor(ddf.rearrange("p (a w) -> p a w", w=512), spkv, bspk_b, ALU.mult),
             waits=A.barrier["dd"])
        B.ve(lambda q: q.tensor_reduce(sq[:, BSV:BSV + 80].rearrange("p (b o) -> p b o", o=1),
                                       ddf.rearrange("p (b k) -> p b k", k=64),
                                       mybir.AxisListType.X, ALU.add))
        B.ve(lambda q: q.tensor_tensor(sq[:, BSV:BSV + 80], sq[:, BSV:BSV + 80], sq[:, BBS:BBS + 80], ALU.add))
        wbt = B.ac(lambda q: q.activation(sq[:, BETM:BETM + 80], sq[:, BSV:BSV + 80], AF.Sigmoid),
                   waits=[("ve", self.now("ve"))])
        B.ve(lambda q: q.tensor_tensor(sq[:, BETM:BETM + 80], sq[:, BETM:BETM + 80],
                                       sq[:, ACT01:ACT01 + 80], ALU.mult), waits=[wbt])
        act_b = sq[:, ACT01:ACT01 + 80].rearrange("p (b o) -> p b o", o=1).broadcast_to([128, 80, 64])
        B.ve(lambda q: q.tensor_scalar(alfa, alfa, -1.0, None, ALU.add), waits=[wsg])
        B.ve(lambda q: q.tensor_tensor(alfa.rearrange("p (b k) -> p b k", k=64),
                                       alfa.rearrange("p (b k) -> p b k", k=64), act_b, ALU.mult))
        wal = B.ve(lambda q: q.tensor_scalar(alfa, alfa, 1.0, None, ALU.add))
        A.free("spk"); A.free("abb")
        A.alloc("alfm", 5120)
        alfm = A.f32("alfm")
        wtr = self.pe_transpose_block(alfa, alfm, wal, A.barrier["alfm"])
        A.free("alfa")
        A.alloc("pfm", 5120)
        pfm = A.f32("pfm")
        d0 = ddf[:, 0:1280]; d1 = ddf[:, 1280:2560]
        wp = []
        for g in range(4):
            sl = slice(g * 1280, (g + 1) * 1280)
            B.ve(lambda q, sl=sl: q.memset(alfm[:, sl.start:sl.start + 64], 1.0), waits=[wtr])
            B.ve(lambda q, sl=sl: q.memset(alfm[:, sl.start + 1184:sl.start + 1280], 1.0))
            B.ve(lambda q, sl=sl: q.tensor_tensor(d1, alfm[:, sl], self.cmsk[:, :], ALU.mult))
            B.ve(lambda q, sl=sl: q.tensor_tensor(d0, alfm[:, sl], d1, ALU.subtract))
            wp.append(B.ve(lambda q, sl=sl: q.tensor_tensor_scan(
                pfm[:, sl], d0, d1, 1.0, ALU.mult, ALU.add),
                waits=A.barrier["pfm"] if g == 0 else ()))
        wpe_ = []
        for g in range(4):
            srcap = pfm[:, g * 1280 + 95: g * 1280 + 95 + 32 * NCH2].rearrange("p (c o) -> p c o", o=32)[:, :, 0:1]
            wpe_.append(B.ve(lambda q, srcap=srcap, g=g: q.tensor_copy(
                self.pend[:, g * NCH2:(g + 1) * NCH2].rearrange("p (c o) -> p c o", o=1), srcap),
                waits=[wp[g]]))
        A.alloc("prow", 5120)
        prow = A.f32("prow")
        wtr2 = self.pe_transpose_block(pfm, prow, wp[3], A.barrier["prow"], from_fm=True)
        A.free("alfm")
        A.alloc("rpr", 5120)
        rpr = A.f32("rpr")
        wrp = B.ve(lambda q: q.reciprocal(rpr, prow), waits=[wtr2] + A.barrier["rpr"])
        A.free("pfm")
        A.alloc("qraw", 2560); A.alloc("kraw", 2560)
        B.stream("sDL")
        q3 = A.bf("qraw").rearrange("p (a w) -> p a w", w=512)
        k3 = A.bf("kraw").rearrange("p (a w) -> p a w", w=512)
        wq1 = self.load_scb(q3, self.qr_h, nt0, 512, "sync", waits=A.barrier["qraw"])
        wk1 = self.load_scb(k3, self.kr_h, nt0, 512, "scalar", waits=A.barrier["kraw"])
        qrn_b = sq[:, RNS:RNS + 160].rearrange("p (a w) -> p a w", w=16)[:, :, 0:8]
        krn_b = sq[:, RNS:RNS + 160].rearrange("p (a w) -> p a w", w=16)[:, :, 8:16]
        B.ve(lambda q: q.tensor_tensor(
            sq[:, BSV:BSV + 80].rearrange("p (a w) -> p a w", w=8), krn_b,
            sq[:, BETM:BETM + 80].rearrange("p (a w) -> p a w", w=8), ALU.mult), waits=wrns)
        A.alloc("brow", 2560); A.alloc("arow", 2560); A.alloc("qtrow", 2560)
        kraw80 = A.bf("kraw").rearrange("p (a w k) -> p a w k", w=8, k=64)
        qraw80 = A.bf("qraw").rearrange("p (a w k) -> p a w k", w=8, k=64)
        dd80 = ddf.rearrange("p (a w k) -> p a w k", w=8, k=64)
        krn_e = krn_b[:, :, :, None].broadcast_to([128, 10, 8, 64])
        qrn_e = qrn_b[:, :, :, None].broadcast_to([128, 10, 8, 64])
        kb_e = sq[:, BSV:BSV + 80].rearrange("p (a w) -> p a w", w=8)[:, :, :, None].broadcast_to([128, 10, 8, 64])
        B.ve(lambda q: q.tensor_tensor(dd80, kraw80, krn_e, ALU.mult), waits=list(wk1) + wrns)
        _brow_ap = A.bf("brow")
        wbr = B.ve(lambda q, o=_brow_ap: q.tensor_tensor(o, ddf, prow, ALU.mult),
                   waits=[wtr2] + A.barrier["brow"])
        B.ve(lambda q: q.tensor_tensor(dd80, kraw80, kb_e, ALU.mult))
        _arow_ap = A.bf("arow")
        war = B.ve(lambda q, o=_arow_ap: q.tensor_tensor(o, ddf, rpr, ALU.mult), waits=A.barrier["arow"])
        B.ve(lambda q: q.tensor_tensor(dd80, qraw80, qrn_e, ALU.mult), waits=list(wq1) + wrns)
        _qtrow_ap = A.bf("qtrow")
        wqr = B.ve(lambda q, o=_qtrow_ap: q.tensor_tensor(o, ddf, prow, ALU.mult),
                   waits=[wtr2] + A.barrier["qtrow"])
        A.free("prow"); A.free("rpr"); A.free("dd"); A.free("qraw"); A.free("kraw")
        A.alloc("bcol", 2560); A.alloc("acol", 2560); A.alloc("qtcol", 2560)
        wcols = None
        B.stream("sCT")
        for (rown, coln, wrow) in (("brow", "bcol", wbr), ("arow", "acol", war), ("qtrow", "qtcol", wqr)):
            rsrc = A.bf(rown)
            cdst = A.bf(coln)
            for blk in range(10):
                for g in range(4):
                    wts = ([wrow] + A.barrier[coln]) if blk == 0 and g == 0 else ()
                    wcols = B.dmaT("scalar",
                                   cdst[:, g * 1280 + blk * 128:g * 1280 + (blk + 1) * 128],
                                   rsrc[:, blk * 512 + g * 128:blk * 512 + (g + 1) * 128], waits=wts)
        A.free("brow"); A.free("qtrow")
        A.alloc("vscb", 5120)
        B.stream("sDL")
        v3 = A.bf("vscb").rearrange("p (a w) -> p a w", w=1024)
        self.load_scb(v3, self.v_h, nt0, 1024, "gpsimd", waits=A.barrier["vscb"])
        A.alloc("negT", 1120); A.alloc("hqT", 1152); A.alloc("yscb", 5120)
        A.alloc("urm", 512); A.alloc("r2r", 256)
        negT = A.bf("negT"); hqT = A.bf("hqT")
        urm = A.bf("urm"); r2rm = A.bf("r2r")
        bcol = A.bf("bcol"); acol = A.bf("acol"); qtcol = A.bf("qtcol")
        CN = self.CN
        wprep = self.mark()
        pbg = {i: None for i in range(7)}
        solg = None
        import os as _os
        _ncl = int(_os.environ.get('KDBG_NCL', NCH2))
        _sec = _os.environ.get('KDBG_SEC', 'z')
        _secord = ["g", "sq", "t", "hq", "r2", "u", "o", "z"]
        _seccut = _secord.index(_sec)
        for cl in range(_ncl):
            self.w4_drip(12)
            fcs = [(h // 2) * 1280 + 64 + cl * 32 for h in range(8)]
            prange = [slice((h % 2) * 64, (h % 2) * 64 + 64) for h in range(8)]
            for (bank, lhs_t, rhs_t) in ((0, bcol, acol), (1, acol, bcol)):
                for h in range(8):
                    wts = []
                    if h == 0:
                        wts = list(wprep) if cl == 0 else []
                        if pbg[bank]:
                            wts.append(pbg[bank])
                    B.pe(lambda q, h=h, bank=bank, lhs_t=lhs_t, rhs_t=rhs_t, fc=fcs[h], pr=prange[h]: q.matmul(
                        self.pb[bank][(h % 4) * 32:(h % 4) * 32 + 32, (h // 4) * 32:(h // 4) * 32 + 32],
                        lhs_t[pr, fc:fc + 32], rhs_t[pr, fc:fc + 32],
                        start=True, stop=True, tile_position=((h % 2) * 64, (h % 4) * 32)), waits=wts)
            wg = ("pe", self.now("pe"))
            wn = B.ve(lambda q: q.tensor_tensor(self.sol[:, SO["N"]:SO["N"] + 64], self.pb[0][:, 0:64],
                                                self.consts[:, CN["MN"]:CN["MN"] + 64], ALU.mult),
                      waits=[wg] + ([solg] if solg else []))
            wnt = B.ve(lambda q: q.tensor_tensor(self.sol[:, SO["NT"]:SO["NT"] + 64], self.pb[1][:, 0:64],
                                                 self.consts[:, CN["MNT"]:CN["MNT"] + 64], ALU.mult))
            pbg[0] = wn; pbg[1] = wnt
            if _seccut <= 0:
                continue
            evac_w = {}

            def sq_mms(dk, lk, rk, bank, wts0):
                for h in range(8):
                    st32 = (h % 4) * 32
                    co = (h // 4) * 32
                    wts = []
                    if h == 0:
                        wts = list(wts0)
                        if pbg[bank]:
                            wts.append(pbg[bank])
                    B.pe(lambda q, lk=lk, rk=rk, bank=bank, st32=st32, co=co: q.matmul(
                        self.pb[bank][st32:st32 + 32, co:co + 32],
                        self.sol[st32:st32 + 32, SO[lk] + co:SO[lk] + co + 32],
                        self.sol[st32:st32 + 32, SO[rk] + co:SO[rk] + co + 32],
                        start=True, stop=True, tile_position=(st32, st32)), waits=wts)

            def sq_evac(dk, bank, wtok):
                ew = B.ac(lambda q, dk=dk, bank=bank: q.activation(
                    self.sol[:, SO[dk]:SO[dk] + 64], self.pb[bank][:, 0:64], AF.Copy),
                    waits=[wtok] if wtok else [])
                evac_w[dk] = ew
                pbg[bank] = ew
                return ew

            def aux_mms(bank, lk, rk, wts0):
                for h in range(8):
                    st32 = (h % 4) * 32
                    co = (h // 4) * 32
                    wts = []
                    if h == 0:
                        wts = list(wts0)
                        if pbg[bank]:
                            wts.append(pbg[bank])
                    B.pe(lambda q, bank=bank, lk=lk, rk=rk, st32=st32, co=co: q.matmul(
                        self.pb[bank][st32:st32 + 32, co:co + 32],
                        self.sol[st32:st32 + 32, SO[lk] + co:SO[lk] + co + 32],
                        self.sol[st32:st32 + 32, SO[rk] + co:SO[rk] + co + 32],
                        start=True, stop=True, tile_position=(st32, st32)), waits=wts)

            sq_mms("N2", "NT", "N", 2, [wnt])
            sq_mms("N2T", "N", "NT", 3, [])
            w_r2 = ("pe", self.now("pe"))
            sq_evac("N2", 2, w_r2); sq_evac("N2T", 3, None)
            sq_mms("N4", "N2T", "N2", 2, [evac_w["N2T"]])
            sq_mms("N4T", "N2", "N2T", 3, [])
            aux_mms(5, "N2", "NT", [evac_w["N2T"]])
            w_r3 = ("pe", self.now("pe"))
            sq_evac("N4", 2, w_r3); sq_evac("N4T", 3, None)
            B.ve(lambda q: q.tensor_tensor(self.sol[:, SO["P1T"]:SO["P1T"] + 64], self.pb[5][:, 0:64],
                                           self.sol[:, SO["N2T"]:SO["N2T"] + 64], ALU.add), waits=[w_r3])
            B.ve(lambda q: q.tensor_tensor(self.sol[:, SO["P1T"]:SO["P1T"] + 64],
                                           self.sol[:, SO["P1T"]:SO["P1T"] + 64],
                                           self.sol[:, SO["NT"]:SO["NT"] + 64], ALU.add))
            wp1 = B.ve(lambda q: q.tensor_tensor(self.sol[:, SO["P1T"]:SO["P1T"] + 64],
                                                 self.sol[:, SO["P1T"]:SO["P1T"] + 64],
                                                 self.consts[:, CN["IP"]:CN["IP"] + 64], ALU.add))
            pbg[5] = wp1
            sq_mms("N8", "N4T", "N4", 2, [evac_w["N4T"]])
            sq_mms("N8T", "N4", "N4T", 3, [])
            w_r4 = ("pe", self.now("pe"))
            sq_evac("N8", 2, w_r4); sq_evac("N8T", 3, None)
            sq_mms("N16", "N8T", "N8", 2, [evac_w["N8T"]])
            aux_mms(4, "N4T", "N8", [evac_w["N8"]])
            w_r5 = ("pe", self.now("pe"))
            sq_evac("N16", 2, w_r5)
            B.ve(lambda q: q.tensor_tensor(self.sol[:, SO["P2"]:SO["P2"] + 64], self.pb[4][:, 0:64],
                                           self.sol[:, SO["N4"]:SO["N4"] + 64], ALU.add), waits=[w_r5])
            B.ve(lambda q: q.tensor_tensor(self.sol[:, SO["P2"]:SO["P2"] + 64],
                                           self.sol[:, SO["P2"]:SO["P2"] + 64],
                                           self.sol[:, SO["N8"]:SO["N8"] + 64], ALU.add))
            wp2 = B.ve(lambda q: q.tensor_tensor(self.sol[:, SO["P2"]:SO["P2"] + 64],
                                                 self.sol[:, SO["P2"]:SO["P2"] + 64],
                                                 self.consts[:, CN["IP"]:CN["IP"] + 64], ALU.add))
            pbg[4] = wp2
            if _seccut <= 1:
                continue
            for h in range(8):
                st32 = (h % 4) * 32; co = (h // 4) * 32
                wts = []
                if h == 0:
                    wts = [wp2, wp1]
                    if pbg[6]:
                        wts.append(pbg[6])
                B.pe(lambda q, st32=st32, co=co: q.matmul(
                    self.pb[6][st32:st32 + 32, co:co + 32],
                    self.sol[st32:st32 + 32, SO["P2"] + co:SO["P2"] + co + 32],
                    self.sol[st32:st32 + 32, SO["P1T"] + co:SO["P1T"] + co + 32],
                    start=True, stop=True, tile_position=(st32, st32)), waits=wts)
            wpe5 = ("pe", self.now("pe"))
            wy1 = B.ac(lambda q: q.activation(self.sol[:, SO["Y1"]:SO["Y1"] + 64],
                                              self.pb[6][:, 0:64], AF.Copy), waits=[wpe5])
            pbg[6] = wy1
            for h in range(8):
                st32 = (h % 4) * 32; co = (h // 4) * 32
                wts = []
                if h == 0:
                    wts = [wy1, evac_w["N16"], wp2]
                B.pe(lambda q, st32=st32, co=co: q.matmul(
                    self.pb[4][st32:st32 + 32, co:co + 32],
                    self.sol[st32:st32 + 32, SO["N16"] + co:SO["N16"] + co + 32],
                    self.sol[st32:st32 + 32, SO["Y1"] + co:SO["Y1"] + co + 32],
                    start=True, stop=True, tile_position=(st32, st32)), waits=wts)
            wpe6 = ("pe", self.now("pe"))
            wneg = B.ve(lambda q, cl=cl: q.scalar_tensor_tensor(
                negT[:, cl * 64:cl * 64 + 64], self.sol[:, SO["Y1"]:SO["Y1"] + 64], -1.0,
                self.pb[4][:, 0:64], ALU.mult, ALU.subtract),
                waits=[wpe6] + (A.barrier["negT"] if cl == 0 else []))
            pbg[4] = wneg
            if _seccut <= 2:
                continue
            vb = (64 + cl * 32) % 128
            vblk = (64 + cl * 32) // 128
            for idx, h in enumerate((0, 2, 4, 6, 1, 3, 5, 7)):
                bk = 3 if h % 2 == 0 else 2
                wts = []
                if idx == 0:
                    wts = [evac_w["N8T"], evac_w["N16"]]
                    if pbg[3]:
                        wts.append(pbg[3])
                elif idx == 4:
                    if pbg[2]:
                        wts = [pbg[2]]
                B.pe(lambda q, h=h, bk=bk, fc=fcs[h], pr=prange[h], vb=vb: q.matmul(
                    self.pb[bk][vb:vb + 32, (h // 2) * 32:(h // 2) * 32 + 32],
                    acol[pr, fc:fc + 32], qtcol[pr, fc:fc + 32],
                    start=True, stop=True, tile_position=((h % 2) * 64, vb)), waits=wts)
            wpe7 = ("pe", self.now("pe"))
            whq = B.ve(lambda q, cl=cl, vb=vb: q.tensor_tensor(
                hqT[vb:vb + 32, (cl // 4) * 256:(cl // 4) * 256 + 128],
                self.pb[3][vb:vb + 32, 0:128],
                self.consts[vb:vb + 32, CN["MHQ"]:CN["MHQ"] + 128], ALU.mult),
                waits=[wpe7] + (A.barrier["hqT"] if cl == 0 else []))
            whq = B.ve(lambda q, cl=cl, vb=vb: q.tensor_tensor(
                hqT[vb:vb + 32, (cl // 4) * 256 + 128:(cl // 4) * 256 + 256],
                self.pb[2][vb:vb + 32, 0:128],
                self.consts[vb:vb + 32, CN["MHQ"] + 128:CN["MHQ"] + 256], ALU.mult))
            pbg[3] = whq
            pbg[2] = whq
            solg = whq
            if _seccut <= 3:
                continue
            zw = getattr(self, "zw", None)
            for h in range(8):
                wts = []
                if h == 0:
                    wts = [wneg, whq]
                    if pbg[0]:
                        wts.append(pbg[0])
                    if zw:
                        wts.append(zw)
                B.pe(lambda q, h=h, fc=fcs[h], pr=prange[h]: q.matmul(
                    self.pb[0][(h % 4) * 32:(h % 4) * 32 + 32, (h // 4) * 128:(h // 4) * 128 + 128],
                    bcol[pr, fc:fc + 32],
                    self.zsbb[pr, (h // 2) * 128:(h // 2) * 128 + 128],
                    start=(h < 4), stop=False, tile_position=((h % 2) * 64, (h % 4) * 32),
                    skip_group_check=True), waits=wts)
            for h in range(8):
                vap = A.bf("vscb", vblk * 1024 + h * 128, vblk * 1024 + (h + 1) * 128, parts=slice(vb, vb + 32))
                B.pe(lambda q, h=h, vb=vb, vap=vap: q.matmul(
                    self.pb[0][(h % 4) * 32:(h % 4) * 32 + 32, (h // 4) * 128:(h // 4) * 128 + 128],
                    self.negib[vb:vb + 32, 0:32], vap,
                    start=False, stop=(h >= 4), tile_position=(vb, (h % 4) * 32),
                    skip_group_check=True),
                    waits=([("pe", self.now("pe"))] if h == 0 else []))
            wpe8 = ("pe", self.now("pe"))
            wr2 = B.ac(lambda q, r2rm=r2rm: q.activation(r2rm[:, 0:512], self.pb[0][:, 0:512], AF.Copy),
                       waits=[wpe8] + (A.barrier["r2r"] if cl == 0 else []))
            pbg[0] = wr2
            if _seccut <= 4:
                continue
            ubank = [1, 2, 5, 6]
            for idx, h in enumerate((0, 4, 1, 5, 2, 6, 3, 7)):
                bank = ubank[h % 4]
                wts = []
                if idx == 0:
                    wts = [wr2]
                if h < 4 and pbg[bank]:
                    wts = wts + [pbg[bank]]
                st32 = (h % 4) * 32
                B.pe(lambda q, h=h, cl=cl, bank=bank, st32=st32, vb=vb: q.matmul(
                    self.pb[bank][vb:vb + 32, (h // 4) * 128:(h // 4) * 128 + 128],
                    negT[st32:st32 + 32, cl * 64 + (h // 4) * 32:cl * 64 + (h // 4) * 32 + 32],
                    r2rm[st32:st32 + 32, (h // 4) * 128:(h // 4) * 128 + 128],
                    start=(h < 4), stop=(h >= 4), tile_position=(st32, vb)), waits=wts)
            wpe9 = ("pe", self.now("pe"))
            wu2 = None
            for h in range(8):
                bank = ubank[h % 4]
                wts = []
                if h == 0:
                    wts = [wpe9] + (A.barrier["urm"] if cl == 0 else [])
                wu2 = B.ac(lambda q, h=h, bank=bank, vb=vb, urm=urm: q.activation(
                    urm[vb:vb + 32, h * 128:(h + 1) * 128],
                    self.pb[bank][vb:vb + 32, (h // 4) * 128:(h // 4) * 128 + 128], AF.Copy),
                    waits=wts)
            for bank in ubank:
                pbg[bank] = wu2
            if _seccut <= 5:
                continue
            for idx, h in enumerate((0, 2, 4, 6, 1, 3, 5, 7)):
                bank = 5 if h < 4 else 6
                wts = []
                if h == 0 or h == 4:
                    wts = [wu2]
                    if pbg[bank]:
                        wts.append(pbg[bank])
                if idx == 4:
                    wts = wts + [("pe", self.now("pe"))]
                B.pe(lambda q, h=h, fc=fcs[h], pr=prange[h], bank=bank, vb=vb: q.matmul(
                    self.pb[bank][vb:vb + 32, (h % 4) * 128:(h % 4) * 128 + 128],
                    qtcol[pr, fc:fc + 32],
                    self.zsbb[pr, (h // 2) * 128:(h // 2) * 128 + 128],
                    start=(h % 4 == 0), stop=False, tile_position=((h % 2) * 64, vb)), waits=wts)
            for h in range(8):
                bank = 5 if h < 4 else 6
                hp = (h % 2) * 128 + (h // 2) * 32
                B.pe(lambda q, h=h, cl=cl, bank=bank, vb=vb, hp=hp: q.matmul(
                    self.pb[bank][vb:vb + 32, (h % 4) * 128:(h % 4) * 128 + 128],
                    hqT[vb:vb + 32, (cl // 4) * 256 + hp:(cl // 4) * 256 + hp + 32],
                    urm[vb:vb + 32, h * 128:(h + 1) * 128],
                    start=False, stop=(h % 4 == 3), tile_position=(vb, vb)),
                    waits=([("pe", self.now("pe"))] if h == 0 else []))
            wpeA = ("pe", self.now("pe"))
            yv = A.bf("yscb")
            wo1 = B.ac(lambda q, vb=vb, vblk=vblk, yv=yv: q.activation(
                yv[vb:vb + 32, vblk * 1024:vblk * 1024 + 512], self.pb[5][vb:vb + 32, 0:512], AF.Copy),
                waits=[wpeA] + (A.barrier["yscb"] if cl == 0 else []))
            wo2 = B.ac(lambda q, vb=vb, vblk=vblk, yv=yv: q.activation(
                yv[vb:vb + 32, vblk * 1024 + 512:vblk * 1024 + 1024], self.pb[6][vb:vb + 32, 0:512], AF.Copy))
            pbg[5] = wo1; pbg[6] = wo2
            if _seccut <= 6:
                continue
            for h in range(8):
                wts = []
                if h == 0:
                    wts = [wneg, wu2]
                    if pbg[4]:
                        wts.append(pbg[4])
                aap = A.bf("arow", vblk * 512 + h * 64, vblk * 512 + (h + 1) * 64, parts=slice(vb, vb + 32))
                B.pe(lambda q, h=h, vb=vb, aap=aap, urm=urm: q.matmul(
                    self.pb[4][(h % 2) * 64:(h % 2) * 64 + 64, (h // 2) * 128:(h // 2) * 128 + 128],
                    aap, urm[vb:vb + 32, h * 128:(h + 1) * 128],
                    start=(h < 2), stop=(h >= 6), tile_position=(vb, (h % 2) * 64),
                    skip_group_check=True), waits=wts)
            wpeB = ("pe", self.now("pe"))
            wzt = B.ve(lambda q: q.tensor_tensor(self.scr[0][:, 0:512], self.zsb[:, :],
                                                 self.pb[4][:, 0:512], ALU.add), waits=[wpeB])
            for g in range(4):
                B.ve(lambda q, g=g, cl=cl: q.tensor_scalar(
                    self.zsb[:, g * 128:(g + 1) * 128], self.scr[0][:, g * 128:(g + 1) * 128],
                    self.pend[:, g * NCH2 + cl:g * NCH2 + cl + 1], None, ALU.mult))
            self.zw = B.ve(lambda q: q.tensor_copy(self.zsbb[:, :], self.zsb[:, :]))
            pbg[4] = wzt
        y3 = A.bf("yscb").rearrange("p (a w) -> p a w", w=1024)
        B.stream("sYS")
        self.yst += self.store_scb(self.y_h, y3, s0, 1024, "sync", waits=[self.now("ac") and ("ac", self.now("ac"))])
        import os as _os
        if not _os.environ.get("KDBG_NOFREE"):
            for nm in ("bcol", "acol", "qtcol", "arow", "vscb", "negT", "hqT", "yscb", "urm", "r2r"):
                A.free(nm)

    def pe_transpose_block(self, src, dst, wsrc, bar, from_fm=False):
        B = self
        wev = None
        pbi = 0
        pguard = [None, None]
        for blk in range(10):
            for g in range(4):
                if from_fm:
                    s = src[:, g * 1280 + blk * 128:g * 1280 + (blk + 1) * 128]
                    d = dst[:, blk * 512 + g * 128:blk * 512 + (g + 1) * 128]
                else:
                    s = src[:, blk * 512 + g * 128:blk * 512 + (g + 1) * 128]
                    d = dst[:, g * 1280 + blk * 128:g * 1280 + (blk + 1) * 128]
                bank = pbi % 2
                pbi += 1
                wts = [wsrc] if (blk == 0 and g == 0) else []
                if pguard[bank]:
                    wts = wts + [pguard[bank]]
                wt = B.pe(lambda q, s=s, bank=bank: q.transpose(
                    self.pb[bank][:, 0:128], s, self.consts[:, self.CN["ID"]:self.CN["ID"] + 128]), waits=wts)
                wev = B.ac(lambda q, d=d, bank=bank: q.activation(d, self.pb[bank][:, 0:128], AF.Copy),
                           waits=[wt] + (bar if blk == 0 and g == 0 else []))
                pguard[bank] = wev
        return wev

    def ph_g(self):
        B, A = self, self.A
        sq = self.smallq
        OT = OTT
        A.alloc("woutb", 4096)
        B.stream("sWP")
        for dc in range(8):
            B.dma("gpsimd", A.bf("woutb", dc * 1024, (dc + 1) * 1024),
                  self.wout_d[dc * 128:(dc + 1) * 128, :],
                  waits=A.barrier["woutb"] if dc == 0 else ())
        wwo = ("sWPG", self.now("sWPG"))
        A.alloc("h2T", 8192)
        A.alloc("xnT", 1024)
        A.alloc("y2T", 1024)
        A.alloc("g1g", 1024)
        A.alloc("g1b", 1024)
        A.alloc("g1a", 128)
        wu1b3 = self.wu1b[:, :]
        whnb = self.bcast[:, 1024:2048]
        bu1row = self.biasb[0:1, 584:648]
        sq_guard = None
        xn_guard = [None, None]
        yb_guard = [None, None]
        xnT_guard = [None, None]
        y2T_guard = [None, None]
        g1g_guard = [None, None]
        g1b_guard = [None, None]
        g1a_guard = [None, None]
        tmpb_guard = [None, None]
        yaf_guard = [[], []]
        pbh_guard = None
        bank_g = {i: None for i in range(7)}
        wsp = None
        for i in range(OT):
            par = i % 2
            xnf = self.scr[par]
            y2f = self.scr[2 + par]
            yaf = self.scr[4 + par]
            yb = self.scrb[par]
            tmpb = self.scrb[2 + par]
            xnTb = A.bf("xnT", par * 1024, (par + 1) * 1024)
            y2Tb = A.bf("y2T", par * 1024, (par + 1) * 1024)
            gateb = A.bf("g1g", par * 1024, (par + 1) * 1024)
            h2b = A.bf("g1b", par * 1024, (par + 1) * 1024)
            au2b = A.bf("g1a", par * 128, (par + 1) * 128, parts=slice(0, 64))
            B.stream("sXL")
            wxl = B.dma("sync", xnf[:, :], self.xn32_h[i * 128:(i + 1) * 128, :],
                        waits=([xn_guard[par]] if xn_guard[par] else []) + (self.def_done if i == 0 else []))
            B.stream("sDL")
            wyl = B.dma("sync", yb[:, :], self.y_h[WMEM + i * 128:WMEM + (i + 1) * 128, :],
                        waits=[yb_guard[par]] if yb_guard[par] else [])
            B.stream("sXT")
            wxT = None
            for dc in range(8):
                wts = []
                if dc == 0:
                    wts = A.barrier["xnT"] if i == 0 else ([xnT_guard[par]] if xnT_guard[par] else [])
                wxT = B.dmaT("scalar", xnTb[:, dc * 128:(dc + 1) * 128],
                             self.xnb_h[i * 128:(i + 1) * 128, dc * 128:(dc + 1) * 128], waits=wts)
            wsq = B.ac(lambda q, yb=yb: q.activation(self.sqscr[:, :], yb[:, :], AF.Square),
                       waits=[wyl] + ([sq_guard] if sq_guard else []))
            wrd = B.ve(lambda q: q.tensor_reduce(
                sq[:, 16:24].rearrange("p (a o) -> p a o", o=1),
                self.sqscr[:, :].rearrange("p (a f) -> p a f", f=128),
                mybir.AxisListType.X, ALU.add), waits=[wsq])
            B.ve(lambda q: q.tensor_scalar(sq[:, 16:24], sq[:, 16:24], 1.0 / 128.0, 1e-6, ALU.mult, ALU.add))
            wve0 = ("ve", self.now("ve"))
            wl1 = B.ac(lambda q: q.activation(sq[:, 24:32], sq[:, 16:24], AF.Ln), waits=[wve0])
            wl2 = B.ac(lambda q: q.activation(sq[:, 24:32], sq[:, 24:32], AF.Exp, scale=-0.5), waits=[wl1])
            rstd8b = sq[:, 24:32].rearrange("p (a o) -> p a o", o=1).broadcast_to([128, 8, 128])
            B.ve(lambda q, yb=yb, y2f=y2f: q.tensor_tensor(
                y2f[:, :].rearrange("p (a f) -> p a f", f=128),
                yb[:, :].rearrange("p (a f) -> p a f", f=128), rstd8b, ALU.mult), waits=[wl2])
            yb_guard[par] = ("ve", self.now("ve"))
            wy2 = B.ve(lambda q, y2f=y2f, tmpb=tmpb: q.tensor_tensor(tmpb[:, :], y2f[:, :], whnb, ALU.mult),
                       waits=[tmpb_guard[par]] if tmpb_guard[par] else [])
            for dc in range(8):
                wts = []
                if dc == 0:
                    wts = [wy2] + ([pbh_guard] if pbh_guard else [])
                B.pe(lambda q, dc=dc, tmpb=tmpb: q.transpose(
                    self.pbh[:, dc * 128:(dc + 1) * 128], tmpb[:, dc * 128:(dc + 1) * 128],
                    self.identb[:, :]), waits=wts)
            wpet = ("pe", self.now("pe"))
            tmpb_guard[par] = wpet
            wy2T = B.ac(lambda q, y2Tb=y2Tb: q.activation(y2Tb[:, :], self.pbh[:, :], AF.Copy),
                        waits=[wpet] + (A.barrier["y2T"] if i == 0 else ([y2T_guard[par]] if y2T_guard[par] else [])))
            pbh_guard = wy2T
            for dc in range(8):
                for dh in range(2):
                    wts = []
                    if dc == 0 and dh == 0:
                        wts = [wy2T, wwo]
                        for bb in (0, 1):
                            if bank_g[bb]:
                                wts.append(bank_g[bb])
                    wo_rhs = A.bf("woutb", dc * 1024 + dh * 512, dc * 1024 + (dh + 1) * 512)
                    B.pe(lambda q, dc=dc, dh=dh, y2Tb=y2Tb, wo_rhs=wo_rhs: q.matmul(
                        self.pb[dh][:, 0:512], y2Tb[:, dc * 128:(dc + 1) * 128], wo_rhs,
                        start=(dc == 0), stop=(dc == 7)), waits=wts)
            wpe_wo = ("pe", self.now("pe"))
            y2T_guard[par] = wpe_wo
            for dc in range(8):
                wts = []
                if dc == 0:
                    wts = [wxT] + ([bank_g[2]] if bank_g[2] else [])
                B.pe(lambda q, dc=dc, xnTb=xnTb: q.matmul(
                    self.pb[2][0:64, 0:128], wu1b3[:, dc * 64:(dc + 1) * 64],
                    xnTb[:, dc * 128:(dc + 1) * 128],
                    start=(dc == 0), stop=False), waits=wts)
            B.pe(lambda q: q.matmul(self.pb[2][0:64, 0:128], bu1row, self.onesb[0:1, :],
                                    start=False, stop=True))
            wpe_u1 = ("pe", self.now("pe"))
            xnT_guard[par] = wpe_u1
            wau = B.ac(lambda q, au2b=au2b: q.activation(au2b, self.pb[2][0:64, 0:128], AF.Silu),
                       waits=[wpe_u1] + (A.barrier["g1a"] if i == 0 else ([g1a_guard[par]] if g1a_guard[par] else [])))
            bank_g[2] = wau
            for dh in range(2):
                wts = []
                if dh == 0:
                    wts = [wau]
                    for bb in (3, 4):
                        if bank_g[bb]:
                            wts.append(bank_g[bb])
                B.pe(lambda q, dh=dh, au2b=au2b: q.matmul(
                    self.pb[3 + dh][:, 0:512], au2b, self.wu2b[:, dh * 512:(dh + 1) * 512],
                    start=True, stop=False), waits=wts)
                B.pe(lambda q, dh=dh: q.matmul(
                    self.pb[3 + dh][:, 0:512], self.onesb[0:1, :],
                    self.biasb[0:1, 648 + dh * 512:648 + (dh + 1) * 512],
                    start=False, stop=True))
            wpe_u2 = ("pe", self.now("pe"))
            wsg = None
            for dh in range(2):
                wts = []
                if dh == 0:
                    wts = [wpe_u2] + (A.barrier["g1g"] if i == 0 else ([g1g_guard[par]] if g1g_guard[par] else []))
                wsg = B.ac(lambda q, dh=dh, gateb=gateb: q.activation(
                    gateb[:, dh * 512:(dh + 1) * 512], self.pb[3 + dh][:, 0:512], AF.Sigmoid), waits=wts)
            bank_g[3] = wsg
            bank_g[4] = wsg
            g1g_guard[par] = wsg
            g1a_guard[par] = wsg
            for dh in range(2):
                wts = []
                if dh == 0:
                    wts = [wpe_wo, wsg] + yaf_guard[par]
                B.ve(lambda q, dh=dh, yaf=yaf, gateb=gateb: q.tensor_tensor(
                    yaf[:, dh * 512:(dh + 1) * 512], self.pb[dh][:, 0:512],
                    gateb[:, dh * 512:(dh + 1) * 512], ALU.mult), waits=wts)
            bank_g[0] = bank_g[1] = ("ve", self.now("ve"))
            wya = B.ve(lambda q, yaf=yaf, xnf=xnf: q.tensor_tensor(yaf[:, :], yaf[:, :], xnf[:, :], ALU.add),
                       waits=[wxl])
            xn_guard[par] = wya
            B.stream("sYS")
            wsp = B.dma("gpsimd", self.s_h[i * 128:(i + 1) * 128, :], yaf[:, :], waits=[wya])
            wsq2 = B.ac(lambda q, yaf=yaf: q.activation(self.sqscr[:, :], yaf[:, :], AF.Square),
                        waits=[wya, wrd])
            wrd2 = B.ve(lambda q: q.tensor_reduce(
                sq[:, 32:33].rearrange("p (a o) -> p a o", a=1),
                self.sqscr[:, :].rearrange("p (a f) -> p a f", a=1),
                mybir.AxisListType.X, ALU.add), waits=[wsq2])
            sq_guard = wrd2
            B.ve(lambda q: q.tensor_scalar(sq[:, 32:33], sq[:, 32:33], 1.0 / 1024.0, 1e-6, ALU.mult, ALU.add))
            wve1 = ("ve", self.now("ve"))
            wl3 = B.ac(lambda q: q.activation(sq[:, 33:34], sq[:, 32:33], AF.Ln), waits=[wve1])
            wl4 = B.ac(lambda q: q.activation(sq[:, 33:34], sq[:, 33:34], AF.Exp, scale=-0.5), waits=[wl3])
            wts_h2s = B.ve(lambda q, yaf=yaf, y2f=y2f: q.tensor_scalar(
                y2f[:, :], yaf[:, :], sq[:, 33:34], None, ALU.mult), waits=[wl4])
            yaf_guard[par] = [wsp, wsq2]
            wh2 = B.ve(lambda q, y2f=y2f, h2b=h2b: q.tensor_tensor(h2b[:, :], y2f[:, :], self.wffnb[:, :], ALU.mult),
                       waits=(A.barrier["g1b"] if i == 0 else ([g1b_guard[par]] if g1b_guard[par] else [])))
            for dc in range(8):
                wts = []
                if dc == 0:
                    wts = [wh2, pbh_guard]
                B.pe(lambda q, dc=dc, h2b=h2b: q.transpose(
                    self.pbh[:, dc * 128:(dc + 1) * 128], h2b[:, dc * 128:(dc + 1) * 128],
                    self.identb[:, :]), waits=wts)
            wpet2 = ("pe", self.now("pe"))
            wh2T = None
            for dc in range(8):
                dsth = A.bf("h2T", dc * 2048 + i * 128, dc * 2048 + (i + 1) * 128)
                wts = []
                if dc == 0:
                    wts = [wpet2] + (A.barrier["h2T"] if i == 0 else [])
                wh2T = B.ac(lambda q, dc=dc, dsth=dsth: q.activation(
                    dsth, self.pbh[:, dc * 128:(dc + 1) * 128], AF.Copy), waits=wts)
            pbh_guard = wh2T
            g1b_guard[par] = wh2T
        g1_done = self.mark()
        A.free("woutb"); A.free("xnT"); A.free("y2T"); A.free("g1g"); A.free("g1a")
        A.alloc("wf1q", 4096); A.alloc("wf3q", 4096); A.alloc("wf2q", 4096)
        A.alloc("midb", 128)
        A.alloc("wgqs", 2048); A.alloc("wuqs", 2048)
        A.alloc("xqTp", 2048)
        A.alloc("hqo", 512)
        A.alloc("hqx", 512)
        g3_blocks = [(nb, i) for nb in range(6) for i in range(16)]
        g3_state = dict(idx=0, wgl=None, bank56=None, hq_guard=[None, None],
                        xq_guard=[None, None], wq_pe=None)
        g3_stores = []
        sqg = [sq_guard]
        hqx = [A.bf("hqx", j * 128, (j + 1) * 128) for j in range(8)]
        hqtp = dict(w=[None] * 8, n=0)

        def drip_hqT():
            bidx = g3_state["idx"] - 1
            nb, ti = g3_blocks[bidx]
            tok = g3_stores[-1]
            cur = getattr(self, "_cur_stream", None)
            B.stream("sTS")
            nfc = 4 if nb < 5 else 2
            for k in range(nfc):
                fc = nb * 4 + k
                j = hqtp["n"] % 8
                hqtp["n"] += 1
                wts = [tok] + ([hqtp["w"][j]] if hqtp["w"][j] else []) + \
                      (A.barrier["hqx"] if hqtp["n"] <= 8 else [])
                wt = B.dmaT("scalar", hqx[j],
                            self.hq_h[ti * 128:(ti + 1) * 128, fc * 128:(fc + 1) * 128], waits=wts)
                hqtp["w"][j] = B.dma("sync",
                                     self.hqT_h[ti * NQW + fc * 128:ti * NQW + (fc + 1) * 128, :],
                                     hqx[j], waits=[wt])
            self.stream(cur)

        def emit_g3_block():
            st = g3_state
            nb, i = g3_blocks[st["idx"]]
            st["idx"] += 1
            nbw = 512 if nb < 5 else 256
            par = i % 2
            if i == 0:
                B.stream("sFC")
                wgl = None
                for dc in range(8):
                    wts = []
                    if dc == 0:
                        wts = (A.barrier["wgqs"] + A.barrier["wuqs"]) if nb == 0 else [st["wq_pe"]]
                    B.dma("gpsimd", A.bf("wgqs", dc * 512, dc * 512 + nbw),
                          self.wgqT_h[dc * 128:(dc + 1) * 128, nb * 512:nb * 512 + nbw], waits=wts)
                    wgl = B.dma("gpsimd", A.bf("wuqs", dc * 512, dc * 512 + nbw),
                                self.wuqT_h[dc * 128:(dc + 1) * 128, nb * 512:nb * 512 + nbw])
                st["wgl"] = wgl
            B.stream("sSP")
            wxt = None
            for dc in range(8):
                wts = []
                if dc == 0:
                    wts = (A.barrier["xqTp"] if st["idx"] <= 2 else
                           ([st["xq_guard"][par]] if st["xq_guard"][par] else []))
                wxt = B.dma("sync", A.bf("xqTp", par * 1024 + dc * 128, par * 1024 + (dc + 1) * 128),
                            self.xqT_h[dc * 128:(dc + 1) * 128, i * 128:(i + 1) * 128], waits=wts)
            hqb = A.bf("hqo", par * 512, par * 512 + nbw)
            for dc in range(8):
                wts = []
                if dc == 0:
                    wts = [wxt, st["wgl"]]
                    if st["bank56"]:
                        wts.append(st["bank56"])
                    if st["idx"] == 1:
                        wts = wts + A.barrier["wf1q"]
                rx = A.bf("xqTp", par * 1024 + dc * 128, par * 1024 + (dc + 1) * 128)
                rg = A.bf("wgqs", dc * 512, dc * 512 + nbw)
                B.pe(lambda q, rx=rx, rg=rg, nbw=nbw, dc=dc: q.matmul(
                    self.pb[5][:, 0:nbw], rx, rg,
                    start=(dc == 0), stop=(dc == 7)), waits=wts)
            for dc in range(8):
                rx = A.bf("xqTp", par * 1024 + dc * 128, par * 1024 + (dc + 1) * 128)
                ru = A.bf("wuqs", dc * 512, dc * 512 + nbw)
                B.pe(lambda q, rx=rx, ru=ru, nbw=nbw, dc=dc: q.matmul(
                    self.pb[6][:, 0:nbw], rx, ru,
                    start=(dc == 0), stop=(dc == 7)))
            wpeg = ("pe", self.now("pe"))
            st["xq_guard"][par] = wpeg
            if i == 15:
                st["wq_pe"] = wpeg
            wts = [wpeg]
            if sqg[0]:
                wts.append(sqg[0])
                sqg[0] = None
            wsil = B.ac(lambda q, nbw=nbw: q.activation(
                self.sqscr[:, 512:512 + nbw], self.pb[5][:, 0:nbw], AF.Silu), waits=wts)
            hmid = self.scr[2]
            wmul = B.ve(lambda q, nbw=nbw, hmid=hmid: q.tensor_tensor(
                hmid[:, 0:nbw], self.sqscr[:, 512:512 + nbw], self.pb[6][:, 0:nbw], ALU.mult),
                waits=[wsil])
            st["bank56"] = wmul
            eow = (st["hq_guard"][par] if st["hq_guard"][par] else
                   (A.barrier["hqo"] if st["idx"] <= 2 else None))
            wqc = self.quant_chain(hmid[:, 0:nbw], hqb, width=nbw, blocks=nbw // 16, sq_lo=64,
                                   wait=wmul, extra_out_wait=eow,
                                   scr_r=self.scr[4], scr_t2=self.scr[5])
            B.stream("sSP")
            st["hq_guard"][par] = B.dma("gpsimd", self.hq_h[i * 128:(i + 1) * 128, nb * 512:nb * 512 + nbw],
                                        hqb, waits=[wqc])
            g3_stores.append(st["hq_guard"][par])

        wmid_guard = [None, None]
        g2mul_prev = None
        sprev = [None, None]
        bank23_g = {b: None for b in (2, 3)}
        b01_g = None
        g2w_guard = None
        nblk = 0
        for qf in range(4):
            B.stream("sFC")
            wq1 = None
            for dc in range(8):
                wts = []
                if dc == 0:
                    wts = (A.barrier["wf1q"] + A.barrier["wf3q"] + A.barrier["wf2q"]) if qf == 0 else [g2w_guard]
                B.dma("gpsimd", A.bf("wf1q", dc * 1024, (dc + 1) * 1024),
                      self.wf1b_h[dc * 128:(dc + 1) * 128, qf * 1024:(qf + 1) * 1024], waits=wts)
                B.dma("gpsimd", A.bf("wf3q", dc * 1024, (dc + 1) * 1024),
                      self.wf3b_h[dc * 128:(dc + 1) * 128, qf * 1024:(qf + 1) * 1024])
                wq1 = B.dma("gpsimd", A.bf("wf2q", dc * 1024, (dc + 1) * 1024),
                            self.wf2b_h[qf * 1024 + dc * 128:qf * 1024 + (dc + 1) * 128, 0:1024])
            for tc in range(16):
                par = tc % 2
                s0f = self.scr[par]
                B.stream("sDL")
                wsl = B.dma("sync", s0f[:, :], self.s_h[tc * 128:(tc + 1) * 128, :],
                            waits=([wsp] if qf == 0 else []) + ([sprev[par]] if sprev[par] else []))
                for Fc in range(8):
                    mpar = Fc % 2
                    midv = A.bf("midb", mpar * 128, (mpar + 1) * 128)
                    for dc in range(8):
                        wts = []
                        if dc == 0:
                            if tc == 0 and Fc == 0:
                                wts = [wq1]
                                if qf == 0:
                                    wts = wts + g1_done
                            if b01_g:
                                wts = wts + [b01_g]
                        r1 = A.bf("wf1q", dc * 1024 + Fc * 128, dc * 1024 + (Fc + 1) * 128)
                        rh = A.bf("h2T", dc * 2048 + tc * 128, dc * 2048 + (tc + 1) * 128)
                        B.pe(lambda q, r1=r1, rh=rh, dc=dc: q.matmul(
                            self.pb[0][:, 0:128], r1, rh,
                            start=(dc == 0), stop=(dc == 7)), waits=wts)
                    for dc in range(8):
                        r3 = A.bf("wf3q", dc * 1024 + Fc * 128, dc * 1024 + (Fc + 1) * 128)
                        rh = A.bf("h2T", dc * 2048 + tc * 128, dc * 2048 + (tc + 1) * 128)
                        B.pe(lambda q, r3=r3, rh=rh, dc=dc: q.matmul(
                            self.pb[1][:, 0:128], r3, rh,
                            start=(dc == 0), stop=(dc == 7)))
                    wpef = ("pe", self.now("pe"))
                    wts = [wpef]
                    if sqg[0]:
                        wts.append(sqg[0])
                        sqg[0] = None
                    if g2mul_prev:
                        wts.append(g2mul_prev)
                    wsil = B.ac(lambda q: q.activation(self.sqscr[:, 0:128], self.pb[0][:, 0:128], AF.Silu),
                                waits=wts)
                    wmul = B.ve(lambda q, midv=midv: q.tensor_tensor(
                        midv, self.sqscr[:, 0:128], self.pb[1][:, 0:128], ALU.mult),
                        waits=[wsil] + (A.barrier["midb"] if (qf == 0 and tc == 0 and Fc < 2)
                                        else ([wmid_guard[mpar]] if wmid_guard[mpar] else [])))
                    b01_g = wmul
                    g2mul_prev = wmul
                    for dh in range(2):
                        bk = 2 + dh
                        wts = [wmul]
                        if Fc == 0 and bank23_g[bk]:
                            wts.append(bank23_g[bk])
                        r2 = A.bf("wf2q", Fc * 1024 + dh * 512, Fc * 1024 + (dh + 1) * 512)
                        B.pe(lambda q, bk=bk, midv=midv, r2=r2, Fc=Fc: q.matmul(
                            self.pb[bk][:, 0:512], midv, r2,
                            start=(Fc == 0), stop=(Fc == 7)), waits=wts)
                    wmid_guard[mpar] = ("pe", self.now("pe"))
                wpe2 = ("pe", self.now("pe"))
                for dh in range(2):
                    bk = 2 + dh
                    wts = [wpe2, wsl] if dh == 0 else []
                    wadd = B.ve(lambda q, s0f=s0f, dh=dh, bk=bk: q.tensor_tensor(
                        s0f[:, dh * 512:(dh + 1) * 512], s0f[:, dh * 512:(dh + 1) * 512],
                        self.pb[bk][:, 0:512], ALU.add), waits=wts)
                    bank23_g[bk] = wadd
                wadd2 = ("ve", self.now("ve"))
                B.stream("sYS")
                sprev[par] = B.dma("gpsimd", self.s_h[tc * 128:(tc + 1) * 128, :], s0f[:, :], waits=[wadd2])
                nblk += 1
                import os as _os
                target = 0 if _os.environ.get("KDBG_NOMIX") else (nblk * 96) // 64
                while g3_state["idx"] < target:
                    emit_g3_block()
                    drip_hqT()
            g2w_guard = ("pe", self.now("pe"))
        while g3_state["idx"] < 96:
            emit_g3_block()
            drip_hqT()
        g3_scr_free = [("sYSG", self.now("sYSG")), ("ve", self.now("ve")),
                       ("sSPG", self.now("sSPG"))]
        A.free("wf1q"); A.free("wf3q"); A.free("wf2q"); A.free("midb"); A.free("g1b"); A.free("h2T")
        A.free("wgqs"); A.free("wuqs"); A.free("xqTp"); A.free("hqo"); A.free("hqx")
        A.alloc("wdqT", 11264)
        B.stream("sFC")
        for fc in range(22):
            B.dma("gpsimd", A.bf("wdqT", fc * 1024, (fc + 1) * 1024),
                  self.wdqT_h[fc * 128:(fc + 1) * 128, :],
                  waits=A.barrier["wdqT"] if fc == 0 else ())
        wdl = ("sFCG", self.now("sFCG"))
        A.alloc("hqTg", 2816)
        hqT_guard = [None, None]
        s_guard = [None, None]
        gf_guard = None
        of_guard = None
        b01g4 = None
        for i in range(OT):
            par = i % 2
            hqTb = A.bf("hqTg", par * 2816, (par + 1) * 2816)
            sf = self.scr[2 * par]
            xf = self.scr[2 * par + 1]
            gf = self.scr[4]
            of = self.scr[5]
            B.stream("sTS")
            wts = [("sTSH", 0)] + (list(A.barrier["hqTg"]) if i < 2 else
                                   ([hqT_guard[par]] if hqT_guard[par] else []))
            whqT = B.dma("sync", hqTb.rearrange("p (a w) -> p a w", w=128),
                         self.hqT_h[i * NQW:(i + 1) * NQW, :].rearrange("(a p) w -> p a w", p=128),
                         waits=wts)
            B.stream("sDL")
            wsl = B.dma("sync", sf[:, :], self.s_h[i * 128:(i + 1) * 128, :],
                        waits=(g3_scr_free if i < 2 else []) + ([s_guard[par]] if s_guard[par] else []))
            B.stream("sXL")
            wxl = B.dma("sync", xf[:, :], self.xg_d[HALO + i * 128:HALO + (i + 1) * 128, :],
                        waits=(g3_scr_free if i < 2 else []))
            for fc in range(22):
                for dh in range(2):
                    wts = []
                    if fc == 0 and dh == 0:
                        wts = [whqT, wdl] + ([b01g4] if b01g4 else [])
                    rw = A.bf("wdqT", fc * 1024 + dh * 512, fc * 1024 + (dh + 1) * 512)
                    B.pe(lambda q, fc=fc, dh=dh, hqTb=hqTb, rw=rw: q.matmul(
                        self.pb[dh][:, 0:512], hqTb[:, fc * 128:(fc + 1) * 128], rw,
                        start=(fc == 0), stop=(fc == 21)), waits=wts)
            wped = ("pe", self.now("pe"))
            hqT_guard[par] = wped
            wg0 = B.ac(lambda q, gf=gf: q.activation(gf[:, 0:512], self.pb[0][:, 0:512], AF.Silu),
                       waits=[wped] + ([gf_guard] if gf_guard else []))
            wg1 = B.ac(lambda q, gf=gf: q.activation(gf[:, 512:1024], self.pb[1][:, 0:512], AF.Silu))
            b01g4 = wg1
            wo1 = B.ve(lambda q, of=of, sf=sf, gf=gf: q.tensor_tensor(of[:, :], sf[:, :], gf[:, :], ALU.mult),
                       waits=[wg1, wsl] + ([of_guard] if of_guard else []))
            gf_guard = wo1
            wo2 = B.ve(lambda q, of=of, xf=xf: q.tensor_tensor(of[:, :], of[:, :], xf[:, :], ALU.add),
                       waits=[wxl])
            s_guard[par] = wo2
            of_guard = B.dma("gpsimd", self.out_d[i * 128:(i + 1) * 128, :], of[:, :],
                             sem="dO", waits=[wo2])
        self.g_done = self.mark()
        A.free("wdqT"); A.free("hqTg")

    def finish(self):
        B = self
        for nm in getattr(self, "_dump_names", []):
            if nm in self.A.regions:
                w2 = 2 * self.A.regions[nm][1]
                o = self.nc.dram_tensor("dbg_" + nm, [128, w2], BF16, kind="ExternalOutput").ap()
                B.dma("gpsimd", o, self.A.bf(nm), waits=self.mark())
        dumps = getattr(self, "dbg_dumps", [])
        for (name, dram_ap) in dumps:
            o = self.nc.dram_tensor("dbg_" + name, list(dram_ap.tensor.shape), dram_ap.dtype,
                                    kind="ExternalOutput").ap()
            B.dma("gpsimd", o, dram_ap, waits=self.mark())
        if getattr(self, "stop", "g") != "g":
            w = B.ve(lambda q: q.memset(self.scr[0][:, :], 0.0), waits=self.mark())
            for i in range(OTT):
                B.dma("gpsimd", self.out_d[i * 128:(i + 1) * 128, :], self.scr[0][:, :], waits=[w])
        final = self.mark()
        self.op("gpsimd", lambda q: q.wait_ge(self.sems["dO"], 0), waits=final)

    def emit(self):
        sems = self.sems
        import concourse.bass as bass_
        with self.nc.Block() as block:
            def mk(eng):
                def run(q):
                    for (e, fn, waits, inc, amt) in self.sched:
                        if e != eng:
                            continue
                        merged = {}
                        for (s, v) in waits:
                            merged[s] = max(merged.get(s, 0), v)
                        for (s, v) in merged.items():
                            q.wait_ge(sems[s], v)
                        ins = fn(q)
                        if inc is not None and ins is not None:
                            ins.then_inc(sems[inc], amt)
                return run
            block.sync(mk("sync"))
            block.tensor(mk("tensor"))
            block.vector(mk("vector"))
            block.scalar(mk("scalar"))
            block.gpsimd(mk("gpsimd"))


def host_constants():
    r = np.arange(128)[:, None] % 32
    c64 = np.arange(64)[None, :] % 32
    c256 = np.arange(256)[None, :] % 32
    cst = {}
    cst["c_ident"] = np.eye(128, dtype=np.float32)
    cst["c_identb"] = np.eye(128, dtype=np.float32).astype(ml_dtypes.bfloat16)
    negi = np.zeros((128, 32), np.float32)
    negi[np.arange(128), np.arange(128) % 32] = -1.0
    cst["c_negi"] = negi.astype(ml_dtypes.bfloat16)
    cst["c_ones"] = np.ones((1, 128), ml_dtypes.bfloat16)
    cst["c_onesf"] = np.ones((128, 128), np.float32)
    cst["c_mskN"] = np.where(c64 < r, -1.0, 0.0).astype(np.float32)
    cst["c_mskNT"] = np.where(c64 > r, -1.0, 0.0).astype(np.float32)
    cst["c_mskHq"] = np.where(r <= c256, 1.0, 0.0).astype(np.float32)
    cst["c_identp"] = np.where(c64 == r, 1.0, 0.0).astype(np.float32)
    mst = np.zeros(SCW, np.float32)
    mst[0] = 1.0
    for j in range(NCH2):
        mst[64 + 32 * j] = 1.0
    cst["c_mst"] = np.tile(mst[None, :], (128, 1)).astype(ml_dtypes.bfloat16)
    return cst


def make_inmaps(z):
    cst = host_constants()
    x = np.asarray(z["L0_attn_x"])
    wmap = dict(
        wq="L0_attn_wq", wk="L0_attn_wk", wv="L0_attn_wv", w_spike="L0_attn_w_spike",
        w_au="L0_attn_w_au", w_ad="L0_attn_w_ad", w_beta="L0_attn_w_beta",
        w_u1="L0_attn_w_u1", w_u2="L0_attn_w_u2", w_ff1="L0_attn_w_ff1",
        w_ff3="L0_attn_w_ff3", w_ff2="L0_attn_w_ff2", w_out="L0_attn_w_out",
        conv_w="L0_attn_conv_w",
    )
    rowmap = dict(
        b_au="L0_attn_b_au", b_ad="L0_attn_b_ad", b_beta="L0_attn_b_beta",
        b_u1="L0_attn_b_u1", b_u2="L0_attn_b_u2", w_ffn="L0_attn_w_ffn",
    )
    base = {}
    for k, v in wmap.items():
        base[k] = np.ascontiguousarray(np.asarray(z[v], np.float32))
    for k, v in rowmap.items():
        base[k] = np.ascontiguousarray(np.asarray(z[v], np.float32).reshape(1, -1))
    base["conv_b"] = np.asarray(z["L0_attn_conv_b"], np.float32).reshape(D, 1)
    base["w_norm_in"] = np.asarray(z["L0_attn_w_norm_in"], np.float32).reshape(D, 1)
    base["alpha_spike"] = np.asarray(z["L0_attn_alpha_spike"], np.float32).reshape(1, 512)
    base["beta_spike"] = np.asarray(z["L0_attn_beta_spike"], np.float32).reshape(1, 512)
    base["w_hn"] = np.asarray(z["L0_attn_w_hn"], np.float32).reshape(1, 1024)
    wg = np.asarray(z["L0_ffn_w_gate"], np.float32)
    wu = np.asarray(z["L0_ffn_w_up"], np.float32)
    wd = np.asarray(z["L0_ffn_w_down"], np.float32)
    base.update(cst)
    ims = []
    for c in range(8):
        b, t0 = c // 4, (c % 4) * 2048
        lo = t0 - HALO
        m = dict(base)
        if lo >= 0:
            m["xg"] = np.ascontiguousarray(x[b, lo:t0 + 2048], np.float32)
            sdm = np.ones((NT, 1), np.float32)
            xh = np.zeros((128, D), np.float32)
            if lo >= 3:
                xh[125:128] = x[b, lo - 3:lo]
            m["xh128"] = xh
        else:
            m["xg"] = np.concatenate([np.zeros((HALO, D), np.float32),
                                      np.ascontiguousarray(x[b, 0:2048], np.float32)], 0)
            sdm = np.concatenate([np.zeros((HALO, 1), np.float32),
                                  np.ones((2048, 1), np.float32)], 0)
            m["xh128"] = np.zeros((128, D), np.float32)
        m["sdmask"] = sdm
        m["wg_my"] = np.ascontiguousarray(wg[c * 352:(c + 1) * 352])
        m["wu_my"] = np.ascontiguousarray(wu[c * 352:(c + 1) * 352])
        m["wd_my"] = np.ascontiguousarray(wd[c * 128:(c + 1) * 128])
        ims.append(m)
    return ims



def kernel(**inputs):
    ims = make_inmaps(inputs)
    k = Kern()
    nc = k.build()
    k.emit()
    res = run_bass_kernel_spmd(nc, ims, core_ids=list(range(8)))
    out = np.zeros((2, 8192, 1024), np.float32)
    for c in range(8):
        b, t0 = c // 4, (c % 4) * 2048
        out[b, t0:t0 + 2048] = np.asarray(res.results[c]["out"], np.float32)
    k.es.close()
    return out


# version 10 (final)
# speedup vs baseline: 1.1400x; 1.1400x over previous
import numpy as np
import ml_dtypes
import concourse.bass as bass
import concourse.mybir as mybir
from concourse.bass_utils import run_bass_kernel_spmd
from contextlib import ExitStack

F32 = mybir.dt.float32
BF16 = mybir.dt.bfloat16
AF = mybir.ActivationFunctionType
ALU = mybir.AluOpType

H, DK, DV, D = 8, 64, 128, 1024
OWN, HALO, WMEM = 2048, 384, 192
NT = OWN + HALO
NTT = NT // 128
TSC = NT - WMEM
OTT = OWN // 128
CCS = 32
HLF = 1120
NCH2 = HLF // CCS
SCW = 1280
NSEG, LSEG, WSEG = 32, 70, 192
STEPS = WSEG + LSEG
MAGIC = 6291456.0
VELT = float(2 ** 22 + 1)
AW = 29696
NQW = 2816


class Arena:
    def __init__(self, t, kern=None):
        self.t = t
        self.kern = kern
        self.barrier = {}
        self.regions = {}
        self.free_list = [(0, AW)]

    def alloc(self, name, words, align=16):
        assert name not in self.regions
        if self.kern is not None:
            self.barrier[name] = self.kern.mark()
        for idx, (off, sz) in enumerate(self.free_list):
            a = (off + align - 1) // align * align
            if a + words <= off + sz:
                repl = [p for p in ((off, a - off), (a + words, off + sz - a - words)) if p[1] > 0]
                self.free_list[idx:idx + 1] = repl
                self.regions[name] = (a, words)
                return
        raise MemoryError(f"arena OOM: {name} ({words}w); live={self.regions}")

    def free(self, name):
        off, words = self.regions.pop(name)
        self.free_list.append((off, words))
        self.free_list.sort()
        m = []
        for o, s in self.free_list:
            if m and m[-1][0] + m[-1][1] == o:
                m[-1] = (m[-1][0], m[-1][1] + s)
            else:
                m.append((o, s))
        self.free_list = m

    def f32(self, name, lo=0, hi=None, parts=slice(0, 128)):
        off, words = self.regions[name]
        hi = words if hi is None else hi
        assert 0 <= lo <= hi <= words, (name, lo, hi, words)
        return self.t[parts, off + lo: off + hi]

    def bf(self, name, lo=0, hi=None, parts=slice(0, 128)):
        off, words = self.regions[name]
        hi = 2 * words if hi is None else hi
        assert 0 <= lo <= hi <= 2 * words, (name, lo, hi, words)
        return self.t[parts, off: off + words].bitcast(BF16)[:, lo:hi]


class Kern:

    def __init__(self):
        self.nc = bass.Bass("TRN2", target_bir_lowering=False, debug=False)
        self.sched = []
        self.counts = {}
        self.es = ExitStack()

    SELF_SYNC = {"vector": "ve", "scalar": "ac"}
    DMA_SEMS = ("dS", "dA", "dG", "sC0", "sC1", "dO", "sCNH", "sCNG", "sXLH", "sXLG", "sH0H", "sH0G", "sXQH", "sXQG", "sSTH", "sSTG", "sWPH", "sWPG", "sQKH", "sQKG", "sVH", "sVG", "sSDH", "sSDG", "sABH", "sABG", "sSPH", "sSPG", "sW4H", "sW4G", "sXTH", "sXTG", "sTSH", "sTSG", "sFCH", "sFCG", "sDLH", "sDLG", "sCTH", "sCTG", "sYSH", "sYSG",
                "wp0", "wp1", "wp2", "wp3", "wp4", "wp5", "wp6", "wp7")

    def op(self, eng, fn, waits=(), inc=None, amt=1):
        waits = list(waits)
        waits = [(s, self.counts.get(s, 0) if s in self.DMA_SEMS else v) for (s, v) in waits]
        ss = self.SELF_SYNC.get(eng)
        if ss is not None and inc == ss:
            prev = self.counts.get(ss, 0)
            if prev > 0:
                waits.append((ss, prev))
        if inc is not None:
            self.counts[inc] = self.counts.get(inc, 0) + amt
        self.sched.append((eng, fn, tuple(waits), inc, amt))

    def stream(self, name):
        self._cur_stream = name

    def dma(self, eng, out, in_, waits=(), sem=None, **kw):
        if sem is None:
            base = getattr(self, "_cur_stream", None)
            if base is not None:
                sem = base + ("G" if eng == "gpsimd" else "H")
            else:
                sem = {"sync": "dS", "scalar": "dA", "gpsimd": "dG"}[eng]
        self.op(eng, (lambda q, o=out, i=in_, kw=kw: q.dma_start(out=o, in_=i, **kw)),
                waits=waits, inc=sem, amt=16)
        return (sem, self.now(sem))

    def dmaT(self, eng, out, in_, waits=(), sem=None):
        if sem is None:
            base = getattr(self, "_cur_stream", None)
            sem = (base + "H") if base is not None else {"sync": "dS", "scalar": "dA"}[eng]
        self.op(eng, (lambda q, o=out, i=in_: q.dma_start_transpose(o, i)),
                waits=waits, inc=sem, amt=16)
        return (sem, self.now(sem))

    def ve(self, fn, waits=()):
        self.op("vector", fn, waits=waits, inc="ve")
        return ("ve", self.now("ve"))

    def ac(self, fn, waits=()):
        self.op("scalar", fn, waits=waits, inc="ac")
        return ("ac", self.now("ac"))

    def pe(self, fn, waits=()):
        self.op("tensor", fn, waits=waits, inc="pe")
        return ("pe", self.now("pe"))

    def now(self, sem):
        return self.counts.get(sem, 0)

    def mark(self):
        return [(s, v) for s, v in self.counts.items() if v > 0]

    def build_with_dumps(self, dump_names=()):
        self._dump_names = list(dump_names)
        return self.build()

    def build(self):
        nc = self.nc
        S = self.es.enter_context

        def din(name, shape, dt=F32):
            return nc.dram_tensor(name, shape, dt, kind="ExternalInput").ap()

        def hbm(name, shape, dt=F32):
            return nc.dram_tensor(name, shape, dt, kind="Internal").ap()

        self.xg_d = din("xg", [NT, D])
        self.xh_d = din("xh128", [128, D])
        self.sdm_d = din("sdmask", [NT, 1])
        self.wq_d = din("wq", [D, 512]); self.wk_d = din("wk", [D, 512])
        self.wv_d = din("wv", [D, 1024]); self.wsp_d = din("w_spike", [D, 512])
        self.wau_d = din("w_au", [D, 64]); self.wad_d = din("w_ad", [64, 512])
        self.wbe_d = din("w_beta", [D, 8])
        self.wu1_d = din("w_u1", [D, 64]); self.wu2_d = din("w_u2", [64, D])
        self.wf1_d = din("w_ff1", [D, 4096]); self.wf3_d = din("w_ff3", [D, 4096])
        self.wf2_d = din("w_ff2", [4096, D]); self.wout_d = din("w_out", [D, D])
        self.cw_d = din("conv_w", [D, 4]); self.cb_d = din("conv_b", [D, 1])
        self.wni_d = din("w_norm_in", [D, 1])
        self.wg_my_d = din("wg_my", [352, D]); self.wu_my_d = din("wu_my", [352, D])
        self.wd_my_d = din("wd_my", [128, NQW])
        self.bau_d = din("b_au", [1, 64]); self.bad_d = din("b_ad", [1, 512])
        self.aspk_d = din("alpha_spike", [1, 512]); self.bspk_d = din("beta_spike", [1, 512])
        self.bbe_d = din("b_beta", [1, 8]); self.whn_d = din("w_hn", [1, 1024])
        self.bu1_d = din("b_u1", [1, 64]); self.bu2_d = din("b_u2", [1, D])
        self.wffn_d = din("w_ffn", [1, D])
        self.id_d = din("c_ident", [128, 128])
        self.idb_d = din("c_identb", [128, 128], BF16)
        self.negi_d = din("c_negi", [128, 32], BF16)
        self.ones_d = din("c_ones", [1, 128], BF16)
        self.onesf_d = din("c_onesf", [128, 128])
        self.mskN_d = din("c_mskN", [128, 64]); self.mskNT_d = din("c_mskNT", [128, 64])
        self.mskHq_d = din("c_mskHq", [128, 256]); self.idp_d = din("c_identp", [128, 64])
        self.mst_d = din("c_mst", [128, SCW], BF16)

        self.out_d = nc.dram_tensor("out", [OWN, D], F32, kind="ExternalOutput").ap()

        self.sd_h = hbm("sd_h", [NT, 512])
        self.qr_h = hbm("qr_h", [NT, 512], BF16)
        self.kr_h = hbm("kr_h", [NT, 512], BF16)
        self.v_h = hbm("v_h", [NT, 1024], BF16)
        self.ab_h = hbm("ab_h", [NT, 512])
        self.rn_h = hbm("rn_h", [NT, 16])
        self.bb_h = hbm("bb_h", [NT, 8])
        self.spk_h = hbm("spk_h", [TSC, 512], BF16)
        self.xq_h = hbm("xq_h", [OWN, D], BF16)
        self.xnb_h = hbm("xnb_h", [OWN, D], BF16)
        self.xqT_h = hbm("xqT_h", [D, OWN], BF16)
        self.y_h = hbm("y_h", [TSC, 1024], BF16)
        self.hq_h = hbm("hq_h", [OWN, NQW], BF16)
        self.wgq_my = hbm("wgq_my", [352, D], BF16)
        self.wuq_my = hbm("wuq_my", [352, D], BF16)
        self.wdq_my = hbm("wdq_my", [128, NQW], BF16)
        self.wgq_all = hbm("wgq_all", [NQW, D], BF16)
        self.wuq_all = hbm("wuq_all", [NQW, D], BF16)
        self.wdq_all = hbm("wdq_all", [1024, NQW], BF16)
        self.wgqT_h = hbm("wgqT_h", [D, NQW], BF16)
        self.wuqT_h = hbm("wuqT_h", [D, NQW], BF16)
        self.wdqT_h = hbm("wdqT_h", [NQW, D], BF16)
        self.h0_h = hbm("h0_h", [2560, D], BF16)
        self.h0l_h = hbm("h0l_h", [2560, D], BF16)
        self.wf1b_h = hbm("wf1b_h", [D, 4096], BF16)
        self.wf3b_h = hbm("wf3b_h", [D, 4096], BF16)
        self.wf2b_h = hbm("wf2b_h", [4096, D], BF16)
        self.xn32_h = hbm("xn32_h", [OWN, D])
        self.s_h = hbm("s_h", [OWN, D])
        cand = dict(h0=self.h0_h, sd=self.sd_h, qr=self.qr_h, kr=self.kr_h, v=self.v_h, ab=self.ab_h,
                    rn=self.rn_h, bb=self.bb_h, spk=self.spk_h, xq=self.xq_h, xnb=self.xnb_h,
                    y=self.y_h, hq=self.hq_h, wgq=self.wgq_all, wdq=self.wdq_all, wgqmy=self.wgq_my,
                    s=self.s_h, xn32=self.xn32_h, xqT=self.xqT_h, wgqT=self.wgqT_h,
                    wdqT=self.wdqT_h)
        self.dbg_dumps = [(n, cand[n]) for n in getattr(self, "_dump_names", []) if n in cand]

        self.A = Arena(S(nc.sbuf_tensor("arena", [128, AW], F32)), kern=self)
        self.consts = S(nc.sbuf_tensor("consts", [128, 1024], F32))
        self.CN = dict(ID=0, MN=128, MNT=192, MHQ=256, IP=512, CW=576, CB=608, WNI=616)
        self.cmsk = S(nc.sbuf_tensor("cmsk", [128, SCW], BF16))
        self.identb = S(nc.sbuf_tensor("identb", [128, 128], BF16))
        self.onesb = S(nc.sbuf_tensor("onesb", [1, 128], BF16))
        self.onesf = S(nc.sbuf_tensor("onesf", [128, 128], F32))
        self.negib = S(nc.sbuf_tensor("negib", [128, 32], BF16))
        self.biasb = S(nc.sbuf_tensor("biasb", [128, 2048], BF16))
        self.biasf = S(nc.sbuf_tensor("biasf", [128, 2048], F32))
        self.bcast = S(nc.sbuf_tensor("bcast", [128, 2048], BF16))
        self.wffnb = S(nc.sbuf_tensor("wffnb", [128, 1024], BF16))
        self.sdm_s = S(nc.sbuf_tensor("sdm_s", [128, NTT], F32))
        self.nrm2 = S(nc.sbuf_tensor("nrm2", [128, NTT * 16], F32))
        self.aum = S(nc.sbuf_tensor("aum", [128, NTT * 64], BF16))
        self.aumT = S(nc.sbuf_tensor("aumT", [128, NT], BF16))
        self.wadb = S(nc.sbuf_tensor("wadb", [64, 512], BF16))
        self.wu1b = S(nc.sbuf_tensor("wu1b", [128, 8 * 64], BF16))
        self.wu2b = S(nc.sbuf_tensor("wu2b", [64, 1024], BF16))
        self.scr = [S(nc.sbuf_tensor(f"scr{i}", [128, 1024], F32)) for i in range(6)]
        self.sqscr = S(nc.sbuf_tensor("sqscr", [128, 1024], F32))
        self.bbsb = S(nc.sbuf_tensor("bbsb", [128, NTT * 8], F32))
        self.rnsb = S(nc.sbuf_tensor("rnsb", [128, NTT * 16], F32))
        self.scrb = [S(nc.sbuf_tensor(f"scrb{i}", [128, 1024], BF16)) for i in range(4)]
        self.memM = S(nc.sbuf_tensor("memM", [128, 128], F32))
        self.dscr = S(nc.sbuf_tensor("dscr", [128, 256], F32))
        self.smallq = S(nc.sbuf_tensor("smallq", [128, 512], F32))
        self.zsb = S(nc.sbuf_tensor("zsb", [128, 512], F32))
        self.zsbb = S(nc.sbuf_tensor("zsbb", [128, 512], BF16))
        self.pend = S(nc.sbuf_tensor("pendt", [128, 4 * NCH2], F32))
        self.sol = S(nc.sbuf_tensor("sol", [128, 13 * 64], BF16))
        self.SO = {k: i * 64 for i, k in enumerate(
            ["N", "NT", "N2", "N2T", "N4", "N4T", "N8", "N8T", "N16", "N16T", "P2", "P1T", "Y1"])}
        self.pb = [S(nc.psum_tensor(f"pb{i}", [128, 512], F32)) for i in range(7)]
        self.pbh = S(nc.psum_tensor("pbh", [128, 1024], BF16))
        self.sems = {k: S(nc.semaphore(k)) for k in ("dS", "dA", "dG", "pe", "ve", "ac", "cc", "dO", "sC0", "sC1", "sCNH", "sCNG", "sXLH", "sXLG", "sH0H", "sH0G", "sXQH", "sXQG", "sSTH", "sSTG", "sWPH", "sWPG", "sQKH", "sQKG", "sVH", "sVG", "sSDH", "sSDG", "sABH", "sABG", "sSPH", "sSPG", "sW4H", "sW4G", "sXTH", "sXTG", "sTSH", "sTSG", "sFCH", "sFCG", "sDLH", "sDLG", "sCTH", "sCTG", "sYSH", "sYSG", "wp0", "wp1", "wp2", "wp3", "wp4", "wp5", "wp6", "wp7")}

        phases = ["w0", "a", "b", "w4", "c", "def_", "g"]
        upto = getattr(self, "stop", "g")
        skip = getattr(self, "skip", set())
        for p in phases:
            if p not in skip:
                getattr(self, "ph_" + p)()
            if p == upto:
                break
        self.finish()
        return nc

    def ph_w0(self):
        B, A, CN = self, self.A, self.CN
        c = self.consts
        B.stream("sCN")
        B.dma("sync", c[:, CN["ID"]:CN["ID"] + 128], self.id_d)
        B.dma("sync", c[:, CN["MN"]:CN["MN"] + 64], self.mskN_d)
        B.dma("sync", c[:, CN["MNT"]:CN["MNT"] + 64], self.mskNT_d)
        B.dma("sync", c[:, CN["MHQ"]:CN["MHQ"] + 256], self.mskHq_d)
        B.dma("sync", c[:, CN["IP"]:CN["IP"] + 64], self.idp_d)
        for d in range(8):
            rows = slice(d * 128, (d + 1) * 128)
            B.dma("sync", c[:, CN["CW"] + 4 * d:CN["CW"] + 4 * d + 4], self.cw_d[rows, :])
            B.dma("sync", c[:, CN["CB"] + d:CN["CB"] + d + 1], self.cb_d[rows, :])
            B.dma("sync", c[:, CN["WNI"] + d:CN["WNI"] + d + 1], self.wni_d[rows, :])
        B.dma("sync", self.cmsk[:, 0:SCW], self.mst_d)
        B.dma("gpsimd", self.identb[:, :], self.idb_d)
        B.dma("gpsimd", self.onesb[0:1, :], self.ones_d)
        B.dma("sync", self.onesf[:, :], self.onesf_d)
        B.dma("gpsimd", self.negib[:, :], self.negi_d)
        B.dma("sync", self.biasf[0:1, 0:64], self.bau_d)
        B.dma("sync", self.biasf[0:1, 64:576], self.bad_d)
        B.dma("sync", self.biasf[0:1, 576:584], self.bbe_d)
        B.dma("sync", self.biasf[0:1, 584:648], self.bu1_d)
        B.dma("sync", self.biasf[0:1, 648:1672], self.bu2_d)
        B.dma("sync", self.biasf[32:33, 0:512], self.aspk_d)
        B.dma("sync", self.biasf[32:33, 512:1024], self.bspk_d)
        B.dma("sync", self.biasf[32:33, 1024:2048], self.whn_d)
        B.dma("sync", self.biasf[64:65, 0:1024], self.wffn_d)
        B.dma("sync", self.sdm_s[:, :].rearrange("p (a o) -> p a o", o=1), self.sdm_d.rearrange("(a p) o -> p a o", p=128), allow_slow_non_contiguous=True)
        self.WPO = dict(q=0, k=512, v=1024, sd=2048, au=2560, be=2624)
        B.dma("gpsimd", self.wadb[:, :], self.wad_d)
        B.dma("gpsimd", self.wu1b[:, :].rearrange("p (a f) -> p a f", f=64), self.wu1_d.rearrange("(a p) f -> p a f", p=128))
        B.dma("gpsimd", self.wu2b[:, :], self.wu2_d)
        w0 = self.mark()

        for d in range(8):
            B.ve(lambda q, d=d: q.tensor_tensor(
                self.consts[:, CN["CW"] + 4 * d:CN["CW"] + 4 * d + 4],
                self.consts[:, CN["CW"] + 4 * d:CN["CW"] + 4 * d + 4],
                self.consts[:, CN["WNI"] + d:CN["WNI"] + d + 1].broadcast_to([128, 4]), ALU.mult),
                waits=w0 if d == 0 else ())
        B.ve(lambda q: q.tensor_copy(self.biasb[0:1, 0:1672], self.biasf[0:1, 0:1672]))
        B.ve(lambda q: q.tensor_copy(self.biasb[0:1, 1672:1736], self.biasf[0:1, 0:64]))
        B.ve(lambda q: q.tensor_copy(self.biasb[0:1, 1736:1744], self.biasf[0:1, 576:584]))
        wv = ("ve", self.now("ve"))

        nonlocal_guard = []
        def bc_guarded(dst, dst_lo, src, width):
            srow, src_lo = src
            for o in range(0, width, 512):
                w = min(512, width - o)
                waits = [wv] + nonlocal_guard[-1:]
                pw = B.pe(lambda q, o=o, w=w, srow=srow, src_lo=src_lo: q.matmul(
                    self.pb[6][:, 0:w], self.onesf[srow:srow + 1, :],
                    self.biasf[srow:srow + 1, src_lo + o:src_lo + o + w],
                    start=True, stop=True), waits=waits)
                aw = B.ac(lambda q, dst=dst, dst_lo=dst_lo, o=o, w=w: q.activation(
                    dst[:, dst_lo + o:dst_lo + o + w], self.pb[6][:, 0:w], AF.Copy),
                    waits=[pw])
                nonlocal_guard.append(aw)

        bc_guarded(self.bcast, 0, (32, 0), 512)
        bc_guarded(self.bcast, 512, (32, 512), 512)
        bc_guarded(self.bcast, 1024, (32, 1024), 1024)
        bc_guarded(self.wffnb, 0, (64, 0), 1024)
        self.w0_done = self.mark()

    def ph_a(self):
        self.stream(None)
        """LN + rms-fold + xn/xq own spills + h0 transpose + conv -> hbf."""
        B, A = self, self.A
        A.alloc("h0s", 2560)
        A.alloc("h0l", 2560)
        A.alloc("h0f", 2560)
        A.alloc("h0lo", 1024)
        A.alloc("hbf", 9728)
        sq = self.smallq
        last_dmaT = [None, None]
        last_dmaL = [None, None]
        last_xt_use = [None, None]
        last_xnbq = [None, None]
        last_s2red = None
        for i in range(20):
            par = i % 2
            xt = self.scr[par]
            xnf = self.scr[2 + par]
            sqr = self.sqscr
            h0b = self.scrb[par]
            xnb = self.scrb[2 + par]
            xqb = self.scrb[2 + par]
            src = self.xh_d if i == 0 else self.xg_d[(i - 1) * 128:i * 128, :]
            o = par * 8
            M, S2, VAR, RSTD, T1, HSC = o, o + 1, o + 2, o + 3, o + 4, o + 5
            xw = []
            if i == 0:
                xw = self.w0_done
            elif last_xt_use[par] is not None:
                xw = [last_xt_use[par]]
            B.stream("sXL")
            wx = B.dma("sync", xt[:, :], src, waits=xw)
            wr = B.ve(lambda q, xt=xt, M=M: q.tensor_reduce(
                sq[:, M:M + 1].rearrange("p (a o) -> p a o", a=1), xt[:, :].rearrange("p (a f) -> p a f", a=1),
                mybir.AxisListType.X, ALU.add), waits=[wx])
            B.ve(lambda q, M=M: q.tensor_scalar(sq[:, M:M + 1], sq[:, M:M + 1], 1.0 / D, None, ALU.mult))
            wsq = B.ac(lambda q, xt=xt, sqr=sqr: q.activation(sqr[:, :], xt[:, :], AF.Square),
                       waits=[wx] + ([last_s2red] if last_s2red else []))
            last_s2red = B.ve(lambda q, sqr=sqr, S2=S2: q.tensor_reduce(
                sq[:, S2:S2 + 1].rearrange("p (a o) -> p a o", a=1), sqr[:, :].rearrange("p (a f) -> p a f", a=1),
                mybir.AxisListType.X, ALU.add), waits=[wsq])
            B.ve(lambda q, S2=S2: q.tensor_scalar(sq[:, S2:S2 + 1], sq[:, S2:S2 + 1], 1.0 / D, None, ALU.mult))
            B.ve(lambda q, M=M, T1=T1: q.tensor_tensor(sq[:, T1:T1 + 1], sq[:, M:M + 1], sq[:, M:M + 1], ALU.mult))
            wv1 = B.ve(lambda q, S2=S2, T1=T1, VAR=VAR: q.tensor_tensor(
                sq[:, VAR:VAR + 1], sq[:, S2:S2 + 1], sq[:, T1:T1 + 1], ALU.subtract))
            wv1b = B.ve(lambda q, VAR=VAR, T1=T1: q.tensor_scalar(
                sq[:, T1:T1 + 1], sq[:, VAR:VAR + 1], 1e-5, None, ALU.add), waits=[wv1])
            wl = B.ac(lambda q, T1=T1: q.activation(sq[:, T1:T1 + 1], sq[:, T1:T1 + 1], AF.Ln),
                      waits=[wv1b])
            wr2 = B.ac(lambda q, T1=T1, RSTD=RSTD: q.activation(sq[:, RSTD:RSTD + 1], sq[:, T1:T1 + 1], AF.Exp, scale=-0.5),
                       waits=[wl])
            B.ve(lambda q, RSTD=RSTD, T1=T1: q.tensor_tensor(sq[:, T1:T1 + 1], sq[:, RSTD:RSTD + 1], sq[:, RSTD:RSTD + 1], ALU.mult), waits=[wr2])
            B.ve(lambda q, T1=T1, VAR=VAR: q.tensor_tensor(sq[:, T1:T1 + 1], sq[:, T1:T1 + 1], sq[:, VAR:VAR + 1], ALU.mult))
            wv2 = B.ve(lambda q, T1=T1: q.tensor_scalar(sq[:, T1:T1 + 1], sq[:, T1:T1 + 1], 1e-6, None, ALU.add))
            wl2 = B.ac(lambda q, T1=T1: q.activation(sq[:, T1:T1 + 1], sq[:, T1:T1 + 1], AF.Ln), waits=[wv2])
            wl3 = B.ac(lambda q, T1=T1: q.activation(sq[:, T1:T1 + 1], sq[:, T1:T1 + 1], AF.Exp, scale=-0.5), waits=[wl2])
            whs = B.ve(lambda q, RSTD=RSTD, T1=T1, HSC=HSC: q.tensor_tensor(
                sq[:, HSC:HSC + 1], sq[:, RSTD:RSTD + 1], sq[:, T1:T1 + 1], ALU.mult), waits=[wl3])
            wxn = B.ve(lambda q, xt=xt, xnf=xnf, M=M, RSTD=RSTD: q.scalar_tensor_tensor(
                xnf[:, :], xt[:, :], sq[:, M:M + 1], sq[:, RSTD:RSTD + 1].broadcast_to([128, D]),
                ALU.subtract, ALU.mult), waits=[wr2])
            if i >= 4:
                ownrow = (i - 4) * 128
                guard = [last_xnbq[par]] if last_xnbq[par] else []
                wcb = B.ve(lambda q, xnf=xnf, xnb=xnb: q.tensor_copy(xnb[:, :], xnf[:, :]),
                           waits=[wxn] + guard)
                B.stream("sXQ")
                wnb = B.dma("gpsimd", self.xnb_h[ownrow:ownrow + 128, :], xnb[:, :], waits=[wcb])
                wsp32 = B.dma("gpsimd", self.xn32_h[ownrow:ownrow + 128, :], xnf[:, :], waits=[wxn])
                wq = self.quant_chain(xnf, xqb, width=D, blocks=64, sq_lo=16 + par * 192,
                                      wait=wsp32, extra_out_wait=wnb)
                last_xnbq[par] = B.dma("gpsimd", self.xq_h[ownrow:ownrow + 128, :], xqb[:, :], waits=[wq])
            lob = A.bf("h0lo", par * 1024, (par + 1) * 1024)
            wh0f = B.ve(lambda q, xt=xt, sqr=sqr, M=M, HSC=HSC: q.scalar_tensor_tensor(
                sqr[:, :], xt[:, :], sq[:, M:M + 1], sq[:, HSC:HSC + 1].broadcast_to([128, D]),
                ALU.subtract, ALU.mult), waits=[whs])
            last_xt_use[par] = wh0f
            guard2 = [last_dmaT[par]] if last_dmaT[par] else []
            wh0 = B.ac(lambda q, sqr=sqr, h0b=h0b: q.activation(h0b[:, :], sqr[:, :], AF.Copy),
                       waits=[wh0f] + guard2)
            guard3 = [last_dmaL[par]] if last_dmaL[par] else []
            wlo = B.ve(lambda q, sqr=sqr, h0b=h0b, lob=lob: q.tensor_tensor(
                lob[:, :], sqr[:, :], h0b[:, :], ALU.subtract), waits=[wh0] + guard3)
            last_s2red = wlo
            B.stream("sH0")
            last_dmaT[par] = B.dma("sync", self.h0_h[i * 128:(i + 1) * 128, :], h0b[:, :], waits=[wh0])
            last_dmaL[par] = B.dma("sync", self.h0l_h[i * 128:(i + 1) * 128, :], lob[:, :], waits=[wlo])
        wallH = ("sH0H", self.now("sH0H"))
        A.alloc("cva", 2432)
        conv_done = []
        strip_rdy = [None, None]
        for dc in range(8):
            sp = dc % 2
            slo = sp * 1280
            tw = None
            for i in range(20):
                wts = [wallH] if i == 0 else []
                if i == 0 and strip_rdy[sp] is not None:
                    wts = [wallH, strip_rdy[sp]]
                B.stream("sST")
                tw = B.dmaT("scalar", A.bf("h0s", slo * 2 + i * 128, slo * 2 + (i + 1) * 128),
                            self.h0_h[i * 128:(i + 1) * 128, dc * 128:(dc + 1) * 128], waits=wts)
            twl = None
            for i in range(20):
                B.stream("sST")
                twl = B.dmaT("scalar", A.bf("h0l", slo * 2 + i * 128, slo * 2 + (i + 1) * 128),
                             self.h0l_h[i * 128:(i + 1) * 128, dc * 128:(dc + 1) * 128], waits=[])
            hiv = A.bf("h0s", slo * 2, slo * 2 + 2560)
            lov = A.bf("h0l", slo * 2, slo * 2 + 2560)
            h0fv = A.f32("h0f", 0, 2560)
            wcomb = B.ve(lambda q, h0fv=h0fv, hiv=hiv, lov=lov: q.tensor_tensor(
                h0fv, hiv, lov, ALU.add), waits=[tw, twl] + (A.barrier["h0f"] if dc == 0 else []))
            strip_rdy[sp] = wcomb
            CW0 = self.CN["CW"] + 4 * dc
            waits = [] if dc < 1 else [conv_done[dc - 1]]
            acc = A.f32("cva")
            h0v = [A.f32("h0f", 125 + j, 125 + j + NT) for j in range(4)]
            hbv = A.bf("hbf", dc * 2432, (dc + 1) * 2432)
            B.ve(lambda q, acc=acc, h0v=h0v, CW0=CW0: q.tensor_scalar(
                acc, h0v[0], self.consts[:, CW0:CW0 + 1], None, ALU.mult), waits=waits)
            for j in (1, 2, 3):
                B.ve(lambda q, acc=acc, h0v=h0v, j=j, CW0=CW0: q.scalar_tensor_tensor(
                    acc, h0v[j], self.consts[:, CW0 + j:CW0 + j + 1], acc, ALU.mult, ALU.add))
            wv3 = ("ve", self.now("ve"))
            wcv = B.ac(lambda q, acc=acc, hbv=hbv, dc=dc: q.activation(
                hbv, acc, AF.Silu,
                bias=self.consts[:, self.CN["CB"] + dc:self.CN["CB"] + dc + 1]), waits=[wv3])
            conv_done.append(wcv)
        self.a_done = self.mark()
        A.free("h0s"); A.free("h0l"); A.free("h0f"); A.free("h0lo"); A.free("cva")

    def quant_chain(self, xin, xout, width, blocks, sq_lo, wait, extra_out_wait=None,
                    scr_r=None, scr_t2=None, sq=None):
        B = self
        sq = self.smallq if sq is None else sq
        BM, SC, RS = sq_lo, sq_lo + blocks, sq_lo + 2 * blocks
        r = self.scr[4] if scr_r is None else scr_r
        t2 = self.scr[5] if scr_t2 is None else scr_t2
        Pq = xin.shape[0]
        sqv = (self.smallq if sq is None else sq)[0:Pq, :]
        wb = B.ve(lambda q: q.tensor_reduce(
            sqv[:, BM:BM + blocks].rearrange("p (b o) -> p b o", o=1),
            xin[:, :].rearrange("p (b k) -> p b k", k=16),
            mybir.AxisListType.X, ALU.max, apply_absolute_value=True), waits=[wait])
        B.ve(lambda q: q.tensor_scalar(sqv[:, SC:SC + blocks], sqv[:, BM:BM + blocks],
                                       1.0 / 6.0, 1e-12, ALU.mult, ALU.max))
        wsc = B.ve(lambda q: q.reciprocal(sqv[:, RS:RS + blocks], sqv[:, SC:SC + blocks]))
        r = r[0:Pq, 0:width]; t2 = t2[0:Pq, 0:width]
        wr = B.ve(lambda q: q.tensor_tensor(
            r[:, 0:width], xin[:, :],
            sqv[:, RS:RS + blocks].rearrange("p (b o) -> p b o", o=1).broadcast_to([Pq, blocks, 16]),
            ALU.mult), waits=[wsc])
        B.ve(lambda q: q.tensor_scalar(t2[:, 0:width], r[:, 0:width], MAGIC, -MAGIC, ALU.add, ALU.add))
        B.ve(lambda q: q.tensor_scalar(t2[:, 0:width], t2[:, 0:width], 2.0, -2.0, ALU.min, ALU.max))
        B.ve(lambda q: q.tensor_scalar(xin[:, :], r[:, 0:width], VELT, None, ALU.mult))
        B.ve(lambda q: q.tensor_tensor(r[:, 0:width], xin[:, :], r[:, 0:width], ALU.subtract))
        B.ve(lambda q: q.tensor_tensor(r[:, 0:width], xin[:, :], r[:, 0:width], ALU.subtract))
        B.ve(lambda q: q.tensor_scalar(xin[:, :], r[:, 0:width], 2.0, -2.0, ALU.min, ALU.max))
        B.ve(lambda q: q.scalar_tensor_tensor(r[:, 0:width], xin[:, :], -1.0, r[:, 0:width], ALU.mult, ALU.add))
        B.ve(lambda q: q.tensor_tensor(r[:, 0:width], t2[:, 0:width], r[:, 0:width], ALU.add))
        eow = (extra_out_wait if isinstance(extra_out_wait, list)
               else ([extra_out_wait] if extra_out_wait else []))
        wq = B.ve(lambda q: q.tensor_tensor(
            xout[:, 0:width], r[:, 0:width],
            sqv[:, SC:SC + blocks].rearrange("p (b o) -> p b o", o=1).broadcast_to([Pq, blocks, 16]),
            ALU.mult), waits=eow)
        return wq

    def ph_b(self):
        self.stream(None)
        """Projections q/k/v/sd/aumid/bb + norms stats + ab; spills to HBM."""
        B, A = self, self.A
        WP = self.WPO
        A.alloc("wproj", 10528)
        B.stream("sWP")
        for d in range(8):
            rows = slice(d * 128, (d + 1) * 128)
            base = d * 2632
            B.dma("gpsimd", A.bf("wproj", base + 0, base + 512), self.wq_d[rows, :],
                  waits=A.barrier["wproj"] if d == 0 else ())
            B.dma("gpsimd", A.bf("wproj", base + 512, base + 1024), self.wk_d[rows, :])
            B.dma("gpsimd", A.bf("wproj", base + 1024, base + 2048), self.wv_d[rows, :])
            B.dma("gpsimd", A.bf("wproj", base + 2048, base + 2560), self.wsp_d[rows, :])
            B.dma("gpsimd", A.bf("wproj", base + 2560, base + 2624), self.wau_d[rows, :])
            B.dma("gpsimd", A.bf("wproj", base + 2624, base + 2632), self.wbe_d[rows, :])
        wld = ("sWPG", self.now("sWPG"))
        bank_guard = {b: None for b in range(7)}
        sq_guard = None
        scrb_guard = {}
        pbh_guard = None
        for i in range(19):
            par = i % 2
            hsl = lambda dc, i=i: A.bf("hbf", dc * 2432 + i * 128, dc * 2432 + (i + 1) * 128)
            mm_waits = (self.a_done + [wld]) if i == 0 else []
            blocks = [(0, WP["q"], 512), (1, WP["k"], 512), (2, WP["v"], 512),
                      (3, WP["v"] + 512, 512), (4, WP["sd"], 512), (5, WP["au"], 72)]
            for dc in range(8):
                lhs = A.bf("hbf", dc * 2432 + i * 128, dc * 2432 + (i + 1) * 128)
                for (bank, wo, wn) in blocks:
                    waits = list(mm_waits)
                    if dc == 0 and bank_guard[bank] is not None:
                        waits.append(bank_guard[bank])
                    rhs = A.bf("wproj", dc * 2632 + wo, dc * 2632 + wo + wn)
                    B.pe(lambda q, lhs=lhs, rhs=rhs, bank=bank, wn=wn, dc=dc: q.matmul(
                        self.pb[bank][:, 0:wn], lhs, rhs,
                        start=(dc == 0), stop=(dc == 7 and bank != 5)), waits=waits)
            wbias = B.pe(lambda q: q.matmul(self.pb[5][:, 0:72], self.onesb[0:1, :],
                                            self.biasb[0:1, 1672:1744], start=False, stop=True))
            wpe = ("pe", self.now("pe"))
            wsq1 = B.ac(lambda q: q.activation(self.sqscr[:, 0:512], self.pb[0][:, 0:512], AF.Square),
                        waits=[wpe] + ([sq_guard] if sq_guard else []))
            wr1 = B.ve(lambda q, i=i: q.tensor_reduce(
                self.nrm2[:, i * 16:i * 16 + 8].rearrange("p (b o) -> p b o", o=1),
                self.sqscr[:, 0:512].rearrange("p (b k) -> p b k", k=64),
                mybir.AxisListType.X, ALU.add), waits=[wsq1])
            qb = self.scrb[par]
            g = scrb_guard.get(("qk", par))
            wc1 = B.ac(lambda q, qb=qb: q.activation(qb[:, 0:512], self.pb[0][:, 0:512], AF.Copy),
                       waits=[wpe] + ([g] if g else []))
            B.stream("sQK")
            scrb_guard[("q", par)] = B.dma("gpsimd", self.qr_h[i * 128:(i + 1) * 128, :], qb[:, 0:512], waits=[wc1])
            bank_guard[0] = wc1
            wsq2 = B.ac(lambda q: q.activation(self.sqscr[:, 512:1024], self.pb[1][:, 0:512], AF.Square),
                        waits=[wpe, wr1])
            sq_guard = B.ve(lambda q, i=i: q.tensor_reduce(
                self.nrm2[:, i * 16 + 8:i * 16 + 16].rearrange("p (b o) -> p b o", o=1),
                self.sqscr[:, 512:1024].rearrange("p (b k) -> p b k", k=64),
                mybir.AxisListType.X, ALU.add), waits=[wsq2])
            kb = self.scrb[par]
            g = scrb_guard.get(("qk", par))
            wc2 = B.ac(lambda q, kb=kb: q.activation(kb[:, 512:1024], self.pb[1][:, 0:512], AF.Copy),
                       waits=[wpe] + ([g] if g else []))
            B.stream("sQK")
            kdma = B.dma("gpsimd", self.kr_h[i * 128:(i + 1) * 128, :], kb[:, 512:1024], waits=[wc2])
            scrb_guard[("qk", par)] = kdma
            bank_guard[1] = wc2
            vb = self.scrb[2 + par]
            g = scrb_guard.get(("v", par))
            wc3 = B.ac(lambda q, vb=vb: q.activation(vb[:, 0:512], self.pb[2][:, 0:512], AF.Copy),
                       waits=[wpe] + ([g] if g else []))
            wc4 = B.ac(lambda q, vb=vb: q.activation(vb[:, 512:1024], self.pb[3][:, 0:512], AF.Copy))
            B.stream("sV")
            scrb_guard[("v", par)] = B.dma("gpsimd", self.v_h[i * 128:(i + 1) * 128, :], vb[:, 0:1024], waits=[wc4])
            bank_guard[2] = wc3
            bank_guard[3] = wc4
            sdf = self.scr[par]
            g = scrb_guard.get(("sd", par))
            wc5 = B.ve(lambda q, sdf=sdf, i=i: q.tensor_scalar(
                sdf[:, 0:512], self.pb[4][:, 0:512], self.sdm_s[:, i:i + 1], None, ALU.mult),
                waits=[wpe] + ([g] if g else []))
            B.stream("sSD")
            scrb_guard[("sd", par)] = B.dma("sync", self.sd_h[i * 128:(i + 1) * 128, :], sdf[:, 0:512], waits=[wc5])
            bank_guard[4] = wc5
            wc6 = B.ac(lambda q, i=i: q.activation(self.aum[:, i * 64:(i + 1) * 64], self.pb[5][:, 0:64], AF.Silu),
                       waits=[wpe])
            wc7 = B.ve(lambda q, i=i: q.tensor_copy(self.bbsb[:, i * 8:(i + 1) * 8], self.pb[5][:, 64:72]),
                       waits=[wpe])
            bank_guard[5] = wc7
            wtr = B.pe(lambda q, i=i: q.transpose(self.pbh[0:64, 0:128], self.aum[:, i * 64:(i + 1) * 64],
                                                  self.identb[:, :]),
                       waits=[wc6] + ([pbh_guard] if pbh_guard else []))
            wc8 = B.ac(lambda q, i=i: q.activation(self.aumT[0:64, i * 128:(i + 1) * 128],
                                                   self.pbh[0:64, 0:128], AF.Copy), waits=[wtr])
            pbh_guard = wc8
            wab_waits = [wc8]
            if bank_guard[6] is not None:
                wab_waits.append(bank_guard[6])
            B.pe(lambda q, i=i: q.matmul(self.pb[6][:, 0:512], self.aumT[0:64, i * 128:(i + 1) * 128],
                                         self.wadb[:, :], start=True, stop=False), waits=wab_waits)
            wab = B.pe(lambda q: q.matmul(self.pb[6][:, 0:512], self.onesb[0:1, :],
                                          self.biasb[0:1, 64:576], start=False, stop=True))
            abf = self.scr[2 + par]
            g = scrb_guard.get(("ab", par))
            wc9 = B.ac(lambda q, abf=abf: q.activation(abf[:, 0:512], self.pb[6][:, 0:512], AF.Copy),
                       waits=[wab] + ([g] if g else []))
            B.stream("sAB")
            scrb_guard[("ab", par)] = B.dma("sync", self.ab_h[i * 128:(i + 1) * 128, :], abf[:, 0:512], waits=[wc9])
            bank_guard[6] = wc9
        wl = B.ac(lambda q: q.activation(self.rnsb[:, :], self.nrm2[:, :], AF.Ln), waits=[sq_guard])
        we = B.ac(lambda q: q.activation(self.rnsb[:, :], self.rnsb[:, :], AF.Exp, scale=0.5), waits=[wl])
        wa = B.ve(lambda q: q.tensor_scalar(self.rnsb[:, :], self.rnsb[:, :], 1e-6, None, ALU.add), waits=[we])
        wrc = B.ve(lambda q: q.reciprocal(self.rnsb[:, :], self.rnsb[:, :]))
        B.stream("sAB")
        B.dma("sync", self.rn_h.rearrange("(a p) w -> p a w", p=128), self.rnsb[:, :].rearrange("p (a w) -> p a w", w=16), waits=[wrc])
        B.dma("sync", self.bb_h.rearrange("(a p) w -> p a w", p=128), self.bbsb[:, :].rearrange("p (a w) -> p a w", w=8))
        self.b_done = self.mark()
        if "wproj" not in getattr(self, "_dump_names", []):
            A.free("wproj")
        A.free("hbf")

    def ph_c(self):
        self.stream(None)
        """Segment-parallel rescaled mem/spike scan."""
        B, A = self, self.A
        A.alloc("sdseg", 8192)
        A.alloc("dbuf", 8960)
        A.alloc("spkc", 4480)
        sd3 = self.sd_h.rearrange("t (g c) -> g t c", g=4)
        sseg = [A.f32("sdseg", 0, 4096).rearrange("p (i c) -> p i c", c=128),
                A.f32("sdseg", 4096, 8192).rearrange("p (i c) -> p i c", c=128)]
        CLEN = [32] * 8 + [6]
        buf_free = [None, None]
        chunk_ld = [None] * 9

        def load_chunk(cc):
            par = cc % 2
            i0 = cc * 32
            w0 = [x for x in (buf_free[par],) if x]
            if cc < 2:
                w0 = w0 + A.barrier["sdseg"] + self.b_done
            for s in range(NSEG):
                eng = "sync" if s % 2 == 0 else "scalar"
                wts = w0 if s < 2 else ()
                chunk_ld[cc] = B.dma(eng, sseg[par][4 * s:4 * s + 4, 0:CLEN[cc], :],
                                     sd3[:, s * LSEG + i0:s * LSEG + i0 + CLEN[cc], :],
                                     waits=wts, sem="sC0" if par == 0 else "sC1")

        load_chunk(0)
        load_chunk(1)
        wz = B.ve(lambda q: q.memset(self.memM[:, :], 0.0), waits=self.b_done)
        dbv = A.f32("dbuf")
        for i in range(STEPS):
            cc, par, col = i // 32, (i // 32) % 2, i % 32
            sdcol = sseg[par][:, col, :]
            scale = float(np.float32(0.9) ** (-np.float64(i))) if False else float(0.9 ** (-i))
            thr = float(0.5 * 0.9 ** (-i))
            w1 = [chunk_ld[cc]] if col == 0 else []
            B.ve(lambda q, sdcol=sdcol, scale=scale: q.scalar_tensor_tensor(
                self.memM[:, :], sdcol, scale, self.memM[:, :], ALU.mult, ALU.add), waits=w1)
            if i >= WSEG:
                dcol = dbv[:, (i - WSEG) * 128:(i - WSEG + 1) * 128]
            else:
                dcol = self.dscr[:, (i % 2) * 128:(i % 2) * 128 + 128]
            B.ve(lambda q, dcol=dcol, thr=thr: q.tensor_scalar(
                dcol, self.memM[:, :], thr, -thr, ALU.is_gt, ALU.mult))
            wlast = B.ve(lambda q, dcol=dcol: q.tensor_tensor(
                self.memM[:, :], self.memM[:, :], dcol, ALU.add))
            if col == CLEN[cc] - 1:
                buf_free[par] = wlast
                if cc + 2 <= 8:
                    load_chunk(cc + 2)
        spk_ap, dbuf_ap = A.bf("spkc"), A.f32("dbuf")
        wsp = B.ve(lambda q, o=spk_ap, i=dbuf_ap: q.tensor_scalar(o, i, -1.0, None, ALU.is_lt),
                   waits=A.barrier["spkc"])
        spdst = self.spk_h.rearrange("(s iw) (g c) -> s g iw c", s=NSEG, g=4)
        spsrc = A.bf("spkc").rearrange("p (i c) -> p i c", c=128)
        B.stream("sSP")
        for s in range(NSEG):
            B.dma("sync" if s % 2 == 0 else "scalar", spdst[s], spsrc[4 * s:4 * s + 4, :, :],
                  waits=[wsp] if s < 2 else ())
        self.c_done = self.mark()
        A.free("sdseg"); A.free("dbuf"); A.free("spkc")

    def ph_w4(self):
        self.stream(None)
        """fp4 weight quant (my 1/8 shard) + AllGather + transposed/bf16 spills."""
        B, A = self, self.A
        A.alloc("w4a", 2816)
        A.alloc("w4b", 1408)
        A.alloc("w4r", 2816)
        A.alloc("w4t", 2816)
        A.alloc("w4q", 1024)
        win = A.f32("w4a"); wr = A.f32("w4r"); wt2 = A.f32("w4t"); wsq = A.f32("w4q")
        wout = A.bf("w4b")
        B.stream("sW4")
        jobs = [(self.wg_my_d, self.wgq_my, [128, 128, 96], 1024, 64),
                (self.wu_my_d, self.wuq_my, [128, 128, 96], 1024, 64),
                (self.wd_my_d, self.wdq_my, [128], 2816, 176)]
        wprev = None
        for (srcd, dstd, tiles, width, blocks) in jobs:
            r0 = 0
            for P in tiles:
                wts = list(A.barrier["w4a"]) if (r0 == 0 and srcd is self.wg_my_d) else []
                if wprev:
                    wts.append(wprev)
                wld = B.dma("gpsimd", win[0:P, 0:width], srcd[r0:r0 + P, :], waits=wts)
                wq = self.quant_chain(win[0:P, 0:width], wout[0:P, 0:width], width, blocks,
                                      0, wld, extra_out_wait=wprev, scr_r=wr, scr_t2=wt2, sq=wsq)
                wprev = B.dma("gpsimd", dstd[r0:r0 + P, :], wout[0:P, 0:width], waits=[wq])
                r0 += P
        wg_all = self.mark()
        for (mine, alla) in ((self.wgq_my, self.wgq_all), (self.wuq_my, self.wuq_all),
                             (self.wdq_my, self.wdq_all)):
            self.op("gpsimd", (lambda q, mine=mine, alla=alla: q.collective_compute(
                "AllGather", ALU.bypass, replica_groups=[list(range(8))],
                ins=[mine.opt()], outs=[alla.opt()])), waits=wg_all, inc="cc", amt=1)
        self.w4cc = ("cc", self.now("cc"))
        A.free("w4a"); A.free("w4b"); A.free("w4r"); A.free("w4t"); A.free("w4q")
        A.alloc("w4c", 512)
        A.alloc("w4x", 512)
        self.w4_fill = self._w4_filler()

    def _w4_filler(self):
        B, A = self, self.A
        wcast = None
        for (srcd, dstd) in ((self.wf1_d, self.wf1b_h), (self.wf3_d, self.wf3b_h),
                             (self.wf2_d, self.wf2b_h)):
            nqc = 4 if srcd is not self.wf2_d else 1
            for ch in range(8 if srcd is not self.wf2_d else 32):
                rows = slice(ch * 128, (ch + 1) * 128)
                for qc in range(nqc):
                    B.stream("sFC")
                    wts = []
                    if wcast:
                        wts = [wcast]
                    cols = slice(qc * 1024, (qc + 1) * 1024)
                    wl = B.dma("gpsimd", A.bf("w4c", 0, 1024)[:, 0:1024], srcd[rows, cols], waits=wts)
                    wcast = B.dma("gpsimd", dstd[rows, cols], A.bf("w4c", 0, 1024)[:, 0:1024], waits=[wl])
                    yield
        xsc = [A.bf("w4x", j * 128, (j + 1) * 128) for j in range(8)]
        wx = [None] * 8
        first = True
        for t in range(OTT):
            for dc in range(8):
                B.stream("sXT")
                j = (t * 8 + dc) % 4
                wts = list(A.barrier["w4x"]) if first else ([wx[j]] if wx[j] else [])
                first = False
                wt = B.dmaT("scalar", xsc[j], self.xq_h[t * 128:(t + 1) * 128, dc * 128:(dc + 1) * 128],
                            waits=wts)
                wx[j] = B.dma("sync", self.xqT_h[dc * 128:(dc + 1) * 128, t * 128:(t + 1) * 128],
                              xsc[j], sem=f"wp{j}", waits=[wt])
                yield
        wy = [None] * 8
        for (alla, dstT, R, C) in ((self.wgq_all, self.wgqT_h, NQW, 1024),
                                   (self.wuq_all, self.wuqT_h, NQW, 1024),
                                   (self.wdq_all, self.wdqT_h, 1024, NQW)):
            for rb in range(R // 128):
                for dc in range(C // 128):
                    B.stream("sTS")
                    j = (rb * (C // 128) + dc) % 4
                    wts = [self.w4cc] + ([wy[j]] if wy[j] else [])
                    wt = B.dmaT("scalar", xsc[4 + j], alla[rb * 128:(rb + 1) * 128, dc * 128:(dc + 1) * 128],
                                waits=wts)
                    wy[j] = B.dma("sync", dstT[dc * 128:(dc + 1) * 128, rb * 128:(rb + 1) * 128],
                                  xsc[4 + j], sem=f"wp{j}", waits=[wt])
                    yield

    def w4_drip(self, n):
        g = getattr(self, "w4_fill", None)
        if g is None:
            return
        cur = getattr(self, "_cur_stream", None)
        for _ in range(n):
            try:
                next(g)
            except StopIteration:
                self.w4_fill = None
                break
        self.stream(cur)

    def ph_def_(self):
        B, A = self, self.A
        SO = self.SO
        wz = B.ve(lambda q: q.memset(self.zsb[:, :], 0.0), waits=self.c_done)
        self.zw = B.ve(lambda q: q.memset(self.zsbb[:, :], 0.0))
        self.yst = []
        import os as _os
        for hf in range(int(_os.environ.get("KDBG_NHF", 2))):
            self.half_def(hf)
        self.w4_drip(10 ** 9)
        A.free("w4c"); A.free("w4x")
        self.w4_done = self.mark()
        self.def_done = self.mark()

    def load_scb(self, dst3, dram, row0, width, queue="sync", waits=()):
        B = self
        w1 = B.dma(queue, dst3[64:128, 0:1, :], dram[row0:row0 + 64, None, 0:width], waits=waits)
        w2 = B.dma(queue, dst3[:, 1:9, :],
                   dram[row0 + 64:row0 + 1088, 0:width].rearrange("(a p) w -> p a w", p=128), waits=waits)
        w3 = B.dma(queue, dst3[0:32, 9:10, :], dram[row0 + 1088:row0 + 1120, None, 0:width], waits=waits)
        m1 = B.ve(lambda q: q.memset(dst3[0:64, 0:1, :], 0.0), waits=waits)
        m2 = B.ve(lambda q: q.memset(dst3[32:64, 9:10, :], 0.0))
        m3 = B.ve(lambda q: q.memset(dst3[64:128, 9:10, :], 0.0))
        return [w1, w2, w3, m1, m2, m3]

    def store_scb(self, dram, src3, row0, width, queue="sync", waits=()):
        B = self
        w1 = B.dma(queue, dram[row0:row0 + 64, None, 0:width], src3[64:128, 0:1, :], waits=waits)
        w2 = B.dma(queue, dram[row0 + 64:row0 + 1088, 0:width].rearrange("(a p) w -> p a w", p=128),
                   src3[:, 1:9, :], waits=waits)
        w3 = B.dma(queue, dram[row0 + 1088:row0 + 1120, None, 0:width], src3[0:32, 9:10, :], waits=waits)
        return [w1, w2, w3]

    def half_def(self, hf):
        B, A, SO = self, self.A, self.SO
        self.stream(None)
        s0 = hf * HLF
        nt0 = WMEM + s0
        sq = self.smallq
        RNS, ACT01, BETM, BBS, BSV = 0, 160, 240, 320, 400
        A.alloc("spk", 2560); A.alloc("abb", 5120); A.alloc("alfa", 5120)
        spk3 = A.bf("spk").rearrange("p (a w) -> p a w", w=512)
        abb3 = A.f32("abb").rearrange("p (a w) -> p a w", w=512)
        B.stream("sDL")
        wl1 = self.load_scb(spk3, self.spk_h, s0, 512, "sync", waits=A.barrier["spk"])
        wl2 = self.load_scb(abb3, self.ab_h, nt0, 512, "scalar", waits=A.barrier["abb"])
        rns3 = sq[:, RNS:RNS + 160].rearrange("p (a w) -> p a w", w=16)
        wrns = self.load_scb(rns3, self.rn_h, nt0, 16, "gpsimd", waits=self.c_done)
        bbs3 = sq[:, BBS:BBS + 80].rearrange("p (a w) -> p a w", w=8)
        self.load_scb(bbs3, self.bb_h, nt0, 8, "gpsimd", waits=self.c_done)
        wld = self.mark()
        spkf = A.bf("spk")
        spkv = A.bf("spk").rearrange("p (a w) -> p a w", w=512)
        alfa = A.f32("alfa")
        aspk_b = self.bcast[:, 0:512].rearrange("p (o w) -> p o w", o=1).broadcast_to([128, 10, 512])
        bspk_b = self.bcast[:, 512:1024].rearrange("p (o w) -> p o w", o=1).broadcast_to([128, 10, 512])
        abbf = A.f32("abb")
        B.ve(lambda q: q.tensor_tensor(alfa.rearrange("p (a w) -> p a w", w=512), spkv, aspk_b, ALU.mult),
             waits=wld + A.barrier["alfa"])
        B.ve(lambda q: q.tensor_tensor(alfa, alfa, abbf, ALU.add))
        wsg = B.ac(lambda q: q.activation(alfa, alfa, AF.Sigmoid), waits=[("ve", self.now("ve"))])
        spk64 = A.bf("spk").rearrange("p (a w) -> p a w", w=64)
        B.ve(lambda q: q.tensor_reduce(sq[:, ACT01:ACT01 + 80].rearrange("p (b o) -> p b o", o=1),
                                       spk64, mybir.AxisListType.X, ALU.add), waits=wld)
        B.ve(lambda q: q.tensor_scalar(sq[:, ACT01:ACT01 + 80], sq[:, ACT01:ACT01 + 80],
                                       0.5, None, ALU.is_gt))
        A.alloc("dd", 5120)
        ddf = A.f32("dd")
        B.ve(lambda q: q.tensor_tensor(ddf.rearrange("p (a w) -> p a w", w=512), spkv, bspk_b, ALU.mult),
             waits=A.barrier["dd"])
        B.ve(lambda q: q.tensor_reduce(sq[:, BSV:BSV + 80].rearrange("p (b o) -> p b o", o=1),
                                       ddf.rearrange("p (b k) -> p b k", k=64),
                                       mybir.AxisListType.X, ALU.add))
        B.ve(lambda q: q.tensor_tensor(sq[:, BSV:BSV + 80], sq[:, BSV:BSV + 80], sq[:, BBS:BBS + 80], ALU.add))
        wbt = B.ac(lambda q: q.activation(sq[:, BETM:BETM + 80], sq[:, BSV:BSV + 80], AF.Sigmoid),
                   waits=[("ve", self.now("ve"))])
        B.ve(lambda q: q.tensor_tensor(sq[:, BETM:BETM + 80], sq[:, BETM:BETM + 80],
                                       sq[:, ACT01:ACT01 + 80], ALU.mult), waits=[wbt])
        act_b = sq[:, ACT01:ACT01 + 80].rearrange("p (b o) -> p b o", o=1).broadcast_to([128, 80, 64])
        B.ve(lambda q: q.tensor_scalar(alfa, alfa, -1.0, None, ALU.add), waits=[wsg])
        B.ve(lambda q: q.tensor_tensor(alfa.rearrange("p (b k) -> p b k", k=64),
                                       alfa.rearrange("p (b k) -> p b k", k=64), act_b, ALU.mult))
        wal = B.ve(lambda q: q.tensor_scalar(alfa, alfa, 1.0, None, ALU.add))
        A.free("spk"); A.free("abb")
        A.alloc("alfm", 5120)
        alfm = A.f32("alfm")
        wtr = self.pe_transpose_block(alfa, alfm, wal, A.barrier["alfm"])
        A.free("alfa")
        A.alloc("pfm", 5120)
        pfm = A.f32("pfm")
        d0 = ddf[:, 0:1280]; d1 = ddf[:, 1280:2560]
        wp = []
        for g in range(4):
            sl = slice(g * 1280, (g + 1) * 1280)
            B.ve(lambda q, sl=sl: q.memset(alfm[:, sl.start:sl.start + 64], 1.0), waits=[wtr])
            B.ve(lambda q, sl=sl: q.memset(alfm[:, sl.start + 1184:sl.start + 1280], 1.0))
            B.ve(lambda q, sl=sl: q.tensor_tensor(d1, alfm[:, sl], self.cmsk[:, :], ALU.mult))
            B.ve(lambda q, sl=sl: q.tensor_tensor(d0, alfm[:, sl], d1, ALU.subtract))
            wp.append(B.ve(lambda q, sl=sl: q.tensor_tensor_scan(
                pfm[:, sl], d0, d1, 1.0, ALU.mult, ALU.add),
                waits=A.barrier["pfm"] if g == 0 else ()))
        wpe_ = []
        for g in range(4):
            srcap = pfm[:, g * 1280 + 95: g * 1280 + 95 + 32 * NCH2].rearrange("p (c o) -> p c o", o=32)[:, :, 0:1]
            wpe_.append(B.ve(lambda q, srcap=srcap, g=g: q.tensor_copy(
                self.pend[:, g * NCH2:(g + 1) * NCH2].rearrange("p (c o) -> p c o", o=1), srcap),
                waits=[wp[g]]))
        A.alloc("prow", 5120)
        prow = A.f32("prow")
        wtr2 = self.pe_transpose_block(pfm, prow, wp[3], A.barrier["prow"], from_fm=True)
        A.free("alfm")
        A.alloc("rpr", 5120)
        rpr = A.f32("rpr")
        wrp = B.ve(lambda q: q.reciprocal(rpr, prow), waits=[wtr2] + A.barrier["rpr"])
        A.free("pfm")
        A.alloc("qraw", 2560); A.alloc("kraw", 2560)
        B.stream("sDL")
        q3 = A.bf("qraw").rearrange("p (a w) -> p a w", w=512)
        k3 = A.bf("kraw").rearrange("p (a w) -> p a w", w=512)
        wq1 = self.load_scb(q3, self.qr_h, nt0, 512, "sync", waits=A.barrier["qraw"])
        wk1 = self.load_scb(k3, self.kr_h, nt0, 512, "scalar", waits=A.barrier["kraw"])
        qrn_b = sq[:, RNS:RNS + 160].rearrange("p (a w) -> p a w", w=16)[:, :, 0:8]
        krn_b = sq[:, RNS:RNS + 160].rearrange("p (a w) -> p a w", w=16)[:, :, 8:16]
        B.ve(lambda q: q.tensor_tensor(
            sq[:, BSV:BSV + 80].rearrange("p (a w) -> p a w", w=8), krn_b,
            sq[:, BETM:BETM + 80].rearrange("p (a w) -> p a w", w=8), ALU.mult), waits=wrns)
        A.alloc("brow", 2560); A.alloc("arow", 2560); A.alloc("qtrow", 2560)
        kraw80 = A.bf("kraw").rearrange("p (a w k) -> p a w k", w=8, k=64)
        qraw80 = A.bf("qraw").rearrange("p (a w k) -> p a w k", w=8, k=64)
        dd80 = ddf.rearrange("p (a w k) -> p a w k", w=8, k=64)
        krn_e = krn_b[:, :, :, None].broadcast_to([128, 10, 8, 64])
        qrn_e = qrn_b[:, :, :, None].broadcast_to([128, 10, 8, 64])
        kb_e = sq[:, BSV:BSV + 80].rearrange("p (a w) -> p a w", w=8)[:, :, :, None].broadcast_to([128, 10, 8, 64])
        B.ve(lambda q: q.tensor_tensor(dd80, kraw80, krn_e, ALU.mult), waits=list(wk1) + wrns)
        _brow_ap = A.bf("brow")
        wbr = B.ve(lambda q, o=_brow_ap: q.tensor_tensor(o, ddf, prow, ALU.mult),
                   waits=[wtr2] + A.barrier["brow"])
        B.ve(lambda q: q.tensor_tensor(dd80, kraw80, kb_e, ALU.mult))
        _arow_ap = A.bf("arow")
        war = B.ve(lambda q, o=_arow_ap: q.tensor_tensor(o, ddf, rpr, ALU.mult), waits=A.barrier["arow"])
        B.ve(lambda q: q.tensor_tensor(dd80, qraw80, qrn_e, ALU.mult), waits=list(wq1) + wrns)
        _qtrow_ap = A.bf("qtrow")
        wqr = B.ve(lambda q, o=_qtrow_ap: q.tensor_tensor(o, ddf, prow, ALU.mult),
                   waits=[wtr2] + A.barrier["qtrow"])
        A.free("prow"); A.free("rpr"); A.free("dd"); A.free("qraw"); A.free("kraw")
        A.alloc("bcol", 2560); A.alloc("acol", 2560); A.alloc("qtcol", 2560)
        wcols = None
        B.stream("sCT")
        for (rown, coln, wrow) in (("brow", "bcol", wbr), ("arow", "acol", war), ("qtrow", "qtcol", wqr)):
            rsrc = A.bf(rown)
            cdst = A.bf(coln)
            for blk in range(10):
                for g in range(4):
                    wts = ([wrow] + A.barrier[coln]) if blk == 0 and g == 0 else ()
                    wcols = B.dmaT("scalar",
                                   cdst[:, g * 1280 + blk * 128:g * 1280 + (blk + 1) * 128],
                                   rsrc[:, blk * 512 + g * 128:blk * 512 + (g + 1) * 128], waits=wts)
        A.free("brow"); A.free("qtrow")
        A.alloc("vscb", 5120)
        B.stream("sDL")
        v3 = A.bf("vscb").rearrange("p (a w) -> p a w", w=1024)
        self.load_scb(v3, self.v_h, nt0, 1024, "gpsimd", waits=A.barrier["vscb"])
        A.alloc("negT", 1120); A.alloc("hqT", 1152); A.alloc("yscb", 5120)
        A.alloc("urm", 512); A.alloc("r2r", 256)
        negT = A.bf("negT"); hqT = A.bf("hqT")
        urm = A.bf("urm"); r2rm = A.bf("r2r")
        bcol = A.bf("bcol"); acol = A.bf("acol"); qtcol = A.bf("qtcol")
        CN = self.CN
        wprep = self.mark()
        pbg = {i: None for i in range(7)}
        solg = None
        import os as _os
        _ncl = int(_os.environ.get('KDBG_NCL', NCH2))
        _sec = _os.environ.get('KDBG_SEC', 'z')
        _secord = ["g", "sq", "t", "hq", "r2", "u", "o", "z"]
        _seccut = _secord.index(_sec)
        for cl in range(_ncl):
            self.w4_drip(12)
            fcs = [(h // 2) * 1280 + 64 + cl * 32 for h in range(8)]
            prange = [slice((h % 2) * 64, (h % 2) * 64 + 64) for h in range(8)]
            for (bank, lhs_t, rhs_t) in ((0, bcol, acol), (1, acol, bcol)):
                for h in range(8):
                    wts = []
                    if h == 0:
                        wts = list(wprep) if cl == 0 else []
                        if pbg[bank]:
                            wts.append(pbg[bank])
                    B.pe(lambda q, h=h, bank=bank, lhs_t=lhs_t, rhs_t=rhs_t, fc=fcs[h], pr=prange[h]: q.matmul(
                        self.pb[bank][(h % 4) * 32:(h % 4) * 32 + 32, (h // 4) * 32:(h // 4) * 32 + 32],
                        lhs_t[pr, fc:fc + 32], rhs_t[pr, fc:fc + 32],
                        start=True, stop=True, tile_position=((h % 2) * 64, (h % 4) * 32)), waits=wts)
            wg = ("pe", self.now("pe"))
            wn = B.ve(lambda q: q.tensor_tensor(self.sol[:, SO["N"]:SO["N"] + 64], self.pb[0][:, 0:64],
                                                self.consts[:, CN["MN"]:CN["MN"] + 64], ALU.mult),
                      waits=[wg] + ([solg] if solg else []))
            wnt = B.ve(lambda q: q.tensor_tensor(self.sol[:, SO["NT"]:SO["NT"] + 64], self.pb[1][:, 0:64],
                                                 self.consts[:, CN["MNT"]:CN["MNT"] + 64], ALU.mult))
            pbg[0] = wn; pbg[1] = wnt
            if _seccut <= 0:
                continue
            evac_w = {}

            def sq_mms(dk, lk, rk, bank, wts0):
                for h in range(8):
                    st32 = (h % 4) * 32
                    co = (h // 4) * 32
                    wts = []
                    if h == 0:
                        wts = list(wts0)
                        if pbg[bank]:
                            wts.append(pbg[bank])
                    B.pe(lambda q, lk=lk, rk=rk, bank=bank, st32=st32, co=co: q.matmul(
                        self.pb[bank][st32:st32 + 32, co:co + 32],
                        self.sol[st32:st32 + 32, SO[lk] + co:SO[lk] + co + 32],
                        self.sol[st32:st32 + 32, SO[rk] + co:SO[rk] + co + 32],
                        start=True, stop=True, tile_position=(st32, st32)), waits=wts)

            def sq_evac(dk, bank, wtok):
                ew = B.ac(lambda q, dk=dk, bank=bank: q.activation(
                    self.sol[:, SO[dk]:SO[dk] + 64], self.pb[bank][:, 0:64], AF.Copy),
                    waits=[wtok] if wtok else [])
                evac_w[dk] = ew
                pbg[bank] = ew
                return ew

            def aux_mms(bank, lk, rk, wts0):
                for h in range(8):
                    st32 = (h % 4) * 32
                    co = (h // 4) * 32
                    wts = []
                    if h == 0:
                        wts = list(wts0)
                        if pbg[bank]:
                            wts.append(pbg[bank])
                    B.pe(lambda q, bank=bank, lk=lk, rk=rk, st32=st32, co=co: q.matmul(
                        self.pb[bank][st32:st32 + 32, co:co + 32],
                        self.sol[st32:st32 + 32, SO[lk] + co:SO[lk] + co + 32],
                        self.sol[st32:st32 + 32, SO[rk] + co:SO[rk] + co + 32],
                        start=True, stop=True, tile_position=(st32, st32)), waits=wts)

            sq_mms("N2", "NT", "N", 2, [wnt])
            sq_mms("N2T", "N", "NT", 3, [])
            w_r2 = ("pe", self.now("pe"))
            sq_evac("N2", 2, w_r2); sq_evac("N2T", 3, None)
            sq_mms("N4", "N2T", "N2", 2, [evac_w["N2T"]])
            sq_mms("N4T", "N2", "N2T", 3, [])
            aux_mms(5, "N2", "NT", [evac_w["N2T"]])
            w_r3 = ("pe", self.now("pe"))
            sq_evac("N4", 2, w_r3); sq_evac("N4T", 3, None)
            B.ve(lambda q: q.tensor_tensor(self.sol[:, SO["P1T"]:SO["P1T"] + 64], self.pb[5][:, 0:64],
                                           self.sol[:, SO["N2T"]:SO["N2T"] + 64], ALU.add), waits=[w_r3])
            B.ve(lambda q: q.tensor_tensor(self.sol[:, SO["P1T"]:SO["P1T"] + 64],
                                           self.sol[:, SO["P1T"]:SO["P1T"] + 64],
                                           self.sol[:, SO["NT"]:SO["NT"] + 64], ALU.add))
            wp1 = B.ve(lambda q: q.tensor_tensor(self.sol[:, SO["P1T"]:SO["P1T"] + 64],
                                                 self.sol[:, SO["P1T"]:SO["P1T"] + 64],
                                                 self.consts[:, CN["IP"]:CN["IP"] + 64], ALU.add))
            pbg[5] = wp1
            sq_mms("N8", "N4T", "N4", 2, [evac_w["N4T"]])
            sq_mms("N8T", "N4", "N4T", 3, [])
            w_r4 = ("pe", self.now("pe"))
            sq_evac("N8", 2, w_r4); sq_evac("N8T", 3, None)
            sq_mms("N16", "N8T", "N8", 2, [evac_w["N8T"]])
            aux_mms(4, "N4T", "N8", [evac_w["N8"]])
            w_r5 = ("pe", self.now("pe"))
            sq_evac("N16", 2, w_r5)
            B.ve(lambda q: q.tensor_tensor(self.sol[:, SO["P2"]:SO["P2"] + 64], self.pb[4][:, 0:64],
                                           self.sol[:, SO["N4"]:SO["N4"] + 64], ALU.add), waits=[w_r5])
            B.ve(lambda q: q.tensor_tensor(self.sol[:, SO["P2"]:SO["P2"] + 64],
                                           self.sol[:, SO["P2"]:SO["P2"] + 64],
                                           self.sol[:, SO["N8"]:SO["N8"] + 64], ALU.add))
            wp2 = B.ve(lambda q: q.tensor_tensor(self.sol[:, SO["P2"]:SO["P2"] + 64],
                                                 self.sol[:, SO["P2"]:SO["P2"] + 64],
                                                 self.consts[:, CN["IP"]:CN["IP"] + 64], ALU.add))
            pbg[4] = wp2
            if _seccut <= 1:
                continue
            for h in range(8):
                st32 = (h % 4) * 32; co = (h // 4) * 32
                wts = []
                if h == 0:
                    wts = [wp2, wp1]
                    if pbg[6]:
                        wts.append(pbg[6])
                B.pe(lambda q, st32=st32, co=co: q.matmul(
                    self.pb[6][st32:st32 + 32, co:co + 32],
                    self.sol[st32:st32 + 32, SO["P2"] + co:SO["P2"] + co + 32],
                    self.sol[st32:st32 + 32, SO["P1T"] + co:SO["P1T"] + co + 32],
                    start=True, stop=True, tile_position=(st32, st32)), waits=wts)
            wpe5 = ("pe", self.now("pe"))
            wy1 = B.ac(lambda q: q.activation(self.sol[:, SO["Y1"]:SO["Y1"] + 64],
                                              self.pb[6][:, 0:64], AF.Copy), waits=[wpe5])
            pbg[6] = wy1
            for h in range(8):
                st32 = (h % 4) * 32; co = (h // 4) * 32
                wts = []
                if h == 0:
                    wts = [wy1, evac_w["N16"], wp2]
                B.pe(lambda q, st32=st32, co=co: q.matmul(
                    self.pb[4][st32:st32 + 32, co:co + 32],
                    self.sol[st32:st32 + 32, SO["N16"] + co:SO["N16"] + co + 32],
                    self.sol[st32:st32 + 32, SO["Y1"] + co:SO["Y1"] + co + 32],
                    start=True, stop=True, tile_position=(st32, st32)), waits=wts)
            wpe6 = ("pe", self.now("pe"))
            wneg = B.ve(lambda q, cl=cl: q.scalar_tensor_tensor(
                negT[:, cl * 64:cl * 64 + 64], self.sol[:, SO["Y1"]:SO["Y1"] + 64], -1.0,
                self.pb[4][:, 0:64], ALU.mult, ALU.subtract),
                waits=[wpe6] + (A.barrier["negT"] if cl == 0 else []))
            pbg[4] = wneg
            if _seccut <= 2:
                continue
            vb = (64 + cl * 32) % 128
            vblk = (64 + cl * 32) // 128
            for idx, h in enumerate((0, 2, 4, 6, 1, 3, 5, 7)):
                bk = 3 if h % 2 == 0 else 2
                wts = []
                if idx == 0:
                    wts = [evac_w["N8T"], evac_w["N16"]]
                    if pbg[3]:
                        wts.append(pbg[3])
                elif idx == 4:
                    if pbg[2]:
                        wts = [pbg[2]]
                B.pe(lambda q, h=h, bk=bk, fc=fcs[h], pr=prange[h], vb=vb: q.matmul(
                    self.pb[bk][vb:vb + 32, (h // 2) * 32:(h // 2) * 32 + 32],
                    acol[pr, fc:fc + 32], qtcol[pr, fc:fc + 32],
                    start=True, stop=True, tile_position=((h % 2) * 64, vb)), waits=wts)
            wpe7 = ("pe", self.now("pe"))
            whq = B.ve(lambda q, cl=cl, vb=vb: q.tensor_tensor(
                hqT[vb:vb + 32, (cl // 4) * 256:(cl // 4) * 256 + 128],
                self.pb[3][vb:vb + 32, 0:128],
                self.consts[vb:vb + 32, CN["MHQ"]:CN["MHQ"] + 128], ALU.mult),
                waits=[wpe7] + (A.barrier["hqT"] if cl == 0 else []))
            whq = B.ve(lambda q, cl=cl, vb=vb: q.tensor_tensor(
                hqT[vb:vb + 32, (cl // 4) * 256 + 128:(cl // 4) * 256 + 256],
                self.pb[2][vb:vb + 32, 0:128],
                self.consts[vb:vb + 32, CN["MHQ"] + 128:CN["MHQ"] + 256], ALU.mult))
            pbg[3] = whq
            pbg[2] = whq
            solg = whq
            if _seccut <= 3:
                continue
            zw = getattr(self, "zw", None)
            for h in range(8):
                wts = []
                if h == 0:
                    wts = [wneg, whq]
                    if pbg[0]:
                        wts.append(pbg[0])
                    if zw:
                        wts.append(zw)
                B.pe(lambda q, h=h, fc=fcs[h], pr=prange[h]: q.matmul(
                    self.pb[0][(h % 4) * 32:(h % 4) * 32 + 32, (h // 4) * 128:(h // 4) * 128 + 128],
                    bcol[pr, fc:fc + 32],
                    self.zsbb[pr, (h // 2) * 128:(h // 2) * 128 + 128],
                    start=(h < 4), stop=False, tile_position=((h % 2) * 64, (h % 4) * 32),
                    skip_group_check=True), waits=wts)
            for h in range(8):
                vap = A.bf("vscb", vblk * 1024 + h * 128, vblk * 1024 + (h + 1) * 128, parts=slice(vb, vb + 32))
                B.pe(lambda q, h=h, vb=vb, vap=vap: q.matmul(
                    self.pb[0][(h % 4) * 32:(h % 4) * 32 + 32, (h // 4) * 128:(h // 4) * 128 + 128],
                    self.negib[vb:vb + 32, 0:32], vap,
                    start=False, stop=(h >= 4), tile_position=(vb, (h % 4) * 32),
                    skip_group_check=True),
                    waits=([("pe", self.now("pe"))] if h == 0 else []))
            wpe8 = ("pe", self.now("pe"))
            wr2 = B.ac(lambda q, r2rm=r2rm: q.activation(r2rm[:, 0:512], self.pb[0][:, 0:512], AF.Copy),
                       waits=[wpe8] + (A.barrier["r2r"] if cl == 0 else []))
            pbg[0] = wr2
            if _seccut <= 4:
                continue
            ubank = [1, 2, 5, 6]
            for idx, h in enumerate((0, 4, 1, 5, 2, 6, 3, 7)):
                bank = ubank[h % 4]
                wts = []
                if idx == 0:
                    wts = [wr2]
                if h < 4 and pbg[bank]:
                    wts = wts + [pbg[bank]]
                st32 = (h % 4) * 32
                B.pe(lambda q, h=h, cl=cl, bank=bank, st32=st32, vb=vb: q.matmul(
                    self.pb[bank][vb:vb + 32, (h // 4) * 128:(h // 4) * 128 + 128],
                    negT[st32:st32 + 32, cl * 64 + (h // 4) * 32:cl * 64 + (h // 4) * 32 + 32],
                    r2rm[st32:st32 + 32, (h // 4) * 128:(h // 4) * 128 + 128],
                    start=(h < 4), stop=(h >= 4), tile_position=(st32, vb)), waits=wts)
            wpe9 = ("pe", self.now("pe"))
            wu2 = None
            for h in range(8):
                bank = ubank[h % 4]
                wts = []
                if h == 0:
                    wts = [wpe9] + (A.barrier["urm"] if cl == 0 else [])
                wu2 = B.ac(lambda q, h=h, bank=bank, vb=vb, urm=urm: q.activation(
                    urm[vb:vb + 32, h * 128:(h + 1) * 128],
                    self.pb[bank][vb:vb + 32, (h // 4) * 128:(h // 4) * 128 + 128], AF.Copy),
                    waits=wts)
            for bank in ubank:
                pbg[bank] = wu2
            if _seccut <= 5:
                continue
            for idx, h in enumerate((0, 2, 4, 6, 1, 3, 5, 7)):
                bank = 5 if h < 4 else 6
                wts = []
                if h == 0 or h == 4:
                    wts = [wu2]
                    if pbg[bank]:
                        wts.append(pbg[bank])
                if idx == 4:
                    wts = wts + [("pe", self.now("pe"))]
                B.pe(lambda q, h=h, fc=fcs[h], pr=prange[h], bank=bank, vb=vb: q.matmul(
                    self.pb[bank][vb:vb + 32, (h % 4) * 128:(h % 4) * 128 + 128],
                    qtcol[pr, fc:fc + 32],
                    self.zsbb[pr, (h // 2) * 128:(h // 2) * 128 + 128],
                    start=(h % 4 == 0), stop=False, tile_position=((h % 2) * 64, vb)), waits=wts)
            for h in range(8):
                bank = 5 if h < 4 else 6
                hp = (h % 2) * 128 + (h // 2) * 32
                B.pe(lambda q, h=h, cl=cl, bank=bank, vb=vb, hp=hp: q.matmul(
                    self.pb[bank][vb:vb + 32, (h % 4) * 128:(h % 4) * 128 + 128],
                    hqT[vb:vb + 32, (cl // 4) * 256 + hp:(cl // 4) * 256 + hp + 32],
                    urm[vb:vb + 32, h * 128:(h + 1) * 128],
                    start=False, stop=(h % 4 == 3), tile_position=(vb, vb)),
                    waits=([("pe", self.now("pe"))] if h == 0 else []))
            wpeA = ("pe", self.now("pe"))
            yv = A.bf("yscb")
            wo1 = B.ac(lambda q, vb=vb, vblk=vblk, yv=yv: q.activation(
                yv[vb:vb + 32, vblk * 1024:vblk * 1024 + 512], self.pb[5][vb:vb + 32, 0:512], AF.Copy),
                waits=[wpeA] + (A.barrier["yscb"] if cl == 0 else []))
            wo2 = B.ac(lambda q, vb=vb, vblk=vblk, yv=yv: q.activation(
                yv[vb:vb + 32, vblk * 1024 + 512:vblk * 1024 + 1024], self.pb[6][vb:vb + 32, 0:512], AF.Copy))
            pbg[5] = wo1; pbg[6] = wo2
            if _seccut <= 6:
                continue
            for h in range(8):
                wts = []
                if h == 0:
                    wts = [wneg, wu2]
                    if pbg[4]:
                        wts.append(pbg[4])
                aap = A.bf("arow", vblk * 512 + h * 64, vblk * 512 + (h + 1) * 64, parts=slice(vb, vb + 32))
                B.pe(lambda q, h=h, vb=vb, aap=aap, urm=urm: q.matmul(
                    self.pb[4][(h % 2) * 64:(h % 2) * 64 + 64, (h // 2) * 128:(h // 2) * 128 + 128],
                    aap, urm[vb:vb + 32, h * 128:(h + 1) * 128],
                    start=(h < 2), stop=(h >= 6), tile_position=(vb, (h % 2) * 64),
                    skip_group_check=True), waits=wts)
            wpeB = ("pe", self.now("pe"))
            wzt = B.ve(lambda q: q.tensor_tensor(self.scr[0][:, 0:512], self.zsb[:, :],
                                                 self.pb[4][:, 0:512], ALU.add), waits=[wpeB])
            for g in range(4):
                B.ve(lambda q, g=g, cl=cl: q.tensor_scalar(
                    self.zsb[:, g * 128:(g + 1) * 128], self.scr[0][:, g * 128:(g + 1) * 128],
                    self.pend[:, g * NCH2 + cl:g * NCH2 + cl + 1], None, ALU.mult))
            self.zw = B.ve(lambda q: q.tensor_copy(self.zsbb[:, :], self.zsb[:, :]))
            pbg[4] = wzt
        y3 = A.bf("yscb").rearrange("p (a w) -> p a w", w=1024)
        B.stream("sYS")
        self.yst += self.store_scb(self.y_h, y3, s0, 1024, "sync", waits=[self.now("ac") and ("ac", self.now("ac"))])
        import os as _os
        if not _os.environ.get("KDBG_NOFREE"):
            for nm in ("bcol", "acol", "qtcol", "arow", "vscb", "negT", "hqT", "yscb", "urm", "r2r"):
                A.free(nm)

    def pe_transpose_block(self, src, dst, wsrc, bar, from_fm=False):
        B = self
        wev = None
        pbi = 0
        pguard = [None, None]
        for blk in range(10):
            for g in range(4):
                if from_fm:
                    s = src[:, g * 1280 + blk * 128:g * 1280 + (blk + 1) * 128]
                    d = dst[:, blk * 512 + g * 128:blk * 512 + (g + 1) * 128]
                else:
                    s = src[:, blk * 512 + g * 128:blk * 512 + (g + 1) * 128]
                    d = dst[:, g * 1280 + blk * 128:g * 1280 + (blk + 1) * 128]
                bank = pbi % 2
                pbi += 1
                wts = [wsrc] if (blk == 0 and g == 0) else []
                if pguard[bank]:
                    wts = wts + [pguard[bank]]
                wt = B.pe(lambda q, s=s, bank=bank: q.transpose(
                    self.pb[bank][:, 0:128], s, self.consts[:, self.CN["ID"]:self.CN["ID"] + 128]), waits=wts)
                wev = B.ac(lambda q, d=d, bank=bank: q.activation(d, self.pb[bank][:, 0:128], AF.Copy),
                           waits=[wt] + (bar if blk == 0 and g == 0 else []))
                pguard[bank] = wev
        return wev

    def ph_g(self):
        B, A = self, self.A
        sq = self.smallq
        OT = OTT
        A.alloc("woutb", 4096)
        B.stream("sWP")
        for dc in range(8):
            B.dma("gpsimd", A.bf("woutb", dc * 1024, (dc + 1) * 1024),
                  self.wout_d[dc * 128:(dc + 1) * 128, :],
                  waits=A.barrier["woutb"] if dc == 0 else ())
        wwo = ("sWPG", self.now("sWPG"))
        A.alloc("h2T", 8192)
        A.alloc("xnT", 1024)
        A.alloc("y2T", 1024)
        A.alloc("g1g", 1024)
        A.alloc("g1b", 1024)
        A.alloc("g1a", 128)
        wu1b3 = self.wu1b[:, :]
        whnb = self.bcast[:, 1024:2048]
        bu1row = self.biasb[0:1, 584:648]
        sq_guard = None
        xn_guard = [None, None]
        yb_guard = [None, None]
        xnT_guard = [None, None]
        y2T_guard = [None, None]
        g1g_guard = [None, None]
        g1b_guard = [None, None]
        g1a_guard = [None, None]
        tmpb_guard = [None, None]
        yaf_guard = [[], []]
        pbh_guard = None
        bank_g = {i: None for i in range(7)}
        wsp = None
        for i in range(OT):
            par = i % 2
            xnf = self.scr[par]
            y2f = self.scr[2 + par]
            yaf = self.scr[4 + par]
            yb = self.scrb[par]
            tmpb = self.scrb[2 + par]
            xnTb = A.bf("xnT", par * 1024, (par + 1) * 1024)
            y2Tb = A.bf("y2T", par * 1024, (par + 1) * 1024)
            gateb = A.bf("g1g", par * 1024, (par + 1) * 1024)
            h2b = A.bf("g1b", par * 1024, (par + 1) * 1024)
            au2b = A.bf("g1a", par * 128, (par + 1) * 128, parts=slice(0, 64))
            B.stream("sXL")
            wxl = B.dma("sync", xnf[:, :], self.xn32_h[i * 128:(i + 1) * 128, :],
                        waits=([xn_guard[par]] if xn_guard[par] else []) + (self.def_done if i == 0 else []))
            B.stream("sDL")
            wyl = B.dma("sync", yb[:, :], self.y_h[WMEM + i * 128:WMEM + (i + 1) * 128, :],
                        waits=[yb_guard[par]] if yb_guard[par] else [])
            B.stream("sXT")
            wxT = None
            for dc in range(8):
                wts = []
                if dc == 0:
                    wts = A.barrier["xnT"] if i == 0 else ([xnT_guard[par]] if xnT_guard[par] else [])
                wxT = B.dmaT("scalar", xnTb[:, dc * 128:(dc + 1) * 128],
                             self.xnb_h[i * 128:(i + 1) * 128, dc * 128:(dc + 1) * 128], waits=wts)
            wsq = B.ac(lambda q, yb=yb: q.activation(self.sqscr[:, :], yb[:, :], AF.Square),
                       waits=[wyl] + ([sq_guard] if sq_guard else []))
            wrd = B.ve(lambda q: q.tensor_reduce(
                sq[:, 16:24].rearrange("p (a o) -> p a o", o=1),
                self.sqscr[:, :].rearrange("p (a f) -> p a f", f=128),
                mybir.AxisListType.X, ALU.add), waits=[wsq])
            B.ve(lambda q: q.tensor_scalar(sq[:, 16:24], sq[:, 16:24], 1.0 / 128.0, 1e-6, ALU.mult, ALU.add))
            wve0 = ("ve", self.now("ve"))
            wl1 = B.ac(lambda q: q.activation(sq[:, 24:32], sq[:, 16:24], AF.Ln), waits=[wve0])
            wl2 = B.ac(lambda q: q.activation(sq[:, 24:32], sq[:, 24:32], AF.Exp, scale=-0.5), waits=[wl1])
            rstd8b = sq[:, 24:32].rearrange("p (a o) -> p a o", o=1).broadcast_to([128, 8, 128])
            B.ve(lambda q, yb=yb, y2f=y2f: q.tensor_tensor(
                y2f[:, :].rearrange("p (a f) -> p a f", f=128),
                yb[:, :].rearrange("p (a f) -> p a f", f=128), rstd8b, ALU.mult), waits=[wl2])
            yb_guard[par] = ("ve", self.now("ve"))
            wy2 = B.ve(lambda q, y2f=y2f, tmpb=tmpb: q.tensor_tensor(tmpb[:, :], y2f[:, :], whnb, ALU.mult),
                       waits=[tmpb_guard[par]] if tmpb_guard[par] else [])
            for dc in range(8):
                wts = []
                if dc == 0:
                    wts = [wy2] + ([pbh_guard] if pbh_guard else [])
                B.pe(lambda q, dc=dc, tmpb=tmpb: q.transpose(
                    self.pbh[:, dc * 128:(dc + 1) * 128], tmpb[:, dc * 128:(dc + 1) * 128],
                    self.identb[:, :]), waits=wts)
            wpet = ("pe", self.now("pe"))
            tmpb_guard[par] = wpet
            wy2T = B.ac(lambda q, y2Tb=y2Tb: q.activation(y2Tb[:, :], self.pbh[:, :], AF.Copy),
                        waits=[wpet] + (A.barrier["y2T"] if i == 0 else ([y2T_guard[par]] if y2T_guard[par] else [])))
            pbh_guard = wy2T
            for dc in range(8):
                for dh in range(2):
                    wts = []
                    if dc == 0 and dh == 0:
                        wts = [wy2T, wwo]
                        for bb in (0, 1):
                            if bank_g[bb]:
                                wts.append(bank_g[bb])
                    wo_rhs = A.bf("woutb", dc * 1024 + dh * 512, dc * 1024 + (dh + 1) * 512)
                    B.pe(lambda q, dc=dc, dh=dh, y2Tb=y2Tb, wo_rhs=wo_rhs: q.matmul(
                        self.pb[dh][:, 0:512], y2Tb[:, dc * 128:(dc + 1) * 128], wo_rhs,
                        start=(dc == 0), stop=(dc == 7)), waits=wts)
            wpe_wo = ("pe", self.now("pe"))
            y2T_guard[par] = wpe_wo
            for dc in range(8):
                wts = []
                if dc == 0:
                    wts = [wxT] + ([bank_g[2]] if bank_g[2] else [])
                B.pe(lambda q, dc=dc, xnTb=xnTb: q.matmul(
                    self.pb[2][0:64, 0:128], wu1b3[:, dc * 64:(dc + 1) * 64],
                    xnTb[:, dc * 128:(dc + 1) * 128],
                    start=(dc == 0), stop=False), waits=wts)
            B.pe(lambda q: q.matmul(self.pb[2][0:64, 0:128], bu1row, self.onesb[0:1, :],
                                    start=False, stop=True))
            wpe_u1 = ("pe", self.now("pe"))
            xnT_guard[par] = wpe_u1
            wau = B.ac(lambda q, au2b=au2b: q.activation(au2b, self.pb[2][0:64, 0:128], AF.Silu),
                       waits=[wpe_u1] + (A.barrier["g1a"] if i == 0 else ([g1a_guard[par]] if g1a_guard[par] else [])))
            bank_g[2] = wau
            for dh in range(2):
                wts = []
                if dh == 0:
                    wts = [wau]
                    for bb in (3, 4):
                        if bank_g[bb]:
                            wts.append(bank_g[bb])
                B.pe(lambda q, dh=dh, au2b=au2b: q.matmul(
                    self.pb[3 + dh][:, 0:512], au2b, self.wu2b[:, dh * 512:(dh + 1) * 512],
                    start=True, stop=False), waits=wts)
                B.pe(lambda q, dh=dh: q.matmul(
                    self.pb[3 + dh][:, 0:512], self.onesb[0:1, :],
                    self.biasb[0:1, 648 + dh * 512:648 + (dh + 1) * 512],
                    start=False, stop=True))
            wpe_u2 = ("pe", self.now("pe"))
            wsg = None
            for dh in range(2):
                wts = []
                if dh == 0:
                    wts = [wpe_u2] + (A.barrier["g1g"] if i == 0 else ([g1g_guard[par]] if g1g_guard[par] else []))
                wsg = B.ac(lambda q, dh=dh, gateb=gateb: q.activation(
                    gateb[:, dh * 512:(dh + 1) * 512], self.pb[3 + dh][:, 0:512], AF.Sigmoid), waits=wts)
            bank_g[3] = wsg
            bank_g[4] = wsg
            g1g_guard[par] = wsg
            g1a_guard[par] = wsg
            for dh in range(2):
                wts = []
                if dh == 0:
                    wts = [wpe_wo, wsg] + yaf_guard[par]
                B.ve(lambda q, dh=dh, yaf=yaf, gateb=gateb: q.tensor_tensor(
                    yaf[:, dh * 512:(dh + 1) * 512], self.pb[dh][:, 0:512],
                    gateb[:, dh * 512:(dh + 1) * 512], ALU.mult), waits=wts)
            bank_g[0] = bank_g[1] = ("ve", self.now("ve"))
            wya = B.ve(lambda q, yaf=yaf, xnf=xnf: q.tensor_tensor(yaf[:, :], yaf[:, :], xnf[:, :], ALU.add),
                       waits=[wxl])
            xn_guard[par] = wya
            B.stream("sYS")
            wsp = B.dma("gpsimd", self.s_h[i * 128:(i + 1) * 128, :], yaf[:, :], waits=[wya])
            wsq2 = B.ac(lambda q, yaf=yaf: q.activation(self.sqscr[:, :], yaf[:, :], AF.Square),
                        waits=[wya, wrd])
            wrd2 = B.ve(lambda q: q.tensor_reduce(
                sq[:, 32:33].rearrange("p (a o) -> p a o", a=1),
                self.sqscr[:, :].rearrange("p (a f) -> p a f", a=1),
                mybir.AxisListType.X, ALU.add), waits=[wsq2])
            sq_guard = wrd2
            B.ve(lambda q: q.tensor_scalar(sq[:, 32:33], sq[:, 32:33], 1.0 / 1024.0, 1e-6, ALU.mult, ALU.add))
            wve1 = ("ve", self.now("ve"))
            wl3 = B.ac(lambda q: q.activation(sq[:, 33:34], sq[:, 32:33], AF.Ln), waits=[wve1])
            wl4 = B.ac(lambda q: q.activation(sq[:, 33:34], sq[:, 33:34], AF.Exp, scale=-0.5), waits=[wl3])
            wts_h2s = B.ve(lambda q, yaf=yaf, y2f=y2f: q.tensor_scalar(
                y2f[:, :], yaf[:, :], sq[:, 33:34], None, ALU.mult), waits=[wl4])
            yaf_guard[par] = [wsp, wsq2]
            wh2 = B.ve(lambda q, y2f=y2f, h2b=h2b: q.tensor_tensor(h2b[:, :], y2f[:, :], self.wffnb[:, :], ALU.mult),
                       waits=(A.barrier["g1b"] if i == 0 else ([g1b_guard[par]] if g1b_guard[par] else [])))
            for dc in range(8):
                wts = []
                if dc == 0:
                    wts = [wh2, pbh_guard]
                B.pe(lambda q, dc=dc, h2b=h2b: q.transpose(
                    self.pbh[:, dc * 128:(dc + 1) * 128], h2b[:, dc * 128:(dc + 1) * 128],
                    self.identb[:, :]), waits=wts)
            wpet2 = ("pe", self.now("pe"))
            wh2T = None
            for dc in range(8):
                dsth = A.bf("h2T", dc * 2048 + i * 128, dc * 2048 + (i + 1) * 128)
                wts = []
                if dc == 0:
                    wts = [wpet2] + (A.barrier["h2T"] if i == 0 else [])
                wh2T = B.ac(lambda q, dc=dc, dsth=dsth: q.activation(
                    dsth, self.pbh[:, dc * 128:(dc + 1) * 128], AF.Copy), waits=wts)
            pbh_guard = wh2T
            g1b_guard[par] = wh2T
        g1_done = self.mark()
        A.free("woutb"); A.free("xnT"); A.free("y2T"); A.free("g1g"); A.free("g1a")
        A.alloc("wf1q", 4096); A.alloc("wf3q", 4096); A.alloc("wf2q", 4096)
        A.alloc("midb", 128)
        A.alloc("wgqs", 2048); A.alloc("wuqs", 2048)
        A.alloc("xqTp", 2048)
        A.alloc("hqo", 512)
        g3_blocks = [(nb, i) for nb in range(6) for i in range(16)]
        g3_state = dict(idx=0, wgl=None, bank56=None, hq_guard=[None, None],
                        xq_guard=[None, None], wq_pe=None)
        g3_stores = []
        sqg = [sq_guard]

        def emit_g3_block():
            st = g3_state
            nb, i = g3_blocks[st["idx"]]
            st["idx"] += 1
            nbw = 512 if nb < 5 else 256
            par = i % 2
            if i == 0:
                B.stream("sFC")
                wgl = None
                for dc in range(8):
                    wts = []
                    if dc == 0:
                        wts = (A.barrier["wgqs"] + A.barrier["wuqs"]) if nb == 0 else [st["wq_pe"]]
                    B.dma("gpsimd", A.bf("wgqs", dc * 512, dc * 512 + nbw),
                          self.wgqT_h[dc * 128:(dc + 1) * 128, nb * 512:nb * 512 + nbw], waits=wts)
                    wgl = B.dma("gpsimd", A.bf("wuqs", dc * 512, dc * 512 + nbw),
                                self.wuqT_h[dc * 128:(dc + 1) * 128, nb * 512:nb * 512 + nbw])
                st["wgl"] = wgl
            B.stream("sSP")
            wxt = None
            for dc in range(8):
                wts = []
                if dc == 0:
                    wts = (A.barrier["xqTp"] if st["idx"] <= 2 else
                           ([st["xq_guard"][par]] if st["xq_guard"][par] else []))
                wxt = B.dma("sync", A.bf("xqTp", par * 1024 + dc * 128, par * 1024 + (dc + 1) * 128),
                            self.xqT_h[dc * 128:(dc + 1) * 128, i * 128:(i + 1) * 128], waits=wts)
            hqb = A.bf("hqo", par * 512, par * 512 + nbw)
            for dc in range(8):
                wts = []
                if dc == 0:
                    wts = [wxt, st["wgl"]]
                    if st["bank56"]:
                        wts.append(st["bank56"])
                    if st["idx"] == 1:
                        wts = wts + A.barrier["wf1q"]
                rx = A.bf("xqTp", par * 1024 + dc * 128, par * 1024 + (dc + 1) * 128)
                rg = A.bf("wgqs", dc * 512, dc * 512 + nbw)
                B.pe(lambda q, rx=rx, rg=rg, nbw=nbw, dc=dc: q.matmul(
                    self.pb[5][:, 0:nbw], rx, rg,
                    start=(dc == 0), stop=(dc == 7)), waits=wts)
            for dc in range(8):
                rx = A.bf("xqTp", par * 1024 + dc * 128, par * 1024 + (dc + 1) * 128)
                ru = A.bf("wuqs", dc * 512, dc * 512 + nbw)
                B.pe(lambda q, rx=rx, ru=ru, nbw=nbw, dc=dc: q.matmul(
                    self.pb[6][:, 0:nbw], rx, ru,
                    start=(dc == 0), stop=(dc == 7)))
            wpeg = ("pe", self.now("pe"))
            st["xq_guard"][par] = wpeg
            if i == 15:
                st["wq_pe"] = wpeg
            wts = [wpeg]
            if sqg[0]:
                wts.append(sqg[0])
                sqg[0] = None
            wsil = B.ac(lambda q, nbw=nbw: q.activation(
                self.sqscr[:, 512:512 + nbw], self.pb[5][:, 0:nbw], AF.Silu), waits=wts)
            hmid = self.scr[2]
            wmul = B.ve(lambda q, nbw=nbw, hmid=hmid: q.tensor_tensor(
                hmid[:, 0:nbw], self.sqscr[:, 512:512 + nbw], self.pb[6][:, 0:nbw], ALU.mult),
                waits=[wsil])
            st["bank56"] = wmul
            eow = (st["hq_guard"][par] if st["hq_guard"][par] else
                   (A.barrier["hqo"] if st["idx"] <= 2 else None))
            wqc = self.quant_chain(hmid[:, 0:nbw], hqb, width=nbw, blocks=nbw // 16, sq_lo=64,
                                   wait=wmul, extra_out_wait=eow,
                                   scr_r=self.scr[4], scr_t2=self.scr[5])
            B.stream("sSP")
            st["hq_guard"][par] = B.dma("gpsimd", self.hq_h[i * 128:(i + 1) * 128, nb * 512:nb * 512 + nbw],
                                        hqb, waits=[wqc])
            g3_stores.append(st["hq_guard"][par])

        wmid_guard = [None, None]
        g2mul_prev = None
        sprev = [None, None]
        bank23_g = {b: None for b in (2, 3)}
        b01_g = None
        g2w_guard = None
        nblk = 0
        for qf in range(4):
            B.stream("sFC")
            wq1 = None
            for dc in range(8):
                wts = []
                if dc == 0:
                    wts = (A.barrier["wf1q"] + A.barrier["wf3q"] + A.barrier["wf2q"]) if qf == 0 else [g2w_guard]
                B.dma("gpsimd", A.bf("wf1q", dc * 1024, (dc + 1) * 1024),
                      self.wf1b_h[dc * 128:(dc + 1) * 128, qf * 1024:(qf + 1) * 1024], waits=wts)
                B.dma("gpsimd", A.bf("wf3q", dc * 1024, (dc + 1) * 1024),
                      self.wf3b_h[dc * 128:(dc + 1) * 128, qf * 1024:(qf + 1) * 1024])
                wq1 = B.dma("gpsimd", A.bf("wf2q", dc * 1024, (dc + 1) * 1024),
                            self.wf2b_h[qf * 1024 + dc * 128:qf * 1024 + (dc + 1) * 128, 0:1024])
            for tc in range(16):
                par = tc % 2
                s0f = self.scr[par]
                B.stream("sDL")
                wsl = B.dma("sync", s0f[:, :], self.s_h[tc * 128:(tc + 1) * 128, :],
                            waits=([wsp] if qf == 0 else []) + ([sprev[par]] if sprev[par] else []))
                for Fc in range(8):
                    mpar = Fc % 2
                    midv = A.bf("midb", mpar * 128, (mpar + 1) * 128)
                    for dc in range(8):
                        wts = []
                        if dc == 0:
                            if tc == 0 and Fc == 0:
                                wts = [wq1]
                                if qf == 0:
                                    wts = wts + g1_done
                            if b01_g:
                                wts = wts + [b01_g]
                        r1 = A.bf("wf1q", dc * 1024 + Fc * 128, dc * 1024 + (Fc + 1) * 128)
                        rh = A.bf("h2T", dc * 2048 + tc * 128, dc * 2048 + (tc + 1) * 128)
                        B.pe(lambda q, r1=r1, rh=rh, dc=dc: q.matmul(
                            self.pb[0][:, 0:128], r1, rh,
                            start=(dc == 0), stop=(dc == 7)), waits=wts)
                    for dc in range(8):
                        r3 = A.bf("wf3q", dc * 1024 + Fc * 128, dc * 1024 + (Fc + 1) * 128)
                        rh = A.bf("h2T", dc * 2048 + tc * 128, dc * 2048 + (tc + 1) * 128)
                        B.pe(lambda q, r3=r3, rh=rh, dc=dc: q.matmul(
                            self.pb[1][:, 0:128], r3, rh,
                            start=(dc == 0), stop=(dc == 7)))
                    wpef = ("pe", self.now("pe"))
                    wts = [wpef]
                    if sqg[0]:
                        wts.append(sqg[0])
                        sqg[0] = None
                    if g2mul_prev:
                        wts.append(g2mul_prev)
                    wsil = B.ac(lambda q: q.activation(self.sqscr[:, 0:128], self.pb[0][:, 0:128], AF.Silu),
                                waits=wts)
                    wmul = B.ve(lambda q, midv=midv: q.tensor_tensor(
                        midv, self.sqscr[:, 0:128], self.pb[1][:, 0:128], ALU.mult),
                        waits=[wsil] + (A.barrier["midb"] if (qf == 0 and tc == 0 and Fc < 2)
                                        else ([wmid_guard[mpar]] if wmid_guard[mpar] else [])))
                    b01_g = wmul
                    g2mul_prev = wmul
                    for dh in range(2):
                        bk = 2 + dh
                        wts = [wmul]
                        if Fc == 0 and bank23_g[bk]:
                            wts.append(bank23_g[bk])
                        r2 = A.bf("wf2q", Fc * 1024 + dh * 512, Fc * 1024 + (dh + 1) * 512)
                        B.pe(lambda q, bk=bk, midv=midv, r2=r2, Fc=Fc: q.matmul(
                            self.pb[bk][:, 0:512], midv, r2,
                            start=(Fc == 0), stop=(Fc == 7)), waits=wts)
                    wmid_guard[mpar] = ("pe", self.now("pe"))
                wpe2 = ("pe", self.now("pe"))
                for dh in range(2):
                    bk = 2 + dh
                    wts = [wpe2, wsl] if dh == 0 else []
                    wadd = B.ve(lambda q, s0f=s0f, dh=dh, bk=bk: q.tensor_tensor(
                        s0f[:, dh * 512:(dh + 1) * 512], s0f[:, dh * 512:(dh + 1) * 512],
                        self.pb[bk][:, 0:512], ALU.add), waits=wts)
                    bank23_g[bk] = wadd
                wadd2 = ("ve", self.now("ve"))
                B.stream("sYS")
                sprev[par] = B.dma("gpsimd", self.s_h[tc * 128:(tc + 1) * 128, :], s0f[:, :], waits=[wadd2])
                nblk += 1
                import os as _os
                target = 0 if _os.environ.get("KDBG_NOMIX") else (nblk * 96) // 64
                while g3_state["idx"] < target:
                    emit_g3_block()
            g2w_guard = ("pe", self.now("pe"))
        while g3_state["idx"] < 96:
            emit_g3_block()
        g3_scr_free = [("sYSG", self.now("sYSG")), ("ve", self.now("ve")),
                       ("sSPG", self.now("sSPG"))]
        A.free("wf1q"); A.free("wf3q"); A.free("wf2q"); A.free("midb"); A.free("g1b"); A.free("h2T")
        A.free("wgqs"); A.free("wuqs"); A.free("xqTp"); A.free("hqo")
        A.alloc("wdqT", 11264)
        B.stream("sFC")
        for fc in range(22):
            B.dma("gpsimd", A.bf("wdqT", fc * 1024, (fc + 1) * 1024),
                  self.wdqT_h[fc * 128:(fc + 1) * 128, :],
                  waits=A.barrier["wdqT"] if fc == 0 else ())
        wdl = ("sFCG", self.now("sFCG"))
        A.alloc("hqTg", 2816)
        hqT_guard = [None, None]
        s_guard = [None, None]
        gf_guard = None
        of_guard = None
        b01g4 = None
        for i in range(OT):
            par = i % 2
            hqTb = A.bf("hqTg", par * 2816, (par + 1) * 2816)
            sf = self.scr[2 * par]
            xf = self.scr[2 * par + 1]
            gf = self.scr[4]
            of = self.scr[5]
            B.stream("sTS")
            whqT = None
            for fc in range(22):
                wts = []
                if fc == 0:
                    wts = (A.barrier["hqTg"] + g3_stores) if i < 2 else ([hqT_guard[par]] if hqT_guard[par] else [])
                whqT = B.dmaT("scalar", hqTb[:, fc * 128:(fc + 1) * 128],
                              self.hq_h[i * 128:(i + 1) * 128, fc * 128:(fc + 1) * 128], waits=wts)
            B.stream("sDL")
            wsl = B.dma("sync", sf[:, :], self.s_h[i * 128:(i + 1) * 128, :],
                        waits=(g3_scr_free if i < 2 else []) + ([s_guard[par]] if s_guard[par] else []))
            B.stream("sXL")
            wxl = B.dma("sync", xf[:, :], self.xg_d[HALO + i * 128:HALO + (i + 1) * 128, :],
                        waits=(g3_scr_free if i < 2 else []))
            for fc in range(22):
                for dh in range(2):
                    wts = []
                    if fc == 0 and dh == 0:
                        wts = [whqT, wdl] + ([b01g4] if b01g4 else [])
                    rw = A.bf("wdqT", fc * 1024 + dh * 512, fc * 1024 + (dh + 1) * 512)
                    B.pe(lambda q, fc=fc, dh=dh, hqTb=hqTb, rw=rw: q.matmul(
                        self.pb[dh][:, 0:512], hqTb[:, fc * 128:(fc + 1) * 128], rw,
                        start=(fc == 0), stop=(fc == 21)), waits=wts)
            wped = ("pe", self.now("pe"))
            hqT_guard[par] = wped
            wg0 = B.ac(lambda q, gf=gf: q.activation(gf[:, 0:512], self.pb[0][:, 0:512], AF.Silu),
                       waits=[wped] + ([gf_guard] if gf_guard else []))
            wg1 = B.ac(lambda q, gf=gf: q.activation(gf[:, 512:1024], self.pb[1][:, 0:512], AF.Silu))
            b01g4 = wg1
            wo1 = B.ve(lambda q, of=of, sf=sf, gf=gf: q.tensor_tensor(of[:, :], sf[:, :], gf[:, :], ALU.mult),
                       waits=[wg1, wsl] + ([of_guard] if of_guard else []))
            gf_guard = wo1
            wo2 = B.ve(lambda q, of=of, xf=xf: q.tensor_tensor(of[:, :], of[:, :], xf[:, :], ALU.add),
                       waits=[wxl])
            s_guard[par] = wo2
            of_guard = B.dma("gpsimd", self.out_d[i * 128:(i + 1) * 128, :], of[:, :],
                             sem="dO", waits=[wo2])
        self.g_done = self.mark()
        A.free("wdqT"); A.free("hqTg")

    def finish(self):
        B = self
        for nm in getattr(self, "_dump_names", []):
            if nm in self.A.regions:
                w2 = 2 * self.A.regions[nm][1]
                o = self.nc.dram_tensor("dbg_" + nm, [128, w2], BF16, kind="ExternalOutput").ap()
                B.dma("gpsimd", o, self.A.bf(nm), waits=self.mark())
        dumps = getattr(self, "dbg_dumps", [])
        for (name, dram_ap) in dumps:
            o = self.nc.dram_tensor("dbg_" + name, list(dram_ap.tensor.shape), dram_ap.dtype,
                                    kind="ExternalOutput").ap()
            B.dma("gpsimd", o, dram_ap, waits=self.mark())
        if getattr(self, "stop", "g") != "g":
            w = B.ve(lambda q: q.memset(self.scr[0][:, :], 0.0), waits=self.mark())
            for i in range(OTT):
                B.dma("gpsimd", self.out_d[i * 128:(i + 1) * 128, :], self.scr[0][:, :], waits=[w])
        final = self.mark()
        self.op("gpsimd", lambda q: q.wait_ge(self.sems["dO"], 0), waits=final)

    def emit(self):
        sems = self.sems
        import concourse.bass as bass_
        with self.nc.Block() as block:
            def mk(eng):
                def run(q):
                    for (e, fn, waits, inc, amt) in self.sched:
                        if e != eng:
                            continue
                        merged = {}
                        for (s, v) in waits:
                            merged[s] = max(merged.get(s, 0), v)
                        for (s, v) in merged.items():
                            q.wait_ge(sems[s], v)
                        ins = fn(q)
                        if inc is not None and ins is not None:
                            ins.then_inc(sems[inc], amt)
                return run
            block.sync(mk("sync"))
            block.tensor(mk("tensor"))
            block.vector(mk("vector"))
            block.scalar(mk("scalar"))
            block.gpsimd(mk("gpsimd"))


def host_constants():
    r = np.arange(128)[:, None] % 32
    c64 = np.arange(64)[None, :] % 32
    c256 = np.arange(256)[None, :] % 32
    cst = {}
    cst["c_ident"] = np.eye(128, dtype=np.float32)
    cst["c_identb"] = np.eye(128, dtype=np.float32).astype(ml_dtypes.bfloat16)
    negi = np.zeros((128, 32), np.float32)
    negi[np.arange(128), np.arange(128) % 32] = -1.0
    cst["c_negi"] = negi.astype(ml_dtypes.bfloat16)
    cst["c_ones"] = np.ones((1, 128), ml_dtypes.bfloat16)
    cst["c_onesf"] = np.ones((128, 128), np.float32)
    cst["c_mskN"] = np.where(c64 < r, -1.0, 0.0).astype(np.float32)
    cst["c_mskNT"] = np.where(c64 > r, -1.0, 0.0).astype(np.float32)
    cst["c_mskHq"] = np.where(r <= c256, 1.0, 0.0).astype(np.float32)
    cst["c_identp"] = np.where(c64 == r, 1.0, 0.0).astype(np.float32)
    mst = np.zeros(SCW, np.float32)
    mst[0] = 1.0
    for j in range(NCH2):
        mst[64 + 32 * j] = 1.0
    cst["c_mst"] = np.tile(mst[None, :], (128, 1)).astype(ml_dtypes.bfloat16)
    return cst


def make_inmaps(z):
    cst = host_constants()
    x = np.asarray(z["L0_attn_x"])
    wmap = dict(
        wq="L0_attn_wq", wk="L0_attn_wk", wv="L0_attn_wv", w_spike="L0_attn_w_spike",
        w_au="L0_attn_w_au", w_ad="L0_attn_w_ad", w_beta="L0_attn_w_beta",
        w_u1="L0_attn_w_u1", w_u2="L0_attn_w_u2", w_ff1="L0_attn_w_ff1",
        w_ff3="L0_attn_w_ff3", w_ff2="L0_attn_w_ff2", w_out="L0_attn_w_out",
        conv_w="L0_attn_conv_w",
    )
    rowmap = dict(
        b_au="L0_attn_b_au", b_ad="L0_attn_b_ad", b_beta="L0_attn_b_beta",
        b_u1="L0_attn_b_u1", b_u2="L0_attn_b_u2", w_ffn="L0_attn_w_ffn",
    )
    base = {}
    for k, v in wmap.items():
        base[k] = np.ascontiguousarray(np.asarray(z[v], np.float32))
    for k, v in rowmap.items():
        base[k] = np.ascontiguousarray(np.asarray(z[v], np.float32).reshape(1, -1))
    base["conv_b"] = np.asarray(z["L0_attn_conv_b"], np.float32).reshape(D, 1)
    base["w_norm_in"] = np.asarray(z["L0_attn_w_norm_in"], np.float32).reshape(D, 1)
    base["alpha_spike"] = np.asarray(z["L0_attn_alpha_spike"], np.float32).reshape(1, 512)
    base["beta_spike"] = np.asarray(z["L0_attn_beta_spike"], np.float32).reshape(1, 512)
    base["w_hn"] = np.asarray(z["L0_attn_w_hn"], np.float32).reshape(1, 1024)
    wg = np.asarray(z["L0_ffn_w_gate"], np.float32)
    wu = np.asarray(z["L0_ffn_w_up"], np.float32)
    wd = np.asarray(z["L0_ffn_w_down"], np.float32)
    base.update(cst)
    ims = []
    for c in range(8):
        b, t0 = c // 4, (c % 4) * 2048
        lo = t0 - HALO
        m = dict(base)
        if lo >= 0:
            m["xg"] = np.ascontiguousarray(x[b, lo:t0 + 2048], np.float32)
            sdm = np.ones((NT, 1), np.float32)
            xh = np.zeros((128, D), np.float32)
            if lo >= 3:
                xh[125:128] = x[b, lo - 3:lo]
            m["xh128"] = xh
        else:
            m["xg"] = np.concatenate([np.zeros((HALO, D), np.float32),
                                      np.ascontiguousarray(x[b, 0:2048], np.float32)], 0)
            sdm = np.concatenate([np.zeros((HALO, 1), np.float32),
                                  np.ones((2048, 1), np.float32)], 0)
            m["xh128"] = np.zeros((128, D), np.float32)
        m["sdmask"] = sdm
        m["wg_my"] = np.ascontiguousarray(wg[c * 352:(c + 1) * 352])
        m["wu_my"] = np.ascontiguousarray(wu[c * 352:(c + 1) * 352])
        m["wd_my"] = np.ascontiguousarray(wd[c * 128:(c + 1) * 128])
        ims.append(m)
    return ims



def kernel(**inputs):
    ims = make_inmaps(inputs)
    k = Kern()
    nc = k.build()
    k.emit()
    res = run_bass_kernel_spmd(nc, ims, core_ids=list(range(8)))
    out = np.zeros((2, 8192, 1024), np.float32)
    for c in range(8):
        b, t0 = c // 4, (c % 4) * 2048
        out[b, t0:t0 + 2048] = np.asarray(res.results[c]["out"], np.float32)
    k.es.close()
    return out
